# version 31
# baseline (speedup 1.0000x reference)
import numpy as np
import concourse.bass as bass
import concourse.mybir as mybir
from concourse.bass_utils import run_bass_kernel_spmd

F32 = mybir.dt.float32
BF16 = mybir.dt.bfloat16
I32 = mybir.dt.int32
I16 = mybir.dt.int16
I8 = mybir.dt.int8
AF = mybir.ActivationFunctionType
ALU = mybir.AluOpType
AX = mybir.AxisListType

D = 1024
TOK = 2048
DI = 3072
BLK = 32
N_CORES = 8

LN_EPS = 1e-5
ATT_EPS = 1e-6
QMAX = 32767.0
GUARD = 1e-12

S_S1, S_C1, S_S2, S_F2, S_S3, S_F3, S_S4, S_C4, S_S5, S_C5, S_T0, S_T1 = range(16, 28)
S_MHALF = 30
S_MONE = 31
B_C1, B_F2, B_F3, B_S3, B_C4, B_C5, B_S5 = range(7)
B_UP, B_GATE, B_TMP, B_RUN = 10, 11, 12, 15

SEM_NAMES = ["T", "V", "S", "CC", "G",
             "LX", "GW", "KV1", "KV2", "LX2", "SXR", "LR0", "LR1", "WR0", "WR1",
             "SPWD", "SPW1", "LWDT", "SA0", "SA1", "LB0", "LB1", "SB0", "SB1", "SC0", "SC1", "WR2",
             "LC0", "LC1", "LP0", "LP1", "LXR", "SOUT", "LBR", "GBR", "LW0", "LW1", "LW2", "LW3"]


class Plan:
    ENGINES = ("sync", "gpsimd", "tensor", "vector", "scalar")

    def __init__(self):
        self.q = {e: [] for e in self.ENGINES}
        self.cnt = {s: 0 for s in SEM_NAMES}
        self._last_wait = {e: False for e in self.ENGINES}

    def inc(self, name, k):
        self.cnt[name] += k
        return self.cnt[name]

    def now(self, name):
        return self.cnt[name]

    def emit(self, eng, fn):
        self.q[eng].append(fn)
        self._last_wait[eng] = False

    def wait(self, eng, sem_name, val=None):
        v = self.cnt[sem_name] if val is None else val
        if v > 0:
            if self._last_wait[eng]:
                self.q[eng].append(lambda e: e.nop())
            self.q[eng].append(lambda e, s=sem_name, vv=v: e.wait_ge(SEMS[s], vv))
            self._last_wait[eng] = True


SEMS = {}


class SbufMap:
    BASE = 16640
    TOP = 229360
    LIMIT = TOP - BASE

    def __init__(self, nc):
        self.nc = nc
        self.items = []

    def alloc(self, name, shape, dtype, ph0, ph1, align=64):
        esz = {F32: 4, BF16: 2, I32: 4, I16: 2, I8: 1}[dtype]
        nbytes = int(np.prod(shape[1:])) * esz
        nbytes_al = (nbytes + align - 1) // align * align
        taken = sorted((it[1], it[1] + it[2]) for it in self.items
                       if not (it[4] < ph0 or it[3] > ph1))
        merged = []
        for s, e in taken:
            if merged and s <= merged[-1][1]:
                merged[-1][1] = max(merged[-1][1], e)
            else:
                merged.append([s, e])
        holes = []
        prev = 0
        for s, e in merged:
            if s - prev >= nbytes_al:
                holes.append((prev, s - prev))
            prev = max(prev, e)
        if self.LIMIT - prev >= nbytes_al:
            holes.append((prev, self.LIMIT - prev))
        assert holes, f"SBUF overflow: {name} needs {nbytes_al} (no hole)"
        off = min(holes, key=lambda h: h[1])[0]
        self.items.append((name, off, nbytes_al, ph0, ph1))
        return self.nc.alloc_sbuf_tensor_at(name, list(shape), dtype, offset=self.BASE + off)

    def alloc_chunks(self, name, n, shape, dtype, ph0, ph1, align=64):
        return [self.alloc(f"{name}{i}", shape, dtype, ph0, ph1, align) for i in range(n)]


def build_program(nc, d=D, tok=TOK, di=DI, debug=False):
    nk = d // 128
    ni = di // 128
    ntc = tok // 128
    ns = tok // 512
    nblk_d = d // BLK
    nblk_i = di // BLK

    xT = nc.dram_tensor("xT", [d, tok], F32, kind="ExternalInput").ap()
    W_h = nc.dram_tensor("W", [d, 3 * d], F32, kind="ExternalInput").ap()
    wg_h = nc.dram_tensor("wg", [di, d], F32, kind="ExternalInput").ap()
    wu_h = nc.dram_tensor("wu", [di, d], F32, kind="ExternalInput").ap()
    wd_h = nc.dram_tensor("wd", [d, di], F32, kind="ExternalInput").ap()
    ident_h = nc.dram_tensor("ident", [128, 128], F32, kind="ExternalInput").ap()
    outT = nc.dram_tensor("outT", [d, tok], F32, kind="ExternalOutput").ap()

    up_dram = nc.dram_tensor("up_dram", [di, tok], BF16)
    gate_dram = nc.dram_tensor("gate_dram", [di, tok], BF16)
    prod_dram = nc.dram_tensor("prod_dram", [di, tok], BF16)
    g2_dram = nc.dram_tensor("g2_dram", [di, tok], I16)
    xres_dram = nc.dram_tensor("xres_dram", [d, tok], F32)
    wdT_dram = nc.dram_tensor("wdT_dram", [di, d], BF16)
    wuT_dram = nc.dram_tensor("wuT_dram", [di, d], BF16)
    wgT_dram = nc.dram_tensor("wgT_dram", [di, d], BF16)
    kv_cc_in = nc.dram_tensor("kv_cc_in", [128, nk * d + 16], F32)
    kv_cc_out = nc.dram_tensor("kv_cc_out", [128, nk * d + 16], F32)
    sc_cc_in = [nc.dram_tensor(f"sc_cc_in{i}", [1, 16], F32) for i in range(4)]
    sc_cc_out = [nc.dram_tensor(f"sc_cc_out{i}", [1, 16], F32) for i in range(4)]
    cc_w_in = nc.dram_tensor("cc_w_in", [1, 16], F32)
    cc_w_out = nc.dram_tensor("cc_w_out", [1, 16], F32)

    sm = SbufMap(nc)
    ident_f = sm.alloc("ident_f", [128, 128], F32, 0, 119)
    ident_b = sm.alloc("ident_b", [128, 128], BF16, 0, 119)
    ones_b = sm.alloc("ones_b", [128, 1], BF16, 0, 119)
    onerow_f = sm.alloc("onerow_f", [1, 128], F32, 0, 119)
    row0 = sm.alloc("row0", [1, tok], F32, 0, 79)
    row1a = sm.alloc("row1a", [1, tok], F32, 10, 29)
    row1b = sm.alloc("row1b", [1, tok], F32, 60, 69)
    scal = sm.alloc("scal", [1, 64], F32, 0, 119)
    bc = sm.alloc("bc", [128, 16], F32, 0, 119)
    mxcols = sm.alloc("mxcols", [128, 64], F32, 0, 119)

    W_bf = sm.alloc("W_bf", [128, nk * 3 * d], BF16, 0, 29)
    ttmp = sm.alloc("ttmp", [128, tok], F32, 10, 29)
    ttmp6 = sm.alloc("ttmp6", [128, tok], F32, 60, 69)
    ln1x = sm.alloc_chunks("ln1x", nk, [128, tok], BF16, 10, 29)
    qe = sm.alloc_chunks("qe", nk, [128, tok], BF16, 20, 59)
    kf = sm.alloc_chunks("kf", nk, [128, tok], BF16, 20, 31)
    vf = sm.alloc_chunks("vf", nk, [128, tok], BF16, 20, 33)
    x_f = sm.alloc_chunks("x_f", nk, [128, tok], F32, 10, 19)
    xbf_t = sm.alloc("xbf_t", [128, 4 * tok], BF16, 10, 19)
    keT = sm.alloc_chunks("keT", ntc, [128, d], BF16, 30, 49)
    vT = sm.alloc_chunks("vT", ntc, [128, d], BF16, 34, 49)
    kvt_f = sm.alloc("kvt_f", [128, nk * d + 16], F32, 40, 49)
    kvtks = sm.alloc("kvtks", [128, nk * d + 16], BF16, 50, 59)
    zbc = sm.alloc("zbc", [128, tok], BF16, 50, 59)
    xres_t = sm.alloc("xres_t", [128, tok], F32, 50, 59)
    xres_t2 = sm.alloc("xres_t2", [128, tok], F32, 50, 59)
    xresf = sm.alloc_chunks("xresf", nk, [128, tok], F32, 50, 69)
    wst_u = sm.alloc_chunks("wst_u", 4, [128, nk * 128], BF16, 80, 89)
    wst_g = sm.alloc_chunks("wst_g", 4, [128, nk * 128], BF16, 80, 89)
    wraw_s = sm.alloc("wraw_s", [128, 6 * d], F32, 30, 79)
    wqf_s = None
    wi8_s = sm.alloc("wi8_s", [128, 4 * d], I8, 50, 79)
    wdq_s = sm.alloc("wdq_s", [128, 4 * d], BF16, 50, 79)
    wsc_s = sm.alloc("wsc_s", [128, 12 * nblk_d], F32, 50, 79)
    xbf_t2 = sm.alloc("xbf_t2", [128, 4 * tok], BF16, 60, 69)
    ln2x = sm.alloc_chunks("ln2x", nk, [128, tok], BF16, 60, 79)
    xq = sm.alloc_chunks("xq", nk, [128, tok], BF16, 70, 89)
    xq16 = sm.alloc("xq16", [128, tok], I16, 70, 79)
    xq16b = sm.alloc("xq16b", [128, tok], I16, 70, 79)
    spill_t = sm.alloc_chunks("spill_t", 4, [128, tok], BF16, 80, 119)
    upq = sm.alloc_chunks("upq", ni, [128, tok], I16, 80, 119)
    gi16 = sm.alloc("gi16", [128, tok], I16, 90, 119)
    gi16b = sm.alloc("gi16b", [128, tok], I16, 90, 119)
    gi16c = sm.alloc("gi16c", [128, tok], I16, 90, 99)
    sig32 = sm.alloc("sig32", [128, tok], I32, 90, 119)
    sig32b = sm.alloc("sig32b", [128, tok], I32, 90, 119)
    sigf = sm.alloc("sigf", [128, tok], F32, 90, 99)
    sigfb = sm.alloc("sigfb", [128, tok], F32, 90, 99)
    prodt = sm.alloc_chunks("prodt", 2, [128, tok], BF16, 90, 99)
    wdT = sm.alloc_chunks("wdT", ni, [128, d], BF16, 100, 119)
    of32 = sm.alloc("of32", [128, 2 * tok], F32, 100, 119)

    o_bf = [upq[i].ap().bitcast(BF16) for i in range(ni)]

    PA = nc.alloc_psum_tensor("PA", [128, 2048], F32)
    PB = nc.alloc_psum_tensor("PB", [128, 2048], F32)
    pages = [PA, PB]

    P = Plan()

    def dma(eng_name, out_ap, in_ap, sem_name):
        v = P.inc(sem_name, 16)
        P.emit(eng_name, lambda e, o=out_ap, i=in_ap, s=sem_name:
               e.dma_start(out=o, in_=i).then_inc(SEMS[s], 16))
        return v

    def mm(out_ap, lhsT_ap, rhs_ap, start, stop):
        P.emit("tensor", lambda e, o=out_ap, l=lhsT_ap, r=rhs_ap, st=start, sp=stop:
               e.matmul(o, l, r, start=st, stop=sp))

    def sep(eng):
        P.emit(eng, lambda e: e.drain())

    _tagged = set()

    def _tag(eng, sem):
        key = (eng, len(P.q[eng]))
        assert key not in _tagged, f"double-tag on {eng} at {len(P.q[eng])}"
        _tagged.add(key)
        v = P.inc(sem, 1)
        fn = P.q[eng].pop()
        P.emit(eng, lambda e, f=fn, s=sem: f(e).then_inc(SEMS[s], 1))
        return v

    def t_mark():
        return _tag("tensor", "T")

    def v_op(fn):
        P.emit("vector", fn)

    def v_mark(drain=False):
        if drain:
            v = P.inc("V", 1)
            P.emit("vector", lambda e: e.drain().then_inc(SEMS["V"], 1))
            return v
        return _tag("vector", "V")

    def g_op(fn):
        P.emit("gpsimd", fn)

    def g_mark(drain=False):
        if drain:
            v = P.inc("G", 1)
            P.emit("gpsimd", lambda e: e.drain().then_inc(SEMS["G"], 1))
            return v
        return _tag("gpsimd", "G")

    def s_op(fn):
        P.emit("scalar", fn)

    def s_mark(drain=False):
        if drain:
            v = P.inc("S", 1)
            P.emit("scalar", lambda e: e.drain().then_inc(SEMS["S"], 1))
            return v
        return _tag("scalar", "S")

    R0 = row0.ap()[0:1, 0:tok]

    def sc(i):
        return scal.ap()[0:1, i:i + 1]

    def bcol(i):
        return bc.ap()[:, i:i + 1]

    dma("sync", ident_f[:, :], ident_h, "LX")
    P.wait("sync", "V", 1)
    ccw = dma("sync", cc_w_in.ap(), scal.ap()[0:1, 0:16], "LBR")
    P.wait("gpsimd", "LBR", ccw)
    P.emit("gpsimd", lambda e: e.collective_compute(
        "AllReduce", ALU.max, replica_groups=[[0, 1, 2, 3, 4, 5, 6, 7]],
        ins=[cc_w_in.ap()], outs=[cc_w_out.ap()]).then_inc(SEMS["CC"], 1))
    P.inc("CC", 1)
    P.emit("gpsimd", lambda e: e.collective_compute(
        "AllReduce", ALU.add, replica_groups=[[0, 1], [2, 3], [4, 5], [6, 7]],
        ins=[cc_w_in.ap()], outs=[cc_w_out.ap()]).then_inc(SEMS["CC"], 1))
    P.inc("CC", 1)
    dma("gpsimd", ident_b[:, :], ident_h, "GW")
    v_op(lambda e: e.memset(ones_b[:, :], 1.0))
    v_op(lambda e: e.memset(onerow_f[:, :], 1.0))
    v_op(lambda e: e.memset(scal.ap()[0:1, :], 0.0))
    v_op(lambda e: e.memset(sc(S_MHALF), -0.5))
    v_op(lambda e: e.memset(sc(S_MONE), -1.0))
    v_const = v_mark(drain=True)
    for k in range(nk):
        dma("gpsimd", W_bf[:, k * 3 * d:(k + 1) * 3 * d], W_h[k * 128:(k + 1) * 128, :], "GW")
    gw_all = P.now("GW")
    for k in range(nk // 2):
        dma("sync", x_f[k][:, :], xT[k * 128:(k + 1) * 128, :], "LX")
    lx_lo = P.now("LX")
    for k in range(nk // 2, nk):
        dma("scalar", x_f[k][:, :], xT[k * 128:(k + 1) * 128, :], "LX2")
    lx_hi = P.now("LX2")

    def ln_block(src, arrive, consumed, xbf_buf, R1):
        tks = {}
        for k in range(nk):
            xo = (k % 2) * tok
            so = 2 * tok + (k % 2) * tok
            arrive(k, "vector")
            if k >= 2:
                P.wait("vector", "T", tks[k - 2])
            v_op(lambda e, a=src(k), xb=xbf_buf, xo=xo: e.tensor_copy(xb.ap()[:, xo:xo + tok], a))
            vv = v_mark()
            arrive(k, "scalar")
            if k >= 2:
                P.wait("scalar", "T", tks[k - 2])
            s_op(lambda e, a=src(k), xb=xbf_buf, so=so: e.activation(xb.ap()[:, so:so + tok], a,
                                                                    AF.Square))
            sv = s_mark()
            P.wait("tensor", "V", vv)
            P.wait("tensor", "S", sv)
            for s in range(ns):
                mm(PA[0:1, s * 512:(s + 1) * 512], ones_b[:, 0:1],
                   xbf_buf.ap()[:, xo + s * 512: xo + (s + 1) * 512],
                   start=(k == 0), stop=(k == nk - 1))
                mm(PB[0:1, s * 512:(s + 1) * 512], ones_b[:, 0:1],
                   xbf_buf.ap()[:, so + s * 512: so + (s + 1) * 512],
                   start=(k == 0), stop=(k == nk - 1))
            tks[k] = t_mark()
            consumed(k, tks[k])
        tstat = P.now("T")
        P.wait("vector", "T", tstat)
        v_op(lambda e: e.tensor_scalar(R0, PA[0:1, 0:tok], -1.0 / d, None, ALU.mult))
        v_op(lambda e: e.tensor_scalar(R1, PB[0:1, 0:tok], 1.0 / d, None, ALU.mult))
        v_op(lambda e: e.tensor_tensor(PA[0:1, 0:tok], R0, R0, ALU.mult))
        v_op(lambda e: e.tensor_tensor(R1, R1, PA[0:1, 0:tok], ALU.subtract))
        v_op(lambda e: e.tensor_scalar(R1, R1, LN_EPS, None, ALU.add))
        vrec = v_mark(drain=True)
        P.wait("scalar", "V", vrec)
        s_op(lambda e: e.activation(R1, R1, AF.Ln))
        s_op(lambda e: e.activation(R1, R1, AF.Exp, scale=sc(S_MHALF)))
        ssq = s_mark(drain=True)
        P.wait("vector", "S", ssq)
        v_op(lambda e: e.tensor_tensor(R0, R0, R1, ALU.mult))
        vrows = v_mark(drain=True)
        P.wait("tensor", "V", vrows)
        for s in range(ns):
            mm(PA[:, s * 512:(s + 1) * 512], onerow_f[0:1, :], R1[:, s * 512:(s + 1) * 512], True, True)
            mm(PB[:, s * 512:(s + 1) * 512], onerow_f[0:1, :], R0[:, s * 512:(s + 1) * 512], True, True)
        return t_mark()

    P.wait("vector", "V", v_const)

    def arr1(k, eng):
        if k < nk // 2:
            P.wait(eng, "LX", lx_lo)
        else:
            P.wait(eng, "LX2", lx_hi)

    tb1 = ln_block(lambda k: x_f[k].ap()[:, :], arr1, lambda k, tk: None, xbf_t, row1a.ap()[0:1, :])
    P.wait("vector", "T", tb1)
    for k in range(nk):
        xk = x_f[k].ap()[:, :]
        ok = ln1x[k].ap()[:, :]
        v_op(lambda e, xk=xk: e.tensor_tensor(ttmp[:, 0:tok], xk, PA[:, 0:tok], ALU.mult))
        v_op(lambda e, ok=ok: e.tensor_tensor(ok, ttmp[:, 0:tok], PB[:, 0:tok], ALU.add))
    v_ln1 = v_mark()

    P.wait("tensor", "V", v_ln1)
    P.wait("tensor", "GW", gw_all)
    evac_done = {}
    morder = list(range(nk, 2 * nk)) + list(range(2 * nk, 3 * nk)) + list(range(nk))
    for idx, m in enumerate(morder):
        pg = pages[idx % 2]
        if idx >= 2:
            P.wait("tensor", *evac_done[idx % 2])
        for k in range(nk):
            lhs = W_bf.ap()[:, k * 3 * d + m * 128: k * 3 * d + (m + 1) * 128]
            for s in range(ns):
                mm(pg[:, s * 512:(s + 1) * 512], lhs,
                   ln1x[k].ap()[:, s * 512:(s + 1) * 512],
                   start=(k == 0), stop=(k == nk - 1))
        tq = t_mark()
        if m < 2 * nk:
            dstbuf = qe if m < nk else kf
            dst = dstbuf[m % nk].ap()[:, :]
            P.wait("vector", "T", tq)
            v_op(lambda e, pg=pg: e.tensor_scalar(ttmp[:, 0:tok], pg[:, 0:tok], 0.0, None, ALU.min))
            vv = v_mark()
            P.wait("scalar", "V", vv)
            s_op(lambda e: e.activation(ttmp[:, 0:tok], ttmp[:, 0:tok], AF.Exp))
            sv = s_mark()
            P.wait("vector", "S", sv)
            v_op(lambda e, pg=pg, dst=dst: e.scalar_tensor_tensor(
                dst, pg[:, 0:tok], 0.0, ttmp[:, 0:tok], ALU.max, ALU.add))
            evac_done[idx % 2] = ("V", v_mark())
        else:
            dst = vf[m - 2 * nk].ap()[:, :]
            P.wait("scalar", "T", tq)
            s_op(lambda e, pg=pg, dst=dst: e.copy(dst, pg[:, 0:tok]))
            evac_done[idx % 2] = ("S", s_mark())
    P.wait("tensor", *evac_done[0])
    P.wait("tensor", *evac_done[1])

    regions3 = [(PA, 0), (PA, 1024), (PB, 0), (PB, 1024)]
    tr_evac = {}
    ri = 0
    for src, dst in ((kf, keT), (vf, vT)):
        for c in range(ntc):
            pg, off = regions3[ri % 4]
            if ri >= 4:
                P.wait("tensor", *tr_evac[ri % 4])
            for k in range(nk):
                mm(pg[:, off + k * 128: off + (k + 1) * 128],
                   src[k].ap()[:, c * 128:(c + 1) * 128],
                   ident_b[:, :], True, True)
            tt = t_mark()
            P.wait("scalar", "T", tt)
            s_op(lambda e, pg=pg, off=off, dst=dst, c=c:
                 e.copy(dst[c].ap()[:, :], pg[:, off:off + d]))
            tr_evac[ri % 4] = ("S", s_mark())
            ri += 1
    for r in range(min(4, ri)):
        P.wait("tensor", *tr_evac[r])
    wu_entry = (P.now("S"), P.now("G"), P.now("V"))
    wu_entry_t = P.now("T")

    regions4 = [(PA, 0), (PA, 1024), (PB, 0)]
    kvt_evac = {}
    for dk in range(nk):
        pg, off = regions4[dk % 3]
        if dk >= 3:
            P.wait("tensor", *kvt_evac[dk % 3])
        for c in range(ntc):
            lhs = keT[c].ap()[:, dk * 128:(dk + 1) * 128]
            w2 = min(512, d)
            for s2 in range(max(1, d // 512)):
                mm(pg[:, off + s2 * w2: off + (s2 + 1) * w2], lhs,
                   vT[c].ap()[:, s2 * w2:(s2 + 1) * w2],
                   start=(c == 0), stop=(c == ntc - 1))
            mm(PB[:, 2032 + dk: 2033 + dk], lhs, ones_b[:, 0:1],
               start=(c == 0), stop=(c == ntc - 1))
        tkv = t_mark()
        P.wait("vector", "T", tkv)
        v_op(lambda e, pg=pg, off=off, dk=dk:
             e.tensor_copy(kvt_f.ap()[:, dk * d:(dk + 1) * d], pg[:, off:off + d]))
        kvt_evac[dk % 3] = ("V", v_mark())
    t_ks = P.now("T")
    P.wait("vector", "T", t_ks)
    v_op(lambda e: e.memset(kvt_f.ap()[:, nk * d: nk * d + 16], 0.0))
    v_op(lambda e: e.tensor_copy(kvt_f.ap()[:, nk * d: nk * d + nk], PB[:, 2032:2032 + nk]))
    v_kvt = v_mark(drain=True)

    P.wait("gpsimd", "V", v_kvt)
    kv1 = dma("gpsimd", kv_cc_in[:, :], kvt_f.ap()[:, :], "KV1")
    P.wait("gpsimd", "KV1", kv1)
    P.emit("gpsimd", lambda e: e.collective_compute(
        "AllReduce", ALU.add, replica_groups=[[0, 1], [2, 3], [4, 5], [6, 7]],
        ins=[kv_cc_in.ap()], outs=[kv_cc_out.ap()]).then_inc(SEMS["CC"], 1))
    cc_kv = P.inc("CC", 1)

    def weight_prep(src_h, nch, wc, gsz, dstT_sb, dstT_dram, wraw, wi8, wdq, wsc, wqf, wsem, dbuf=True, col0=0, j0ofs=0, mrow=False, entry=None, entry_t=None):
        njt = wc // 128
        groups = [(s, min(gsz, nch - s)) for s in range(0, nch, gsz)]
        ngr = len(groups)
        gw_max = gsz * wc
        eS, eG, eV = entry if entry is not None else (P.now("S"), P.now("G"), P.now("V"))
        if entry_t is not None:
            P.wait("sync", "T", entry_t)
            P.wait("scalar", "T", entry_t)
        P.wait("sync", "S", eS)
        P.wait("sync", "G", eG)
        P.wait("sync", "V", eV)
        P.wait("scalar", "V", eV)
        P.wait("vector", "G", eG)
        arr = {}
        g_marks = {}
        t_marks = {}
        vtt1_marks = {}
        tpose_evac = {}
        s_prev = P.now("S")
        spwd_prev = {0: P.now("SPWD"), 1: P.now("SPW1")}
        s_hist = {}
        spwd_hist = {}

        def load(g):
            half = (g % 3) * gw_max
            m0, gn = groups[g]
            sem = wsem[g % 3]
            if gn == 1:
                h = wc // 2
                dma("sync", wraw.ap()[:, half:half + h],
                    src_h[m0 * 128:(m0 + 1) * 128, col0:col0 + h], sem)
                arr[g] = dma("scalar", wraw.ap()[:, half + h:half + wc],
                             src_h[m0 * 128:(m0 + 1) * 128, col0 + h:col0 + wc], sem)
            else:
                for mi in range(gn):
                    m = m0 + mi
                    arr[g] = dma("sync" if mi == 0 else "scalar",
                                 wraw.ap()[:, half + mi * wc: half + (mi + 1) * wc],
                                 src_h[m * 128:(m + 1) * 128, col0:col0 + wc], sem)

        def sl(g):
            m0, gn = groups[g]
            gw = gn * wc
            nbg = gw // BLK
            half_s = (g % 3) * 2 * (gw_max // BLK)
            svp = wsc.ap()[:, half_s:half_s + nbg]
            isv = wsc.ap()[:, half_s + (gw_max // BLK):half_s + (gw_max // BLK) + nbg]
            buf = wraw.ap()[:, (g % 3) * gw_max:][:, 0:gw]
            return gw, nbg, svp, isv, buf

        load(0)
        if ngr > 1:
            load(1)
        if ngr > 2:
            load(2)
        gw0, nbg0, svp0, isv0, buf0 = sl(0)
        P.wait("vector", wsem[0], arr[0])
        v_op(lambda e, buf=buf0, svp=svp0: e.tensor_reduce(
            svp, buf.rearrange("p (b c) -> p b c", c=BLK), AX.X, ALU.max,
            apply_absolute_value=True))
        sep("vector")
        v_op(lambda e, svp=svp0: e.tensor_scalar(svp, svp, 1.0 / 7.0, GUARD, ALU.mult, ALU.max))
        sep("vector")
        v_op(lambda e, svp=svp0, isv=isv0: e.reciprocal(isv, svp))
        if ngr == 1:
            sep("vector")
        for g in range(ngr):
            m0, gn = groups[g]
            gw, nbg, svp, isv, buf = sl(g)
            if g + 1 < ngr:
                gw1, nbg1, svp1, isv1, buf1 = sl(g + 1)
                P.wait("vector", wsem[(g + 1) % 3], arr[g + 1])
                if g >= 2:
                    P.wait("vector", "G", g_marks[g - 2])
                v_op(lambda e, buf=buf1, svp=svp1: e.tensor_reduce(
                    svp, buf.rearrange("p (b c) -> p b c", c=BLK), AX.X, ALU.max,
                    apply_absolute_value=True))
            else:
                sep("vector")
            v_op(lambda e, buf=buf, isv=isv, nbg=nbg, gw=gw: e.tensor_tensor(
                buf.rearrange("p (b c) -> p b c", c=BLK),
                buf.rearrange("p (b c) -> p b c", c=BLK),
                isv.rearrange("p (b o) -> p b o", o=1).broadcast_to([128, nbg, BLK]),
                ALU.mult))
            vtt1_marks[g] = v_mark()
            if g + 1 < ngr:
                v_op(lambda e, svp=svp1: e.tensor_scalar(svp, svp, 1.0 / 7.0, GUARD,
                                                         ALU.mult, ALU.max))
            w8h = (g % 2) * gw_max
            if g >= 2:
                P.wait("vector", "G", g_marks[g - 2])
            v_op(lambda e, gw=gw, w8h=w8h, buf=buf: e.tensor_scalar(wi8.ap()[:, w8h:w8h + gw], buf,
                                                                    1.0, None, ALU.mult))
            vr8 = v_mark()
            if g + 3 < ngr:
                P.wait("sync", "V", vr8)
                P.wait("scalar", "V", vr8)
                load(g + 3)
            if g + 1 < ngr:
                v_op(lambda e, svp=svp1, isv=isv1: e.reciprocal(isv, svp))
            P.wait("gpsimd", "V", vr8)
            wofs = (g % 2) * gw_max
            if g >= 2:
                P.wait("gpsimd", "T", t_marks[g - 2])
            if dstT_dram is not None:
                P.wait("gpsimd", "S", s_hist[g - 2] if g >= 2 else s_prev)
                P.wait("gpsimd", ("SPWD", "SPW1")[g % 2],
                       spwd_hist[g - 2] if g >= 2 else spwd_prev[g % 2])
            g_op(lambda e, svp=svp, nbg=nbg, gw=gw, w8h=w8h, wofs=wofs: e.tensor_tensor(
                wdq.ap()[:, wofs:wofs + gw].rearrange("p (b c) -> p b c", c=BLK),
                wi8.ap()[:, w8h:w8h + gw].rearrange("p (b c) -> p b c", c=BLK),
                svp.rearrange("p (b o) -> p b o", o=1).broadcast_to([128, nbg, BLK]),
                ALU.mult))
            g_marks[g] = g_mark()
            P.wait("tensor", "G", g_marks[g])
            for mi in range(gn):
                m = m0 + mi
                nbat = (njt + 7) // 8
                for jb in range(nbat):
                    j0, jn = jb * 8, min(8, njt - jb * 8)
                    rr = ((m0 + mi) * nbat + jb) % 4
                    pg, off = regions3[rr]
                    if (m0 + mi) * nbat + jb >= 4:
                        P.wait("tensor", *tpose_evac[rr])
                    for j in range(j0, j0 + jn):
                        mm(pg[:, off + (j - j0) * 128: off + (j - j0 + 1) * 128],
                           wdq.ap()[:, wofs + mi * wc + j * 128: wofs + mi * wc + (j + 1) * 128],
                           ident_b[:, :], True, True)
                    tt = t_mark()
                    P.wait("scalar", "T", tt)
                    if dstT_sb is not None:
                        s_op(lambda e, pg=pg, off=off, m=m, j0=j0, jn=jn:
                             e.copy(dstT_sb[m].ap()[:, j0 * 128:(j0 + jn) * 128],
                                    pg[:, off: off + jn * 128]))
                        svv = s_mark()
                    else:
                        s_op(lambda e, pg=pg, off=off, j0=j0, jn=jn, mi=mi, wc=wc, wofs=wofs:
                             e.copy(wdq.ap()[:, wofs + mi * wc + j0 * 128: wofs + mi * wc + (j0 + jn) * 128],
                                    pg[:, off: off + jn * 128]))
                        svv = s_mark()
                        P.wait("sync", "S", svv)
                        spsem = ("SPWD", "SPW1")[g % 2]
                        if mrow:
                            dma("sync",
                                dstT_dram.ap()[m * 128:(m + 1) * 128,
                                               (j0ofs + j0) * 128:(j0ofs + j0 + jn) * 128],
                                wdq.ap()[:, wofs + mi * wc + j0 * 128: wofs + mi * wc + (j0 + jn) * 128], spsem)
                        else:
                            dma("sync",
                                dstT_dram.ap()[(j0ofs + j0) * 128:(j0ofs + j0 + jn) * 128,
                                               m * 128:(m + 1) * 128].rearrange("(j p) c -> p j c", p=128),
                                wdq.ap()[:, wofs + mi * wc + j0 * 128: wofs + mi * wc + (j0 + jn) * 128]
                                .rearrange("p (j c) -> p j c", c=128), spsem)
                    tpose_evac[rr] = ("S", svv)
            t_marks[g] = P.now("T")
            s_hist[g] = P.now("S")
            spwd_hist[g] = P.now(("SPWD", "SPW1")[g % 2])
        for r in tpose_evac.values():
            P.wait("tensor", *r)

    weight_prep(wu_h, ni, d, 2, None, wuT_dram, wraw_s, wi8_s, wdq_s, wsc_s, wqf_s,
                ("WR0", "WR1", "WR2"), mrow=True, entry=wu_entry, entry_t=wu_entry_t)
    weight_prep(wg_h, ni, d, 2, None, wgT_dram, wraw_s, wi8_s, wdq_s, wsc_s, wqf_s, ("WR0", "WR1", "WR2"), mrow=True)
    wd_entry = (P.now("S"), P.now("G"), P.now("V"))

    P.wait("gpsimd", "CC", cc_kv)
    kv2 = dma("gpsimd", kvtks.ap()[:, :], kv_cc_out[:, :], "KV2")

    P.wait("tensor", "KV2", kv2)
    for dk in range(nk):
        ksc = kvtks.ap()[:, nk * d + dk: nk * d + dk + 1]
        for s in range(ns):
            mm(PA[0:1, s * 512:(s + 1) * 512], ksc,
               qe[dk].ap()[:, s * 512:(s + 1) * 512],
               start=(dk == 0), stop=(dk == nk - 1))
    tden = t_mark()
    P.wait("vector", "T", tden)
    v_op(lambda e: e.tensor_scalar(R0, PA[0:1, 0:tok], ATT_EPS, None, ALU.add))
    vz0 = v_mark(drain=True)
    P.wait("scalar", "V", vz0)
    s_op(lambda e: e.activation(R0, R0, AF.Ln))
    s_op(lambda e: e.activation(R0, R0, AF.Exp, scale=sc(S_MONE)))
    vz = s_mark(drain=True)
    P.wait("tensor", "S", vz)
    for s in range(ns):
        mm(PB[:, s * 512:(s + 1) * 512], onerow_f[0:1, :], R0[:, s * 512:(s + 1) * 512], True, True)
    tzb = t_mark()
    P.wait("vector", "T", tzb)
    v_op(lambda e: e.tensor_copy(zbc.ap()[:, :], PB[:, 0:tok]))
    v_zbc = v_mark()

    P.wait("tensor", "V", v_zbc)
    xres_evac = {}
    xts = [xres_t, xres_t2]
    xsem = ("LX2", "LXR")
    lx2a = {0: dma("sync", xres_t.ap()[:, :], xT[0:128, :], "LX2"),
            1: dma("sync", xres_t2.ap()[:, :], xT[128:256, :], "LXR")}
    for h in range(nk):
        pg = pages[h % 2]
        if h >= 2:
            P.wait("tensor", *xres_evac[h % 2])
        for dk in range(nk):
            lhs = kvtks.ap()[:, dk * d + h * 128: dk * d + (h + 1) * 128]
            for s in range(ns):
                mm(pg[:, s * 512:(s + 1) * 512], lhs,
                   qe[dk].ap()[:, s * 512:(s + 1) * 512],
                   start=(dk == 0), stop=(dk == nk - 1))
        tv = t_mark()
        P.wait("vector", "T", tv)
        P.wait("vector", xsem[h % 2], lx2a[h])
        v_op(lambda e, pg=pg, h=h: e.tensor_tensor(xresf[h].ap()[:, :], pg[:, 0:tok], zbc.ap()[:, :], ALU.mult))
        v_op(lambda e, h=h: e.tensor_tensor(xresf[h].ap()[:, :], xresf[h].ap()[:, :],
                                            xts[h % 2].ap()[:, :], ALU.add))
        vv = v_mark()
        xres_evac[h % 2] = ("V", vv)
        xres_vv = xres_evac[h % 2]
        P.wait("sync", "V", vv)
        if h + 2 < nk:
            lx2a[h + 2] = dma("sync", xts[h % 2].ap()[:, :], xT[(h + 2) * 128:(h + 3) * 128, :],
                              xsem[h % 2])
        dma("sync", xres_dram[h * 128:(h + 1) * 128, :], xresf[h].ap()[:, :], "SXR")
    sxr_all = P.now("SXR")

    def arrive6(k, eng):
        pass

    v_xres_all = P.now("V")
    P.wait("vector", "V", v_xres_all)
    P.wait("scalar", "V", v_xres_all)
    tb2 = ln_block(lambda k: xresf[k].ap()[:, :], arrive6, lambda k, tk: None,
                   xbf_t2, row1b.ap()[0:1, :])
    P.wait("vector", "T", tb2)
    app6 = {}
    for k in range(nk):
        xk = xresf[k].ap()[:, :]
        ok = ln2x[k].ap()[:, :]
        v_op(lambda e, xk=xk: e.tensor_tensor(ttmp6[:, 0:tok], xk, PA[:, 0:tok], ALU.mult))
        v_op(lambda e, ok=ok: e.tensor_tensor(ok, ttmp6[:, 0:tok], PB[:, 0:tok], ALU.add))
        v_op(lambda e, k=k: e.tensor_reduce(mxcols.ap()[:, k:k + 1], ln2x[k].ap()[:, :],
                                            AX.X, ALU.max, apply_absolute_value=True))
        app6[k] = v_mark()
    v_ln2 = app6[nk - 1]

    def fold_cols(c0, c1, slot, wait_val):
        P.wait("vector", "V", wait_val)
        sep("vector")
        v_op(lambda e, c0=c0, c1=c1: e.tensor_reduce(bcol(B_RUN), mxcols.ap()[:, c0:c1],
                                                     AX.X, ALU.max))
        vr = v_mark(drain=True)
        P.wait("tensor", "V", vr)
        mm(PA[0:1, 0:128], bcol(B_RUN), ident_f[:, :], True, True)
        tf = t_mark()
        P.wait("vector", "T", tf)
        v_op(lambda e, slot=slot: e.tensor_reduce(sc(slot), PA[0:1, 0:128], AX.X, ALU.max))
        return v_mark(drain=True)

    def barrier_start(idx, wait_v):
        P.wait("sync", "V", wait_v)
        lbr = dma("sync", sc_cc_in[idx][:, :], scal.ap()[0:1, 0:16], "LBR")
        P.wait("gpsimd", "LBR", lbr)
        P.emit("gpsimd", lambda e, idx=idx: e.collective_compute(
            "AllReduce", ALU.max, replica_groups=[list(range(N_CORES))],
            ins=[sc_cc_in[idx].ap()], outs=[sc_cc_out[idx].ap()]).then_inc(SEMS["CC"], 1))
        return P.inc("CC", 1)

    def barrier_end(idx, cb):
        P.wait("sync", "CC", cb)
        return dma("sync", scal.ap()[0:1, 0:16], sc_cc_out[idx][:, :], "GBR")

    def barrier(idx, wait_v):
        return barrier_end(idx, barrier_start(idx, wait_v))

    def bcast_scalar(src_slot, dst_col, wait_sem, wait_val):
        P.wait("tensor", wait_sem, wait_val)
        mm(PA[:, 0:1], onerow_f[0:1, :], sc(src_slot), True, True)
        tf = t_mark()
        P.wait("vector", "T", tf)
        v_op(lambda e, dst_col=dst_col: e.tensor_copy(bcol(dst_col), PA[:, 0:1]))
        return v_mark(drain=True)

    weight_prep(wd_h, nk, di // 2, 1, None, wdT_dram, wraw_s, wi8_s, wdq_s, wsc_s, wqf_s,
                ("WR0", "WR1", "WR2"), entry=wd_entry, col0=0, j0ofs=0)
    weight_prep(wd_h, nk, di // 2, 1, None, wdT_dram, wraw_s, wi8_s, wdq_s, wsc_s, wqf_s,
                ("WR0", "WR1", "WR2"), col0=di // 2, j0ofs=di // 256)
    spwd_all = (P.now("SPWD"), P.now("SPW1"))

    vf0 = fold_cols(0, nk, 0, v_ln2)
    gb0 = barrier(0, vf0)
    P.wait("vector", "GBR", gb0)
    v_op(lambda e: e.tensor_scalar(sc(S_S1), sc(0), 1.0 / QMAX, GUARD, ALU.mult, ALU.max))
    sep("vector")
    v_op(lambda e: e.reciprocal(sc(S_C1), sc(S_S1)))
    v_s1 = v_mark(drain=True)
    v_c1b = bcast_scalar(S_C1, B_C1, "V", v_s1)

    P.wait("vector", "V", v_c1b)
    xq16s = [xq16.ap()[:, :], xq16b.ap()[:, :]]
    sxq_m = {}
    for k in range(nk):
        src = ln2x[k].ap()[:, :]
        if k >= 2:
            P.wait("vector", "S", sxq_m[k - 2])
        v_op(lambda e, src=src, k=k: e.tensor_scalar(xq16s[k % 2], src, bcol(B_C1), None, ALU.mult))
        vvq = v_mark()
        P.wait("scalar", "V", vvq)
        s_op(lambda e, k=k: e.activation(xq[k].ap()[:, :], xq16s[k % 2], AF.Copy))
        sxq_m[k] = s_mark()
    v_xq = P.now("V")
    s_xq = P.now("S")

    P.wait("tensor", "V", v_xq)
    P.wait("tensor", "S", s_xq)
    a_evac = {}
    thw = min(1024, tok)
    nth = (tok + 1023) // 1024
    LWS = ("LW0", "LW1", "LW2", "LW3")
    arrW = {}
    ta_m = {}
    P.wait("sync", "SPWD", spwd_all[0])
    P.wait("sync", "SPW1", spwd_all[1])
    P.wait("sync", "V", v_xq)
    P.wait("sync", "S", s_xq)

    def loadw(m):
        dma("sync", wst_u[m % 4][:, :], wuT_dram[m * 128:(m + 1) * 128, :], LWS[m % 4])
        arrW[m] = dma("sync", wst_g[m % 4][:, :], wgT_dram[m * 128:(m + 1) * 128, :], LWS[m % 4])

    loadw(0)
    loadw(1)
    for m in range(ni):
        if m + 2 < ni:
            if m >= 2:
                P.wait("sync", "T", ta_m[m - 2])
            loadw(m + 2)
        P.wait("tensor", LWS[m % 4], arrW[m])
        for th in range(nth):
            pi = (nth * m + th) % 2
            pg = pages[pi]
            if nth * m + th >= 2:
                P.wait("tensor", *a_evac[pi])
            for k in range(nk):
                lu = wst_u[m % 4][:, k * 128:(k + 1) * 128]
                lg = wst_g[m % 4][:, k * 128:(k + 1) * 128]
                for s in range(max(1, thw // 512)):
                    mv = xq[k].ap()[:, th * 1024 + s * 512: th * 1024 + s * 512 + 512]
                    mm(pg[:, s * 512:(s + 1) * 512], lu, mv, start=(k == 0), stop=(k == nk - 1))
                    mm(pg[:, 1024 + s * 512: 1024 + (s + 1) * 512], lg, mv, start=(k == 0), stop=(k == nk - 1))
            ta = t_mark()
            ta_m[m] = ta
            P.wait("scalar", "T", ta)
            if m >= 2 and th == 0:
                P.wait("scalar", ("SA0", "SA1")[m % 2], P.now(("SA0", "SA1")[m % 2]))
            us = o_bf[m]
            gs = spill_t[2 + m % 2].ap()
            tb = th * 1024
            s_op(lambda e, pg=pg, us=us, tb=tb: e.copy(us[:, tb:tb + thw], pg[:, 0:thw]))
            s_op(lambda e, pg=pg, gs=gs, tb=tb: e.copy(gs[:, tb:tb + thw], pg[:, 1024:1024 + thw]))
            a_evac[pi] = ("S", s_mark())
        P.wait("vector", "S", a_evac[(nth * m + nth - 1) % 2][1])
        v_op(lambda e, m=m: e.tensor_reduce(mxcols.ap()[:, m:m + 1], o_bf[m][:, :],
                                            AX.X, ALU.max, apply_absolute_value=True))
        v_op(lambda e, m=m: e.tensor_reduce(mxcols.ap()[:, 32 + m:33 + m], spill_t[2 + m % 2].ap()[:, :],
                                            AX.X, ALU.max, apply_absolute_value=True))
        va = v_mark()
        P.wait("sync", "V", va)
        dma("sync", gate_dram[m * 128:(m + 1) * 128, :], spill_t[2 + m % 2].ap()[:, :], ("SA0", "SA1")[m % 2])
    v_passA = P.now("V")
    sa_all = (P.now("SA0"), P.now("SA1"))

    vfu = fold_cols(0, ni, 0, v_passA)
    vfg = fold_cols(32, 32 + ni, 1, vfu)
    cb1 = barrier_start(1, vfg)
    P.wait("sync", "SA0", sa_all[0])
    P.wait("sync", "SA1", sa_all[1])
    arrB = {}
    arrB[0] = dma("sync", spill_t[2].ap()[:, :], gate_dram[0:128, :], "LB0")
    gb1 = barrier_end(1, cb1)

    P.wait("vector", "GBR", gb1)
    v_op(lambda e: e.tensor_tensor(sc(S_T0), sc(0), sc(S_S1), ALU.mult))
    sep("vector")
    v_op(lambda e: e.tensor_scalar(sc(S_S2), sc(S_T0), 1.0 / QMAX, GUARD, ALU.mult, ALU.max))
    sep("vector")
    v_op(lambda e: e.reciprocal(sc(S_T0), sc(S_S2)))
    sep("vector")
    v_op(lambda e: e.tensor_tensor(sc(S_F2), sc(S_T0), sc(S_S1), ALU.mult))
    v_op(lambda e: e.tensor_tensor(sc(S_T1), sc(1), sc(S_S1), ALU.mult))
    sep("vector")
    v_op(lambda e: e.tensor_scalar(sc(S_S3), sc(S_T1), 1.0 / QMAX, GUARD, ALU.mult, ALU.max))
    sep("vector")
    v_op(lambda e: e.reciprocal(sc(S_T1), sc(S_S3)))
    sep("vector")
    v_op(lambda e: e.tensor_tensor(sc(S_F3), sc(S_T1), sc(S_S1), ALU.mult))
    v_sc1 = v_mark(drain=True)
    v_b1 = bcast_scalar(S_F2, B_F2, "V", v_sc1)
    v_b2 = bcast_scalar(S_F3, B_F3, "V", v_b1)
    v_b3 = bcast_scalar(S_S3, B_S3, "V", v_b2)

    P.wait("vector", "V", v_b3)
    b_done = {}
    gcb = [gi16.ap()[:, :], gi16b.ap()[:, :], gi16c.ap()[:, :]]
    sgf = [sigf[:, :], sigfb[:, :]]
    sg32 = [sig32.ap()[:, :], sig32b.ap()[:, :]]
    vg_m = {}
    vs_m = {}
    sg_m = {}
    gp_m = {}
    for m in range(ni + 2):
        p = m % 2
        if m < ni:
            if m + 1 < ni:
                if m >= 1:
                    P.wait("sync", "V", vg_m[m - 1])
                arrB[m + 1] = dma("sync", spill_t[2 + (m + 1) % 2].ap()[:, :],
                                  gate_dram[(m + 1) * 128:(m + 2) * 128, :], ("LB0", "LB1")[(m + 1) % 2])
            P.wait("vector", ("LB0", "LB1")[p], arrB[m])
            if m >= 3:
                P.wait("vector", "G", gp_m[m - 3])
            v_op(lambda e, m=m: e.tensor_scalar(upq[m].ap()[:, :], o_bf[m][:, :],
                                                bcol(B_F2), None, ALU.mult))
            v_op(lambda e, m=m, p=p: e.tensor_scalar(gcb[m % 3], spill_t[2 + p].ap()[:, :],
                                                     bcol(B_F3), None, ALU.mult))
            vg_m[m] = v_mark()
            P.wait("scalar", "V", vg_m[m])
            if m >= 2:
                P.wait("scalar", "V", vs_m[m - 2])
            s_op(lambda e, m=m, p=p: e.activation(sgf[p], gcb[m % 3], AF.Sigmoid, scale=bcol(B_S3)))
            sg_m[m] = s_mark()
        mm1 = m - 1
        if 0 <= mm1 < ni:
            q = mm1 % 2
            P.wait("vector", "S", sg_m[mm1])
            v_op(lambda e, q=q: e.tensor_scalar(sg32[q], sgf[q], 65536.0, None, ALU.mult))
            vs_m[mm1] = v_mark()
            P.wait("gpsimd", "V", vs_m[mm1])
            if mm1 >= 2:
                P.wait("gpsimd", ("SB0", "SB1")[q], P.now(("SB0", "SB1")[q]))
            g_op(lambda e, q=q, mm1=mm1: e.tensor_tensor(prodt[q].ap()[:, :], gcb[mm1 % 3],
                                                          sg32[q], ALU.mult))
            gp_m[mm1] = g_mark()
        mm2 = m - 2
        if mm2 >= 0:
            q2 = mm2 % 2
            P.wait("vector", "G", gp_m[mm2])
            v_op(lambda e, q2=q2, mm2=mm2: e.tensor_reduce(mxcols.ap()[:, mm2:mm2 + 1],
                                                           prodt[q2].ap()[:, :], AX.X,
                                                           ALU.max, apply_absolute_value=True))
            b_done[q2] = v_mark()
            P.wait("sync", "V", b_done[q2])
            dma("sync", prod_dram[mm2 * 128:(mm2 + 1) * 128, :], prodt[q2].ap()[:, :],
                ("SB0", "SB1")[q2])
    v_passB = P.now("V")
    sb_all = (P.now("SB0"), P.now("SB1"))

    vf2 = fold_cols(0, ni, 0, v_passB)
    cb2 = barrier_start(2, vf2)
    P.wait("sync", "SB0", sb_all[0])
    P.wait("sync", "SB1", sb_all[1])
    arrC = {0: dma("sync", spill_t[0].ap()[:, :], prod_dram[0:128, :], "LC0")}
    P.wait("sync", "SPWD", spwd_all[0])
    P.wait("sync", "SPW1", spwd_all[1])
    for i in range(ni):
        dma("sync", wdT[i][:, :], wdT_dram[i * 128:(i + 1) * 128, :], "LWDT")
    lwdt_all = P.now("LWDT")
    gb2 = barrier_end(2, cb2)
    P.wait("vector", "GBR", gb2)
    v_op(lambda e: e.tensor_scalar(sc(S_T0), sc(S_S3), 1.0 / 65536.0, None, ALU.mult))
    sep("vector")
    v_op(lambda e: e.tensor_tensor(sc(S_T1), sc(0), sc(S_T0), ALU.mult))
    sep("vector")
    v_op(lambda e: e.tensor_scalar(sc(S_S4), sc(S_T1), 1.0 / QMAX, GUARD, ALU.mult, ALU.max))
    sep("vector")
    v_op(lambda e: e.reciprocal(sc(S_T1), sc(S_S4)))
    sep("vector")
    v_op(lambda e: e.tensor_tensor(sc(S_C4), sc(S_T1), sc(S_T0), ALU.mult))
    v_sc2 = v_mark(drain=True)
    v_b4 = bcast_scalar(S_C4, B_C4, "V", v_sc2)

    P.wait("vector", "V", v_b4)
    P.wait("vector", "G", P.now("G"))
    htk = tok // 2
    c_done = {}
    gc = [gi16, gi16b]
    vc_m = {}
    gc_m = {}
    for m in range(ni + 2):
        p = m % 2
        if m < ni:
            if m + 1 < ni:
                if m >= 1:
                    P.wait("sync", "V", vc_m[m - 1])
                arrC[m + 1] = dma("sync", spill_t[(m + 1) % 2].ap()[:, :],
                                  prod_dram[(m + 1) * 128:(m + 2) * 128, :], ("LC0", "LC1")[(m + 1) % 2])
            P.wait("vector", ("LC0", "LC1")[p], arrC[m])
            if m >= 2:
                P.wait("vector", ("SC0", "SC1")[p], P.now(("SC0", "SC1")[p]))
                P.wait("vector", "G", gc_m[m - 2])
            v_op(lambda e, m=m, p=p: e.tensor_scalar(gc[p].ap()[:, :], spill_t[p].ap()[:, :],
                                                     bcol(B_C4), None, ALU.mult))
            vc_m[m] = v_mark()
        mm2 = m - 2
        if mm2 >= 0:
            q2 = mm2 % 2
            P.wait("vector", "G", gc_m[mm2])
            v_op(lambda e, q2=q2, mm2=mm2: e.tensor_reduce(mxcols.ap()[:, mm2:mm2 + 1],
                                                           of32.ap()[:, q2 * tok:q2 * tok + tok],
                                                           AX.X, ALU.max, apply_absolute_value=True))
            c_done[q2] = v_mark()
        if m < ni:
            P.wait("sync", "V", vc_m[m])
            dma("sync", g2_dram[m * 128:(m + 1) * 128, :], gc[p].ap()[:, :], ("SC0", "SC1")[p])
            P.wait("gpsimd", "V", vc_m[m])
            if m >= 2:
                P.wait("gpsimd", "V", c_done[p])
            g_op(lambda e, m=m, p=p: e.tensor_tensor(of32.ap()[:, p * tok:p * tok + htk],
                                                     gc[p].ap()[:, 0:htk],
                                                     upq[m].ap()[:, 0:htk], ALU.mult))
            gc_m[m] = g_mark()
            v_op(lambda e, m=m, p=p: e.tensor_tensor(of32.ap()[:, p * tok + htk:p * tok + tok],
                                                     gc[p].ap()[:, htk:tok],
                                                     upq[m].ap()[:, htk:tok], ALU.mult))
    v_passC = P.now("V")
    scc_all = (P.now("SC0"), P.now("SC1"))

    vf3 = fold_cols(0, ni, 0, v_passC)
    cb3 = barrier_start(3, vf3)
    P.wait("sync", "SC0", scc_all[0])
    P.wait("sync", "SC1", scc_all[1])
    P.wait("sync", "G", P.now("G"))
    arrD = {0: dma("sync", gi16.ap()[:, :], g2_dram[0:128, :], "LP0")}
    gb3 = barrier_end(3, cb3)
    P.wait("vector", "GBR", gb3)
    v_op(lambda e: e.tensor_tensor(sc(S_T0), sc(S_S4), sc(S_S2), ALU.mult))
    sep("vector")
    v_op(lambda e: e.tensor_tensor(sc(S_T1), sc(0), sc(S_T0), ALU.mult))
    sep("vector")
    v_op(lambda e: e.tensor_scalar(sc(S_S5), sc(S_T1), 1.0 / QMAX, GUARD, ALU.mult, ALU.max))
    sep("vector")
    v_op(lambda e: e.reciprocal(sc(S_T1), sc(S_S5)))
    sep("vector")
    v_op(lambda e: e.tensor_tensor(sc(S_C5), sc(S_T1), sc(S_T0), ALU.mult))
    v_sc3 = v_mark(drain=True)
    v_b5 = bcast_scalar(S_C5, B_C5, "V", v_sc3)
    v_b6 = bcast_scalar(S_S5, B_S5, "V", v_b5)

    P.wait("vector", "V", v_b6)
    P.wait("gpsimd", "V", v_b6)
    d_done = {}
    sD_m = {}
    gslot = [gi16.ap()[:, :], gi16b.ap()[:, :]]
    gd_m = {}
    for m in range(ni + 1):
        p = m % 2
        if m < ni:
            if m + 1 < ni:
                if m >= 1:
                    P.wait("sync", "G", gd_m[m - 1])
                    if m >= 2:
                        P.wait("sync", "V", d_done[m % 2])
                arrD[m + 1] = dma("sync", gslot[(m + 1) % 2], g2_dram[(m + 1) * 128:(m + 2) * 128, :],
                                  ("LP0", "LP1")[(m + 1) % 2])
            P.wait("gpsimd", ("LP0", "LP1")[p], arrD[m])
            if m >= 2:
                P.wait("gpsimd", "V", d_done[p])
            g_op(lambda e, m=m, p=p: e.tensor_tensor(of32.ap()[:, p * tok:p * tok + htk],
                                                     gslot[p][:, 0:htk],
                                                     upq[m].ap()[:, 0:htk], ALU.mult))
            gd_m[m] = g_mark()
            P.wait("vector", ("LP0", "LP1")[p], arrD[m])
            v_op(lambda e, m=m, p=p: e.tensor_tensor(of32.ap()[:, p * tok + htk:p * tok + tok],
                                                     gslot[p][:, htk:tok],
                                                     upq[m].ap()[:, htk:tok], ALU.mult))
        mm1 = m - 1
        if mm1 >= 0:
            q = mm1 % 2
            P.wait("vector", "G", gd_m[mm1])
            if mm1 >= 2:
                P.wait("vector", "S", sD_m[mm1 - 2])
            v_op(lambda e, q=q: e.tensor_scalar(sg32[q], of32.ap()[:, q * tok:q * tok + tok],
                                                bcol(B_C5), None, ALU.mult))
            d_done[q] = v_mark()
            P.wait("scalar", "V", d_done[q])
            s_op(lambda e, q=q, mm1=mm1: e.activation(o_bf[mm1][:, :], sg32[q], AF.Copy))
            sD_m[mm1] = s_mark()
    v_passD = P.now("V")
    s_passD = P.now("S")

    P.wait("tensor", "V", v_passD)
    P.wait("tensor", "S", s_passD)
    P.wait("tensor", "LWDT", lwdt_all)
    dn_evac = {}
    for h in range(nk):
        pg = pages[h % 2]
        if h >= 2:
            P.wait("tensor", *dn_evac[h % 2])
        for i in range(ni):
            lhs = wdT[i][:, h * 128:(h + 1) * 128]
            for s in range(ns):
                mm(pg[:, s * 512:(s + 1) * 512], lhs,
                   o_bf[i][:, s * 512:(s + 1) * 512],
                   start=(i == 0), stop=(i == ni - 1))
        td = t_mark()
        if h == 0:
            P.wait("sync", "V", v_passD)
        lxr = dma("sync", of32.ap()[:, tok:2 * tok], xres_dram[h * 128:(h + 1) * 128, :], "LXR")
        P.wait("scalar", "T", td)
        if h >= 1:
            P.wait("scalar", "SOUT", P.now("SOUT"))
        s_op(lambda e, pg=pg: e.activation(of32.ap()[:, 0:tok], pg[:, 0:tok], AF.Copy, scale=bcol(B_S5)))
        sd = s_mark()
        P.wait("vector", "S", sd)
        P.wait("vector", "LXR", lxr)
        v_op(lambda e: e.tensor_tensor(of32.ap()[:, 0:tok], of32.ap()[:, 0:tok],
                                       of32.ap()[:, tok:2 * tok], ALU.add))
        vd = v_mark()
        dn_evac[h % 2] = ("V", vd)
        P.wait("sync", "V", vd)
        dma("sync", outT[h * 128:(h + 1) * 128, :], of32.ap()[:, 0:tok], "SOUT")
    if debug:
        d_obf = nc.dram_tensor("d_obf", [di, tok], BF16, kind="ExternalOutput")
        for i in range(ni):
            dma("sync", d_obf.ap()[i * 128:(i + 1) * 128, :], o_bf[i][:, :], "SOUT")
        dbg = {"d_xres": xres_dram, "d_up": up_dram, "d_gate": gate_dram,
               "d_prod": prod_dram, "d_g2": g2_dram, "d_wdT": wdT_dram,
               "d_kv": kv_cc_out}
        for nm, t in dbg.items():
            o = nc.dram_tensor(nm, list(t.shape), t.dtype, kind="ExternalOutput")
            dma("sync", o.ap(), t.ap(), "SOUT")
        o1 = nc.dram_tensor("d_scal", [1, 64], F32, kind="ExternalOutput")
        dma("sync", o1.ap(), scal.ap()[0:1, :], "SOUT")
        o2 = nc.dram_tensor("d_bc", [128, 16], F32, kind="ExternalOutput")
        dma("sync", o2.ap(), bc.ap()[:, :], "SOUT")
    P.wait("sync", "SOUT", P.now("SOUT"))
    return P


def build(nc, d=D, tok=TOK, di=DI, debug=False):
    global SEMS
    SEMS = {}
    import contextlib
    with contextlib.ExitStack() as es:
        for s in SEM_NAMES:
            SEMS[s] = es.enter_context(nc.semaphore(s))
        P = build_program(nc, d, tok, di, debug)
        with nc.Block() as block:
            for eng_name in Plan.ENGINES:
                deco = getattr(block, eng_name)

                def make(fns):
                    def body(e):
                        for f in fns:
                            f(e)
                    return body
                deco(make(P.q[eng_name]))
    return nc


def kernel(L0_attn_x, L0_attn_W, L0_ffn_w_gate, L0_ffn_w_up, L0_ffn_w_down):
    x = np.ascontiguousarray(L0_attn_x, dtype=np.float32)
    W = np.ascontiguousarray(L0_attn_W, dtype=np.float32)
    wg = np.ascontiguousarray(L0_ffn_w_gate, dtype=np.float32)
    wu = np.ascontiguousarray(L0_ffn_w_up, dtype=np.float32)
    wd = np.ascontiguousarray(L0_ffn_w_down, dtype=np.float32)
    ident = np.eye(128, dtype=np.float32)

    nc = bass.Bass("TRN2", target_bir_lowering=False, detect_race_conditions=False)
    build(nc)

    in_maps = []
    for c in range(N_CORES):
        b, h = c >> 1, c & 1
        xTc = np.ascontiguousarray(x[b, h * TOK:(h + 1) * TOK, :].T)
        in_maps.append({"xT": xTc, "W": W, "wg": wg, "wu": wu, "wd": wd, "ident": ident})
    res = run_bass_kernel_spmd(nc, in_maps, core_ids=list(range(N_CORES)))
    out = np.empty((4, 4096, D), dtype=np.float32)
    for c in range(N_CORES):
        b, h = c >> 1, c & 1
        out[b, h * TOK:(h + 1) * TOK, :] = res.results[c]["outT"].T
    return out


# version 32
# speedup vs baseline: 1.0354x; 1.0354x over previous
import numpy as np
import concourse.bass as bass
import concourse.mybir as mybir
from concourse.bass_utils import run_bass_kernel_spmd

F32 = mybir.dt.float32
BF16 = mybir.dt.bfloat16
I32 = mybir.dt.int32
I16 = mybir.dt.int16
I8 = mybir.dt.int8
AF = mybir.ActivationFunctionType
ALU = mybir.AluOpType
AX = mybir.AxisListType

D = 1024
TOK = 2048
DI = 3072
BLK = 32
N_CORES = 8

LN_EPS = 1e-5
ATT_EPS = 1e-6
QMAX = 32767.0
GUARD = 1e-12

S_S1, S_C1, S_S2, S_F2, S_S3, S_F3, S_S4, S_C4, S_S5, S_C5, S_T0, S_T1 = range(16, 28)
S_MHALF = 30
S_MONE = 31
B_C1, B_F2, B_F3, B_S3, B_C4, B_C5, B_S5 = range(7)
B_UP, B_GATE, B_TMP, B_RUN = 10, 11, 12, 15

SEM_NAMES = ["T", "V", "S", "CC", "G",
             "LX", "GW", "KV1", "KV2", "KV3", "LX2", "SXR", "LR0", "LR1", "WR0", "WR1",
             "SPWD", "SPW1", "LWDT", "SA0", "SA1", "LB0", "LB1", "SB0", "SB1", "SC0", "SC1", "WR2",
             "LC0", "LC1", "LP0", "LP1", "LXR", "SOUT", "LBR", "GBR", "LW0", "LW1", "LW2", "LW3"]


class Plan:
    ENGINES = ("sync", "gpsimd", "tensor", "vector", "scalar")

    def __init__(self):
        self.q = {e: [] for e in self.ENGINES}
        self.cnt = {s: 0 for s in SEM_NAMES}
        self._last_wait = {e: False for e in self.ENGINES}

    def inc(self, name, k):
        self.cnt[name] += k
        return self.cnt[name]

    def now(self, name):
        return self.cnt[name]

    def emit(self, eng, fn):
        self.q[eng].append(fn)
        self._last_wait[eng] = False

    def wait(self, eng, sem_name, val=None):
        v = self.cnt[sem_name] if val is None else val
        if v > 0:
            if self._last_wait[eng]:
                self.q[eng].append(lambda e: e.nop())
            self.q[eng].append(lambda e, s=sem_name, vv=v: e.wait_ge(SEMS[s], vv))
            self._last_wait[eng] = True


SEMS = {}


class SbufMap:
    BASE = 16640
    TOP = 229360
    LIMIT = TOP - BASE

    def __init__(self, nc):
        self.nc = nc
        self.items = []

    def alloc(self, name, shape, dtype, ph0, ph1, align=64):
        esz = {F32: 4, BF16: 2, I32: 4, I16: 2, I8: 1}[dtype]
        nbytes = int(np.prod(shape[1:])) * esz
        nbytes_al = (nbytes + align - 1) // align * align
        taken = sorted((it[1], it[1] + it[2]) for it in self.items
                       if not (it[4] < ph0 or it[3] > ph1))
        merged = []
        for s, e in taken:
            if merged and s <= merged[-1][1]:
                merged[-1][1] = max(merged[-1][1], e)
            else:
                merged.append([s, e])
        holes = []
        prev = 0
        for s, e in merged:
            if s - prev >= nbytes_al:
                holes.append((prev, s - prev))
            prev = max(prev, e)
        if self.LIMIT - prev >= nbytes_al:
            holes.append((prev, self.LIMIT - prev))
        assert holes, f"SBUF overflow: {name} needs {nbytes_al} (no hole)"
        off = min(holes, key=lambda h: h[1])[0]
        self.items.append((name, off, nbytes_al, ph0, ph1))
        return self.nc.alloc_sbuf_tensor_at(name, list(shape), dtype, offset=self.BASE + off)

    def alloc_chunks(self, name, n, shape, dtype, ph0, ph1, align=64):
        return [self.alloc(f"{name}{i}", shape, dtype, ph0, ph1, align) for i in range(n)]


def build_program(nc, d=D, tok=TOK, di=DI, debug=False):
    nk = d // 128
    ni = di // 128
    ntc = tok // 128
    ns = tok // 512
    nblk_d = d // BLK
    nblk_i = di // BLK

    xT = nc.dram_tensor("xT", [d, tok], F32, kind="ExternalInput").ap()
    W_h = nc.dram_tensor("W", [d, 3 * d], F32, kind="ExternalInput").ap()
    wg_h = nc.dram_tensor("wg", [di, d], F32, kind="ExternalInput").ap()
    wu_h = nc.dram_tensor("wu", [di, d], F32, kind="ExternalInput").ap()
    wd_h = nc.dram_tensor("wd", [d, di], F32, kind="ExternalInput").ap()
    ident_h = nc.dram_tensor("ident", [128, 128], F32, kind="ExternalInput").ap()
    outT = nc.dram_tensor("outT", [d, tok], F32, kind="ExternalOutput").ap()

    up_dram = nc.dram_tensor("up_dram", [di, tok], BF16)
    gate_dram = nc.dram_tensor("gate_dram", [di, tok], BF16)
    prod_dram = nc.dram_tensor("prod_dram", [di, tok], BF16)
    g2_dram = nc.dram_tensor("g2_dram", [di, tok], I16)
    xres_dram = nc.dram_tensor("xres_dram", [d, tok], F32)
    wdT_dram = nc.dram_tensor("wdT_dram", [di, d], BF16)
    wuT_dram = nc.dram_tensor("wuT_dram", [di, d], BF16)
    wgT_dram = nc.dram_tensor("wgT_dram", [di, d], BF16)
    nh = nk // 2
    hw_kv = 16 + nh * d
    kv_cc_inA = nc.dram_tensor("kv_cc_inA", [128, hw_kv], F32)
    kv_cc_outA = nc.dram_tensor("kv_cc_outA", [128, hw_kv], F32)
    kv_cc_inB = nc.dram_tensor("kv_cc_inB", [128, hw_kv], F32)
    kv_cc_outB = nc.dram_tensor("kv_cc_outB", [128, hw_kv], F32)
    sc_cc_in = [nc.dram_tensor(f"sc_cc_in{i}", [1, 16], F32) for i in range(4)]
    sc_cc_out = [nc.dram_tensor(f"sc_cc_out{i}", [1, 16], F32) for i in range(4)]
    cc_w_in = nc.dram_tensor("cc_w_in", [1, 16], F32)
    cc_w_out = nc.dram_tensor("cc_w_out", [1, 16], F32)

    sm = SbufMap(nc)
    ident_f = sm.alloc("ident_f", [128, 128], F32, 0, 119)
    ident_b = sm.alloc("ident_b", [128, 128], BF16, 0, 119)
    ones_b = sm.alloc("ones_b", [128, 1], BF16, 0, 119)
    onerow_f = sm.alloc("onerow_f", [1, 128], F32, 0, 119)
    row0 = sm.alloc("row0", [1, tok], F32, 0, 79)
    row1a = sm.alloc("row1a", [1, tok], F32, 10, 29)
    row1b = sm.alloc("row1b", [1, tok], F32, 60, 69)
    scal = sm.alloc("scal", [1, 64], F32, 0, 119)
    bc = sm.alloc("bc", [128, 16], F32, 0, 119)
    mxcols = sm.alloc("mxcols", [128, 64], F32, 0, 119)

    W_bf = sm.alloc("W_bf", [128, nk * 3 * d], BF16, 0, 29)
    ttmp = sm.alloc("ttmp", [128, tok], F32, 10, 29)
    ttmp6 = sm.alloc("ttmp6", [128, tok], F32, 60, 69)
    ln1x = sm.alloc_chunks("ln1x", nk, [128, tok], BF16, 10, 29)
    qe = sm.alloc_chunks("qe", nk, [128, tok], BF16, 20, 59)
    kf = sm.alloc_chunks("kf", nk, [128, tok], BF16, 20, 31)
    vf = sm.alloc_chunks("vf", nk, [128, tok], BF16, 20, 33)
    x_f = sm.alloc_chunks("x_f", nk, [128, tok], F32, 10, 19)
    xbf_t = sm.alloc("xbf_t", [128, 4 * tok], BF16, 10, 19)
    keT = sm.alloc_chunks("keT", ntc, [128, d], BF16, 30, 49)
    vT = sm.alloc_chunks("vT", ntc, [128, d], BF16, 34, 49)
    kvt_f = sm.alloc("kvt_f", [128, nk * d + 32], F32, 40, 49)
    kvtks = sm.alloc("kvtks", [128, nk * d + 32], BF16, 50, 59)
    zbc = sm.alloc("zbc", [128, tok], BF16, 50, 59)
    xres_t = sm.alloc("xres_t", [128, tok], F32, 50, 59)
    xres_t2 = sm.alloc("xres_t2", [128, tok], F32, 50, 59)
    xresf = sm.alloc_chunks("xresf", nk, [128, tok], F32, 50, 69)
    wst_u = sm.alloc_chunks("wst_u", 4, [128, nk * 128], BF16, 80, 89)
    wst_g = sm.alloc_chunks("wst_g", 4, [128, nk * 128], BF16, 80, 89)
    wraw_s = sm.alloc("wraw_s", [128, 6 * d], F32, 30, 79)
    wqf_s = None
    wi8_s = sm.alloc("wi8_s", [128, 4 * d], I8, 50, 79)
    wdq_s = sm.alloc("wdq_s", [128, 4 * d], BF16, 50, 79)
    wsc_s = sm.alloc("wsc_s", [128, 12 * nblk_d], F32, 50, 79)
    xbf_t2 = sm.alloc("xbf_t2", [128, 4 * tok], BF16, 60, 69)
    ln2x = sm.alloc_chunks("ln2x", nk, [128, tok], BF16, 60, 79)
    xq = sm.alloc_chunks("xq", nk, [128, tok], BF16, 70, 89)
    xq16 = sm.alloc("xq16", [128, tok], I16, 70, 79)
    xq16b = sm.alloc("xq16b", [128, tok], I16, 70, 79)
    spill_t = sm.alloc_chunks("spill_t", 4, [128, tok], BF16, 80, 119)
    upq = sm.alloc_chunks("upq", ni, [128, tok], I16, 80, 119)
    gi16 = sm.alloc("gi16", [128, tok], I16, 90, 119)
    gi16b = sm.alloc("gi16b", [128, tok], I16, 90, 119)
    gi16c = sm.alloc("gi16c", [128, tok], I16, 90, 99)
    sig32 = sm.alloc("sig32", [128, tok], I32, 90, 119)
    sig32b = sm.alloc("sig32b", [128, tok], I32, 90, 119)
    sigf = sm.alloc("sigf", [128, tok], F32, 90, 99)
    sigfb = sm.alloc("sigfb", [128, tok], F32, 90, 99)
    prodt = sm.alloc_chunks("prodt", 2, [128, tok], BF16, 90, 99)
    wdT = sm.alloc_chunks("wdT", ni, [128, d], BF16, 100, 119)
    of32 = sm.alloc("of32", [128, 2 * tok], F32, 100, 119)

    o_bf = [upq[i].ap().bitcast(BF16) for i in range(ni)]

    PA = nc.alloc_psum_tensor("PA", [128, 2048], F32)
    PB = nc.alloc_psum_tensor("PB", [128, 2048], F32)
    pages = [PA, PB]

    P = Plan()

    def dma(eng_name, out_ap, in_ap, sem_name):
        v = P.inc(sem_name, 16)
        P.emit(eng_name, lambda e, o=out_ap, i=in_ap, s=sem_name:
               e.dma_start(out=o, in_=i).then_inc(SEMS[s], 16))
        return v

    def mm(out_ap, lhsT_ap, rhs_ap, start, stop):
        P.emit("tensor", lambda e, o=out_ap, l=lhsT_ap, r=rhs_ap, st=start, sp=stop:
               e.matmul(o, l, r, start=st, stop=sp))

    def sep(eng):
        P.emit(eng, lambda e: e.drain())

    _tagged = set()

    def _tag(eng, sem):
        key = (eng, len(P.q[eng]))
        assert key not in _tagged, f"double-tag on {eng} at {len(P.q[eng])}"
        _tagged.add(key)
        v = P.inc(sem, 1)
        fn = P.q[eng].pop()
        P.emit(eng, lambda e, f=fn, s=sem: f(e).then_inc(SEMS[s], 1))
        return v

    def t_mark():
        return _tag("tensor", "T")

    def v_op(fn):
        P.emit("vector", fn)

    def v_mark(drain=False):
        if drain:
            v = P.inc("V", 1)
            P.emit("vector", lambda e: e.drain().then_inc(SEMS["V"], 1))
            return v
        return _tag("vector", "V")

    def g_op(fn):
        P.emit("gpsimd", fn)

    def g_mark(drain=False):
        if drain:
            v = P.inc("G", 1)
            P.emit("gpsimd", lambda e: e.drain().then_inc(SEMS["G"], 1))
            return v
        return _tag("gpsimd", "G")

    def s_op(fn):
        P.emit("scalar", fn)

    def s_mark(drain=False):
        if drain:
            v = P.inc("S", 1)
            P.emit("scalar", lambda e: e.drain().then_inc(SEMS["S"], 1))
            return v
        return _tag("scalar", "S")

    R0 = row0.ap()[0:1, 0:tok]

    def sc(i):
        return scal.ap()[0:1, i:i + 1]

    def bcol(i):
        return bc.ap()[:, i:i + 1]

    dma("sync", ident_f[:, :], ident_h, "LX")
    P.wait("sync", "V", 1)
    ccw = dma("sync", cc_w_in.ap(), scal.ap()[0:1, 0:16], "LBR")
    P.wait("gpsimd", "LBR", ccw)
    P.emit("gpsimd", lambda e: e.collective_compute(
        "AllReduce", ALU.max, replica_groups=[[0, 1, 2, 3, 4, 5, 6, 7]],
        ins=[cc_w_in.ap()], outs=[cc_w_out.ap()]).then_inc(SEMS["CC"], 1))
    P.inc("CC", 1)
    P.emit("gpsimd", lambda e: e.collective_compute(
        "AllReduce", ALU.add, replica_groups=[[0, 1], [2, 3], [4, 5], [6, 7]],
        ins=[cc_w_in.ap()], outs=[cc_w_out.ap()]).then_inc(SEMS["CC"], 1))
    P.inc("CC", 1)
    dma("gpsimd", ident_b[:, :], ident_h, "GW")
    v_op(lambda e: e.memset(ones_b[:, :], 1.0))
    v_op(lambda e: e.memset(onerow_f[:, :], 1.0))
    v_op(lambda e: e.memset(scal.ap()[0:1, :], 0.0))
    v_op(lambda e: e.memset(sc(S_MHALF), -0.5))
    v_op(lambda e: e.memset(sc(S_MONE), -1.0))
    v_const = v_mark(drain=True)
    for k in range(nk):
        dma("gpsimd", W_bf[:, k * 3 * d:(k + 1) * 3 * d], W_h[k * 128:(k + 1) * 128, :], "GW")
    gw_all = P.now("GW")
    for k in range(nk // 2):
        dma("sync", x_f[k][:, :], xT[k * 128:(k + 1) * 128, :], "LX")
    lx_lo = P.now("LX")
    for k in range(nk // 2, nk):
        dma("scalar", x_f[k][:, :], xT[k * 128:(k + 1) * 128, :], "LX2")
    lx_hi = P.now("LX2")

    def ln_block(src, arrive, consumed, xbf_buf, R1):
        tks = {}
        for k in range(nk):
            xo = (k % 2) * tok
            so = 2 * tok + (k % 2) * tok
            arrive(k, "vector")
            if k >= 2:
                P.wait("vector", "T", tks[k - 2])
            v_op(lambda e, a=src(k), xb=xbf_buf, xo=xo: e.tensor_copy(xb.ap()[:, xo:xo + tok], a))
            vv = v_mark()
            arrive(k, "scalar")
            if k >= 2:
                P.wait("scalar", "T", tks[k - 2])
            s_op(lambda e, a=src(k), xb=xbf_buf, so=so: e.activation(xb.ap()[:, so:so + tok], a,
                                                                    AF.Square))
            sv = s_mark()
            P.wait("tensor", "V", vv)
            P.wait("tensor", "S", sv)
            for s in range(ns):
                mm(PA[0:1, s * 512:(s + 1) * 512], ones_b[:, 0:1],
                   xbf_buf.ap()[:, xo + s * 512: xo + (s + 1) * 512],
                   start=(k == 0), stop=(k == nk - 1))
                mm(PB[0:1, s * 512:(s + 1) * 512], ones_b[:, 0:1],
                   xbf_buf.ap()[:, so + s * 512: so + (s + 1) * 512],
                   start=(k == 0), stop=(k == nk - 1))
            tks[k] = t_mark()
            consumed(k, tks[k])
        tstat = P.now("T")
        P.wait("vector", "T", tstat)
        v_op(lambda e: e.tensor_scalar(R0, PA[0:1, 0:tok], -1.0 / d, None, ALU.mult))
        v_op(lambda e: e.tensor_scalar(R1, PB[0:1, 0:tok], 1.0 / d, None, ALU.mult))
        v_op(lambda e: e.tensor_tensor(PA[0:1, 0:tok], R0, R0, ALU.mult))
        v_op(lambda e: e.tensor_tensor(R1, R1, PA[0:1, 0:tok], ALU.subtract))
        v_op(lambda e: e.tensor_scalar(R1, R1, LN_EPS, None, ALU.add))
        vrec = v_mark(drain=True)
        P.wait("scalar", "V", vrec)
        s_op(lambda e: e.activation(R1, R1, AF.Ln))
        s_op(lambda e: e.activation(R1, R1, AF.Exp, scale=sc(S_MHALF)))
        ssq = s_mark(drain=True)
        P.wait("vector", "S", ssq)
        v_op(lambda e: e.tensor_tensor(R0, R0, R1, ALU.mult))
        vrows = v_mark(drain=True)
        P.wait("tensor", "V", vrows)
        for s in range(ns):
            mm(PA[:, s * 512:(s + 1) * 512], onerow_f[0:1, :], R1[:, s * 512:(s + 1) * 512], True, True)
            mm(PB[:, s * 512:(s + 1) * 512], onerow_f[0:1, :], R0[:, s * 512:(s + 1) * 512], True, True)
        return t_mark()

    P.wait("vector", "V", v_const)

    def arr1(k, eng):
        if k < nk // 2:
            P.wait(eng, "LX", lx_lo)
        else:
            P.wait(eng, "LX2", lx_hi)

    tb1 = ln_block(lambda k: x_f[k].ap()[:, :], arr1, lambda k, tk: None, xbf_t, row1a.ap()[0:1, :])
    P.wait("vector", "T", tb1)
    for k in range(nk):
        xk = x_f[k].ap()[:, :]
        ok = ln1x[k].ap()[:, :]
        v_op(lambda e, xk=xk: e.tensor_tensor(ttmp[:, 0:tok], xk, PA[:, 0:tok], ALU.mult))
        v_op(lambda e, ok=ok: e.tensor_tensor(ok, ttmp[:, 0:tok], PB[:, 0:tok], ALU.add))
    v_ln1 = v_mark()

    P.wait("tensor", "V", v_ln1)
    P.wait("tensor", "GW", gw_all)
    evac_done = {}
    morder = list(range(nk, 2 * nk)) + list(range(2 * nk, 3 * nk)) + list(range(nk))
    for idx, m in enumerate(morder):
        pg = pages[idx % 2]
        if idx >= 2:
            P.wait("tensor", *evac_done[idx % 2])
        for k in range(nk):
            lhs = W_bf.ap()[:, k * 3 * d + m * 128: k * 3 * d + (m + 1) * 128]
            for s in range(ns):
                mm(pg[:, s * 512:(s + 1) * 512], lhs,
                   ln1x[k].ap()[:, s * 512:(s + 1) * 512],
                   start=(k == 0), stop=(k == nk - 1))
        tq = t_mark()
        if m < 2 * nk:
            dstbuf = qe if m < nk else kf
            dst = dstbuf[m % nk].ap()[:, :]
            P.wait("vector", "T", tq)
            v_op(lambda e, pg=pg: e.tensor_scalar(ttmp[:, 0:tok], pg[:, 0:tok], 0.0, None, ALU.min))
            vv = v_mark()
            P.wait("scalar", "V", vv)
            s_op(lambda e: e.activation(ttmp[:, 0:tok], ttmp[:, 0:tok], AF.Exp))
            sv = s_mark()
            P.wait("vector", "S", sv)
            v_op(lambda e, pg=pg, dst=dst: e.scalar_tensor_tensor(
                dst, pg[:, 0:tok], 0.0, ttmp[:, 0:tok], ALU.max, ALU.add))
            evac_done[idx % 2] = ("V", v_mark())
        else:
            dst = vf[m - 2 * nk].ap()[:, :]
            P.wait("scalar", "T", tq)
            s_op(lambda e, pg=pg, dst=dst: e.copy(dst, pg[:, 0:tok]))
            evac_done[idx % 2] = ("S", s_mark())
    P.wait("tensor", *evac_done[0])
    P.wait("tensor", *evac_done[1])

    regions3 = [(PA, 0), (PA, 1024), (PB, 0), (PB, 1024)]
    tr_evac = {}
    ri = 0
    for src, dst in ((kf, keT), (vf, vT)):
        for c in range(ntc):
            pg, off = regions3[ri % 4]
            if ri >= 4:
                P.wait("tensor", *tr_evac[ri % 4])
            for k in range(nk):
                mm(pg[:, off + k * 128: off + (k + 1) * 128],
                   src[k].ap()[:, c * 128:(c + 1) * 128],
                   ident_b[:, :], True, True)
            tt = t_mark()
            P.wait("scalar", "T", tt)
            s_op(lambda e, pg=pg, off=off, dst=dst, c=c:
                 e.copy(dst[c].ap()[:, :], pg[:, off:off + d]))
            tr_evac[ri % 4] = ("S", s_mark())
            ri += 1
    for r in range(min(4, ri)):
        P.wait("tensor", *tr_evac[r])
    wu_entry = (P.now("S"), P.now("G"), P.now("V"))
    wu_entry_t = P.now("T")

    regions4 = [(PA, 0), (PA, 1024), (PB, 0)]
    kvt_evac = {}

    def kvoff(dk):
        return 16 + dk * d + (16 if dk >= nh else 0)

    def kscol(dk):
        return dk if dk < nh else 16 + nh * d + (dk - nh)

    v_half = {}
    for dk in range(nk):
        pg, off = regions4[dk % 3]
        if dk >= 3:
            P.wait("tensor", *kvt_evac[dk % 3])
        for c in range(ntc):
            lhs = keT[c].ap()[:, dk * 128:(dk + 1) * 128]
            w2 = min(512, d)
            for s2 in range(max(1, d // 512)):
                mm(pg[:, off + s2 * w2: off + (s2 + 1) * w2], lhs,
                   vT[c].ap()[:, s2 * w2:(s2 + 1) * w2],
                   start=(c == 0), stop=(c == ntc - 1))
            mm(PB[:, 2032 + dk: 2033 + dk], lhs, ones_b[:, 0:1],
               start=(c == 0), stop=(c == ntc - 1))
        tkv = t_mark()
        P.wait("vector", "T", tkv)
        v_op(lambda e, pg=pg, off=off, dk=dk:
             e.tensor_copy(kvt_f.ap()[:, kvoff(dk):kvoff(dk) + d], pg[:, off:off + d]))
        kvt_evac[dk % 3] = ("V", v_mark())
        if dk == nh - 1 or dk == nk - 1:
            ksb = 0 if dk < nh else 16 + nh * d
            dk0 = 0 if dk < nh else nh
            v_op(lambda e, ksb=ksb: e.memset(kvt_f.ap()[:, ksb:ksb + 16], 0.0))
            v_op(lambda e, ksb=ksb, dk0=dk0: e.tensor_copy(
                kvt_f.ap()[:, ksb:ksb + nh], PB[:, 2032 + dk0:2032 + dk0 + nh]))
            v_half[dk >= nh] = v_mark(drain=True)
    v_kvt = v_half[True]

    P.wait("gpsimd", "V", v_half[False])
    kv1a = dma("gpsimd", kv_cc_inA[:, :], kvt_f.ap()[:, 0:hw_kv], "KV1")
    P.wait("gpsimd", "KV1", kv1a)
    P.emit("gpsimd", lambda e: e.collective_compute(
        "AllReduce", ALU.add, replica_groups=[[0, 1], [2, 3], [4, 5], [6, 7]],
        ins=[kv_cc_inA.ap()], outs=[kv_cc_outA.ap()]).then_inc(SEMS["CC"], 1))
    cc_kva = P.inc("CC", 1)
    P.wait("gpsimd", "V", v_half[True])
    kv1b = dma("gpsimd", kv_cc_inB[:, :], kvt_f.ap()[:, hw_kv:2 * hw_kv], "KV1")
    P.wait("gpsimd", "KV1", kv1b)
    P.emit("gpsimd", lambda e: e.collective_compute(
        "AllReduce", ALU.add, replica_groups=[[0, 1], [2, 3], [4, 5], [6, 7]],
        ins=[kv_cc_inB.ap()], outs=[kv_cc_outB.ap()]).then_inc(SEMS["CC"], 1))
    cc_kv = P.inc("CC", 1)

    def weight_prep(src_h, nch, wc, gsz, dstT_sb, dstT_dram, wraw, wi8, wdq, wsc, wqf, wsem, dbuf=True, col0=0, j0ofs=0, mrow=False, entry=None, entry_t=None):
        njt = wc // 128
        groups = [(s, min(gsz, nch - s)) for s in range(0, nch, gsz)]
        ngr = len(groups)
        gw_max = gsz * wc
        eS, eG, eV = entry if entry is not None else (P.now("S"), P.now("G"), P.now("V"))
        if entry_t is not None:
            P.wait("sync", "T", entry_t)
            P.wait("scalar", "T", entry_t)
        P.wait("sync", "S", eS)
        P.wait("sync", "G", eG)
        P.wait("sync", "V", eV)
        P.wait("scalar", "V", eV)
        P.wait("vector", "G", eG)
        arr = {}
        g_marks = {}
        t_marks = {}
        vtt1_marks = {}
        tpose_evac = {}
        s_prev = P.now("S")
        spwd_prev = {0: P.now("SPWD"), 1: P.now("SPW1")}
        s_hist = {}
        spwd_hist = {}

        def load(g):
            half = (g % 3) * gw_max
            m0, gn = groups[g]
            sem = wsem[g % 3]
            if gn == 1:
                h = wc // 2
                dma("sync", wraw.ap()[:, half:half + h],
                    src_h[m0 * 128:(m0 + 1) * 128, col0:col0 + h], sem)
                arr[g] = dma("scalar", wraw.ap()[:, half + h:half + wc],
                             src_h[m0 * 128:(m0 + 1) * 128, col0 + h:col0 + wc], sem)
            else:
                for mi in range(gn):
                    m = m0 + mi
                    arr[g] = dma("sync" if mi == 0 else "scalar",
                                 wraw.ap()[:, half + mi * wc: half + (mi + 1) * wc],
                                 src_h[m * 128:(m + 1) * 128, col0:col0 + wc], sem)

        def sl(g):
            m0, gn = groups[g]
            gw = gn * wc
            nbg = gw // BLK
            half_s = (g % 3) * 2 * (gw_max // BLK)
            svp = wsc.ap()[:, half_s:half_s + nbg]
            isv = wsc.ap()[:, half_s + (gw_max // BLK):half_s + (gw_max // BLK) + nbg]
            buf = wraw.ap()[:, (g % 3) * gw_max:][:, 0:gw]
            return gw, nbg, svp, isv, buf

        load(0)
        if ngr > 1:
            load(1)
        if ngr > 2:
            load(2)
        gw0, nbg0, svp0, isv0, buf0 = sl(0)
        P.wait("vector", wsem[0], arr[0])
        v_op(lambda e, buf=buf0, svp=svp0: e.tensor_reduce(
            svp, buf.rearrange("p (b c) -> p b c", c=BLK), AX.X, ALU.max,
            apply_absolute_value=True))
        sep("vector")
        v_op(lambda e, svp=svp0: e.tensor_scalar(svp, svp, 1.0 / 7.0, GUARD, ALU.mult, ALU.max))
        sep("vector")
        v_op(lambda e, svp=svp0, isv=isv0: e.reciprocal(isv, svp))
        if ngr == 1:
            sep("vector")
        for g in range(ngr):
            m0, gn = groups[g]
            gw, nbg, svp, isv, buf = sl(g)
            if g + 1 < ngr:
                gw1, nbg1, svp1, isv1, buf1 = sl(g + 1)
                P.wait("vector", wsem[(g + 1) % 3], arr[g + 1])
                if g >= 2:
                    P.wait("vector", "G", g_marks[g - 2])
                v_op(lambda e, buf=buf1, svp=svp1: e.tensor_reduce(
                    svp, buf.rearrange("p (b c) -> p b c", c=BLK), AX.X, ALU.max,
                    apply_absolute_value=True))
            else:
                sep("vector")
            v_op(lambda e, buf=buf, isv=isv, nbg=nbg, gw=gw: e.tensor_tensor(
                buf.rearrange("p (b c) -> p b c", c=BLK),
                buf.rearrange("p (b c) -> p b c", c=BLK),
                isv.rearrange("p (b o) -> p b o", o=1).broadcast_to([128, nbg, BLK]),
                ALU.mult))
            vtt1_marks[g] = v_mark()
            if g + 1 < ngr:
                v_op(lambda e, svp=svp1: e.tensor_scalar(svp, svp, 1.0 / 7.0, GUARD,
                                                         ALU.mult, ALU.max))
            w8h = (g % 2) * gw_max
            if g >= 2:
                P.wait("vector", "G", g_marks[g - 2])
            v_op(lambda e, gw=gw, w8h=w8h, buf=buf: e.tensor_scalar(wi8.ap()[:, w8h:w8h + gw], buf,
                                                                    1.0, None, ALU.mult))
            vr8 = v_mark()
            if g + 3 < ngr:
                P.wait("sync", "V", vr8)
                P.wait("scalar", "V", vr8)
                load(g + 3)
            if g + 1 < ngr:
                v_op(lambda e, svp=svp1, isv=isv1: e.reciprocal(isv, svp))
            P.wait("gpsimd", "V", vr8)
            wofs = (g % 2) * gw_max
            if g >= 2:
                P.wait("gpsimd", "T", t_marks[g - 2])
            if dstT_dram is not None:
                P.wait("gpsimd", "S", s_hist[g - 2] if g >= 2 else s_prev)
                P.wait("gpsimd", ("SPWD", "SPW1")[g % 2],
                       spwd_hist[g - 2] if g >= 2 else spwd_prev[g % 2])
            g_op(lambda e, svp=svp, nbg=nbg, gw=gw, w8h=w8h, wofs=wofs: e.tensor_tensor(
                wdq.ap()[:, wofs:wofs + gw].rearrange("p (b c) -> p b c", c=BLK),
                wi8.ap()[:, w8h:w8h + gw].rearrange("p (b c) -> p b c", c=BLK),
                svp.rearrange("p (b o) -> p b o", o=1).broadcast_to([128, nbg, BLK]),
                ALU.mult))
            g_marks[g] = g_mark()
            P.wait("tensor", "G", g_marks[g])
            for mi in range(gn):
                m = m0 + mi
                nbat = (njt + 7) // 8
                for jb in range(nbat):
                    j0, jn = jb * 8, min(8, njt - jb * 8)
                    rr = ((m0 + mi) * nbat + jb) % 4
                    pg, off = regions3[rr]
                    if (m0 + mi) * nbat + jb >= 4:
                        P.wait("tensor", *tpose_evac[rr])
                    for j in range(j0, j0 + jn):
                        mm(pg[:, off + (j - j0) * 128: off + (j - j0 + 1) * 128],
                           wdq.ap()[:, wofs + mi * wc + j * 128: wofs + mi * wc + (j + 1) * 128],
                           ident_b[:, :], True, True)
                    tt = t_mark()
                    P.wait("scalar", "T", tt)
                    if dstT_sb is not None:
                        s_op(lambda e, pg=pg, off=off, m=m, j0=j0, jn=jn:
                             e.copy(dstT_sb[m].ap()[:, j0 * 128:(j0 + jn) * 128],
                                    pg[:, off: off + jn * 128]))
                        svv = s_mark()
                    else:
                        s_op(lambda e, pg=pg, off=off, j0=j0, jn=jn, mi=mi, wc=wc, wofs=wofs:
                             e.copy(wdq.ap()[:, wofs + mi * wc + j0 * 128: wofs + mi * wc + (j0 + jn) * 128],
                                    pg[:, off: off + jn * 128]))
                        svv = s_mark()
                        P.wait("sync", "S", svv)
                        spsem = ("SPWD", "SPW1")[g % 2]
                        if mrow:
                            dma("sync",
                                dstT_dram.ap()[m * 128:(m + 1) * 128,
                                               (j0ofs + j0) * 128:(j0ofs + j0 + jn) * 128],
                                wdq.ap()[:, wofs + mi * wc + j0 * 128: wofs + mi * wc + (j0 + jn) * 128], spsem)
                        else:
                            dma("sync",
                                dstT_dram.ap()[(j0ofs + j0) * 128:(j0ofs + j0 + jn) * 128,
                                               m * 128:(m + 1) * 128].rearrange("(j p) c -> p j c", p=128),
                                wdq.ap()[:, wofs + mi * wc + j0 * 128: wofs + mi * wc + (j0 + jn) * 128]
                                .rearrange("p (j c) -> p j c", c=128), spsem)
                    tpose_evac[rr] = ("S", svv)
            t_marks[g] = P.now("T")
            s_hist[g] = P.now("S")
            spwd_hist[g] = P.now(("SPWD", "SPW1")[g % 2])
        for r in tpose_evac.values():
            P.wait("tensor", *r)

    weight_prep(wu_h, ni, d, 2, None, wuT_dram, wraw_s, wi8_s, wdq_s, wsc_s, wqf_s,
                ("WR0", "WR1", "WR2"), mrow=True, entry=wu_entry, entry_t=wu_entry_t)
    weight_prep(wg_h, ni, d, 2, None, wgT_dram, wraw_s, wi8_s, wdq_s, wsc_s, wqf_s, ("WR0", "WR1", "WR2"), mrow=True)
    wd_entry = (P.now("S"), P.now("G"), P.now("V"))

    P.wait("gpsimd", "CC", cc_kva)
    kv2a = dma("gpsimd", kvtks.ap()[:, 0:hw_kv], kv_cc_outA[:, :], "KV2")
    P.wait("gpsimd", "CC", cc_kv)
    kv2b = dma("gpsimd", kvtks.ap()[:, hw_kv:2 * hw_kv], kv_cc_outB[:, :], "KV3")

    P.wait("tensor", "KV2", kv2a)
    for dk in range(nk):
        if dk == nh:
            P.wait("tensor", "KV3", kv2b)
        ksc = kvtks.ap()[:, kscol(dk): kscol(dk) + 1]
        for s in range(ns):
            mm(PA[0:1, s * 512:(s + 1) * 512], ksc,
               qe[dk].ap()[:, s * 512:(s + 1) * 512],
               start=(dk == 0), stop=(dk == nk - 1))
    tden = t_mark()
    P.wait("vector", "T", tden)
    v_op(lambda e: e.tensor_scalar(R0, PA[0:1, 0:tok], ATT_EPS, None, ALU.add))
    vz0 = v_mark(drain=True)
    P.wait("scalar", "V", vz0)
    s_op(lambda e: e.activation(R0, R0, AF.Ln))
    s_op(lambda e: e.activation(R0, R0, AF.Exp, scale=sc(S_MONE)))
    vz = s_mark(drain=True)
    P.wait("tensor", "S", vz)
    for s in range(ns):
        mm(PB[:, s * 512:(s + 1) * 512], onerow_f[0:1, :], R0[:, s * 512:(s + 1) * 512], True, True)
    tzb = t_mark()
    P.wait("vector", "T", tzb)
    v_op(lambda e: e.tensor_copy(zbc.ap()[:, :], PB[:, 0:tok]))
    v_zbc = v_mark()

    P.wait("tensor", "V", v_zbc)
    xres_evac = {}
    xts = [xres_t, xres_t2]
    xsem = ("LX2", "LXR")
    lx2a = {0: dma("sync", xres_t.ap()[:, :], xT[0:128, :], "LX2"),
            1: dma("sync", xres_t2.ap()[:, :], xT[128:256, :], "LXR")}
    for h in range(nk):
        pg = pages[h % 2]
        if h >= 2:
            P.wait("tensor", *xres_evac[h % 2])
        for dk in range(nk):
            lhs = kvtks.ap()[:, kvoff(dk) + h * 128: kvoff(dk) + (h + 1) * 128]
            for s in range(ns):
                mm(pg[:, s * 512:(s + 1) * 512], lhs,
                   qe[dk].ap()[:, s * 512:(s + 1) * 512],
                   start=(dk == 0), stop=(dk == nk - 1))
        tv = t_mark()
        P.wait("vector", "T", tv)
        P.wait("vector", xsem[h % 2], lx2a[h])
        v_op(lambda e, pg=pg, h=h: e.tensor_tensor(xresf[h].ap()[:, :], pg[:, 0:tok], zbc.ap()[:, :], ALU.mult))
        v_op(lambda e, h=h: e.tensor_tensor(xresf[h].ap()[:, :], xresf[h].ap()[:, :],
                                            xts[h % 2].ap()[:, :], ALU.add))
        vv = v_mark()
        xres_evac[h % 2] = ("V", vv)
        xres_vv = xres_evac[h % 2]
        P.wait("sync", "V", vv)
        if h + 2 < nk:
            lx2a[h + 2] = dma("sync", xts[h % 2].ap()[:, :], xT[(h + 2) * 128:(h + 3) * 128, :],
                              xsem[h % 2])
        dma("sync", xres_dram[h * 128:(h + 1) * 128, :], xresf[h].ap()[:, :], "SXR")
    sxr_all = P.now("SXR")

    def arrive6(k, eng):
        pass

    v_xres_all = P.now("V")
    P.wait("vector", "V", v_xres_all)
    P.wait("scalar", "V", v_xres_all)
    tb2 = ln_block(lambda k: xresf[k].ap()[:, :], arrive6, lambda k, tk: None,
                   xbf_t2, row1b.ap()[0:1, :])
    P.wait("vector", "T", tb2)
    app6 = {}
    for k in range(nk):
        xk = xresf[k].ap()[:, :]
        ok = ln2x[k].ap()[:, :]
        v_op(lambda e, xk=xk: e.tensor_tensor(ttmp6[:, 0:tok], xk, PA[:, 0:tok], ALU.mult))
        v_op(lambda e, ok=ok: e.tensor_tensor(ok, ttmp6[:, 0:tok], PB[:, 0:tok], ALU.add))
        v_op(lambda e, k=k: e.tensor_reduce(mxcols.ap()[:, k:k + 1], ln2x[k].ap()[:, :],
                                            AX.X, ALU.max, apply_absolute_value=True))
        app6[k] = v_mark()
    v_ln2 = app6[nk - 1]

    def fold_cols(c0, c1, slot, wait_val):
        P.wait("vector", "V", wait_val)
        sep("vector")
        v_op(lambda e, c0=c0, c1=c1: e.tensor_reduce(bcol(B_RUN), mxcols.ap()[:, c0:c1],
                                                     AX.X, ALU.max))
        vr = v_mark(drain=True)
        P.wait("tensor", "V", vr)
        mm(PA[0:1, 0:128], bcol(B_RUN), ident_f[:, :], True, True)
        tf = t_mark()
        P.wait("vector", "T", tf)
        v_op(lambda e, slot=slot: e.tensor_reduce(sc(slot), PA[0:1, 0:128], AX.X, ALU.max))
        return v_mark(drain=True)

    def barrier_start(idx, wait_v):
        P.wait("sync", "V", wait_v)
        lbr = dma("sync", sc_cc_in[idx][:, :], scal.ap()[0:1, 0:16], "LBR")
        P.wait("gpsimd", "LBR", lbr)
        P.emit("gpsimd", lambda e, idx=idx: e.collective_compute(
            "AllReduce", ALU.max, replica_groups=[list(range(N_CORES))],
            ins=[sc_cc_in[idx].ap()], outs=[sc_cc_out[idx].ap()]).then_inc(SEMS["CC"], 1))
        return P.inc("CC", 1)

    def barrier_end(idx, cb):
        P.wait("sync", "CC", cb)
        return dma("sync", scal.ap()[0:1, 0:16], sc_cc_out[idx][:, :], "GBR")

    def barrier(idx, wait_v):
        return barrier_end(idx, barrier_start(idx, wait_v))

    def bcast_scalar(src_slot, dst_col, wait_sem, wait_val):
        P.wait("tensor", wait_sem, wait_val)
        mm(PA[:, 0:1], onerow_f[0:1, :], sc(src_slot), True, True)
        tf = t_mark()
        P.wait("vector", "T", tf)
        v_op(lambda e, dst_col=dst_col: e.tensor_copy(bcol(dst_col), PA[:, 0:1]))
        return v_mark(drain=True)

    weight_prep(wd_h, nk, di // 2, 1, None, wdT_dram, wraw_s, wi8_s, wdq_s, wsc_s, wqf_s,
                ("WR0", "WR1", "WR2"), entry=wd_entry, col0=0, j0ofs=0)
    weight_prep(wd_h, nk, di // 2, 1, None, wdT_dram, wraw_s, wi8_s, wdq_s, wsc_s, wqf_s,
                ("WR0", "WR1", "WR2"), col0=di // 2, j0ofs=di // 256)
    spwd_all = (P.now("SPWD"), P.now("SPW1"))

    vf0 = fold_cols(0, nk, 0, v_ln2)
    gb0 = barrier(0, vf0)
    P.wait("vector", "GBR", gb0)
    v_op(lambda e: e.tensor_scalar(sc(S_S1), sc(0), 1.0 / QMAX, GUARD, ALU.mult, ALU.max))
    sep("vector")
    v_op(lambda e: e.reciprocal(sc(S_C1), sc(S_S1)))
    v_s1 = v_mark(drain=True)
    v_c1b = bcast_scalar(S_C1, B_C1, "V", v_s1)

    P.wait("vector", "V", v_c1b)
    xq16s = [xq16.ap()[:, :], xq16b.ap()[:, :]]
    sxq_m = {}
    for k in range(nk):
        src = ln2x[k].ap()[:, :]
        if k >= 2:
            P.wait("vector", "S", sxq_m[k - 2])
        v_op(lambda e, src=src, k=k: e.tensor_scalar(xq16s[k % 2], src, bcol(B_C1), None, ALU.mult))
        vvq = v_mark()
        P.wait("scalar", "V", vvq)
        s_op(lambda e, k=k: e.activation(xq[k].ap()[:, :], xq16s[k % 2], AF.Copy))
        sxq_m[k] = s_mark()
    v_xq = P.now("V")
    s_xq = P.now("S")

    P.wait("tensor", "V", v_xq)
    P.wait("tensor", "S", s_xq)
    a_evac = {}
    thw = min(1024, tok)
    nth = (tok + 1023) // 1024
    LWS = ("LW0", "LW1", "LW2", "LW3")
    arrW = {}
    ta_m = {}
    P.wait("sync", "SPWD", spwd_all[0])
    P.wait("sync", "SPW1", spwd_all[1])
    P.wait("sync", "V", v_xq)
    P.wait("sync", "S", s_xq)

    def loadw(m):
        dma("sync", wst_u[m % 4][:, :], wuT_dram[m * 128:(m + 1) * 128, :], LWS[m % 4])
        arrW[m] = dma("sync", wst_g[m % 4][:, :], wgT_dram[m * 128:(m + 1) * 128, :], LWS[m % 4])

    loadw(0)
    loadw(1)
    for m in range(ni):
        if m + 2 < ni:
            if m >= 2:
                P.wait("sync", "T", ta_m[m - 2])
            loadw(m + 2)
        P.wait("tensor", LWS[m % 4], arrW[m])
        for th in range(nth):
            pi = (nth * m + th) % 2
            pg = pages[pi]
            if nth * m + th >= 2:
                P.wait("tensor", *a_evac[pi])
            for k in range(nk):
                lu = wst_u[m % 4][:, k * 128:(k + 1) * 128]
                lg = wst_g[m % 4][:, k * 128:(k + 1) * 128]
                for s in range(max(1, thw // 512)):
                    mv = xq[k].ap()[:, th * 1024 + s * 512: th * 1024 + s * 512 + 512]
                    mm(pg[:, s * 512:(s + 1) * 512], lu, mv, start=(k == 0), stop=(k == nk - 1))
                    mm(pg[:, 1024 + s * 512: 1024 + (s + 1) * 512], lg, mv, start=(k == 0), stop=(k == nk - 1))
            ta = t_mark()
            ta_m[m] = ta
            P.wait("scalar", "T", ta)
            if m >= 2 and th == 0:
                P.wait("scalar", ("SA0", "SA1")[m % 2], P.now(("SA0", "SA1")[m % 2]))
            us = o_bf[m]
            gs = spill_t[2 + m % 2].ap()
            tb = th * 1024
            s_op(lambda e, pg=pg, us=us, tb=tb: e.copy(us[:, tb:tb + thw], pg[:, 0:thw]))
            s_op(lambda e, pg=pg, gs=gs, tb=tb: e.copy(gs[:, tb:tb + thw], pg[:, 1024:1024 + thw]))
            a_evac[pi] = ("S", s_mark())
        P.wait("vector", "S", a_evac[(nth * m + nth - 1) % 2][1])
        v_op(lambda e, m=m: e.tensor_reduce(mxcols.ap()[:, m:m + 1], o_bf[m][:, :],
                                            AX.X, ALU.max, apply_absolute_value=True))
        v_op(lambda e, m=m: e.tensor_reduce(mxcols.ap()[:, 32 + m:33 + m], spill_t[2 + m % 2].ap()[:, :],
                                            AX.X, ALU.max, apply_absolute_value=True))
        va = v_mark()
        P.wait("sync", "V", va)
        dma("sync", gate_dram[m * 128:(m + 1) * 128, :], spill_t[2 + m % 2].ap()[:, :], ("SA0", "SA1")[m % 2])
    v_passA = P.now("V")
    sa_all = (P.now("SA0"), P.now("SA1"))

    vfu = fold_cols(0, ni, 0, v_passA)
    vfg = fold_cols(32, 32 + ni, 1, vfu)
    cb1 = barrier_start(1, vfg)
    P.wait("sync", "SA0", sa_all[0])
    P.wait("sync", "SA1", sa_all[1])
    arrB = {}
    arrB[0] = dma("sync", spill_t[2].ap()[:, :], gate_dram[0:128, :], "LB0")
    gb1 = barrier_end(1, cb1)

    P.wait("vector", "GBR", gb1)
    v_op(lambda e: e.tensor_tensor(sc(S_T0), sc(0), sc(S_S1), ALU.mult))
    sep("vector")
    v_op(lambda e: e.tensor_scalar(sc(S_S2), sc(S_T0), 1.0 / QMAX, GUARD, ALU.mult, ALU.max))
    sep("vector")
    v_op(lambda e: e.reciprocal(sc(S_T0), sc(S_S2)))
    sep("vector")
    v_op(lambda e: e.tensor_tensor(sc(S_F2), sc(S_T0), sc(S_S1), ALU.mult))
    v_op(lambda e: e.tensor_tensor(sc(S_T1), sc(1), sc(S_S1), ALU.mult))
    sep("vector")
    v_op(lambda e: e.tensor_scalar(sc(S_S3), sc(S_T1), 1.0 / QMAX, GUARD, ALU.mult, ALU.max))
    sep("vector")
    v_op(lambda e: e.reciprocal(sc(S_T1), sc(S_S3)))
    sep("vector")
    v_op(lambda e: e.tensor_tensor(sc(S_F3), sc(S_T1), sc(S_S1), ALU.mult))
    v_sc1 = v_mark(drain=True)
    v_b1 = bcast_scalar(S_F2, B_F2, "V", v_sc1)
    v_b2 = bcast_scalar(S_F3, B_F3, "V", v_b1)
    v_b3 = bcast_scalar(S_S3, B_S3, "V", v_b2)

    P.wait("vector", "V", v_b3)
    b_done = {}
    gcb = [gi16.ap()[:, :], gi16b.ap()[:, :], gi16c.ap()[:, :]]
    sgf = [sigf[:, :], sigfb[:, :]]
    sg32 = [sig32.ap()[:, :], sig32b.ap()[:, :]]
    vg_m = {}
    vs_m = {}
    sg_m = {}
    gp_m = {}
    for m in range(ni + 2):
        p = m % 2
        if m < ni:
            if m + 1 < ni:
                if m >= 1:
                    P.wait("sync", "V", vg_m[m - 1])
                arrB[m + 1] = dma("sync", spill_t[2 + (m + 1) % 2].ap()[:, :],
                                  gate_dram[(m + 1) * 128:(m + 2) * 128, :], ("LB0", "LB1")[(m + 1) % 2])
            P.wait("vector", ("LB0", "LB1")[p], arrB[m])
            if m >= 3:
                P.wait("vector", "G", gp_m[m - 3])
            v_op(lambda e, m=m: e.tensor_scalar(upq[m].ap()[:, :], o_bf[m][:, :],
                                                bcol(B_F2), None, ALU.mult))
            v_op(lambda e, m=m, p=p: e.tensor_scalar(gcb[m % 3], spill_t[2 + p].ap()[:, :],
                                                     bcol(B_F3), None, ALU.mult))
            vg_m[m] = v_mark()
            P.wait("scalar", "V", vg_m[m])
            if m >= 2:
                P.wait("scalar", "V", vs_m[m - 2])
            s_op(lambda e, m=m, p=p: e.activation(sgf[p], gcb[m % 3], AF.Sigmoid, scale=bcol(B_S3)))
            sg_m[m] = s_mark()
        mm1 = m - 1
        if 0 <= mm1 < ni:
            q = mm1 % 2
            P.wait("vector", "S", sg_m[mm1])
            v_op(lambda e, q=q: e.tensor_scalar(sg32[q], sgf[q], 65536.0, None, ALU.mult))
            vs_m[mm1] = v_mark()
            P.wait("gpsimd", "V", vs_m[mm1])
            if mm1 >= 2:
                P.wait("gpsimd", ("SB0", "SB1")[q], P.now(("SB0", "SB1")[q]))
            g_op(lambda e, q=q, mm1=mm1: e.tensor_tensor(prodt[q].ap()[:, :], gcb[mm1 % 3],
                                                          sg32[q], ALU.mult))
            gp_m[mm1] = g_mark()
        mm2 = m - 2
        if mm2 >= 0:
            q2 = mm2 % 2
            P.wait("vector", "G", gp_m[mm2])
            v_op(lambda e, q2=q2, mm2=mm2: e.tensor_reduce(mxcols.ap()[:, mm2:mm2 + 1],
                                                           prodt[q2].ap()[:, :], AX.X,
                                                           ALU.max, apply_absolute_value=True))
            b_done[q2] = v_mark()
            P.wait("sync", "V", b_done[q2])
            dma("sync", prod_dram[mm2 * 128:(mm2 + 1) * 128, :], prodt[q2].ap()[:, :],
                ("SB0", "SB1")[q2])
    v_passB = P.now("V")
    sb_all = (P.now("SB0"), P.now("SB1"))

    vf2 = fold_cols(0, ni, 0, v_passB)
    cb2 = barrier_start(2, vf2)
    P.wait("sync", "SB0", sb_all[0])
    P.wait("sync", "SB1", sb_all[1])
    arrC = {0: dma("sync", spill_t[0].ap()[:, :], prod_dram[0:128, :], "LC0")}
    P.wait("sync", "SPWD", spwd_all[0])
    P.wait("sync", "SPW1", spwd_all[1])
    for i in range(ni):
        dma("sync", wdT[i][:, :], wdT_dram[i * 128:(i + 1) * 128, :], "LWDT")
    lwdt_all = P.now("LWDT")
    gb2 = barrier_end(2, cb2)
    P.wait("vector", "GBR", gb2)
    v_op(lambda e: e.tensor_scalar(sc(S_T0), sc(S_S3), 1.0 / 65536.0, None, ALU.mult))
    sep("vector")
    v_op(lambda e: e.tensor_tensor(sc(S_T1), sc(0), sc(S_T0), ALU.mult))
    sep("vector")
    v_op(lambda e: e.tensor_scalar(sc(S_S4), sc(S_T1), 1.0 / QMAX, GUARD, ALU.mult, ALU.max))
    sep("vector")
    v_op(lambda e: e.reciprocal(sc(S_T1), sc(S_S4)))
    sep("vector")
    v_op(lambda e: e.tensor_tensor(sc(S_C4), sc(S_T1), sc(S_T0), ALU.mult))
    v_sc2 = v_mark(drain=True)
    v_b4 = bcast_scalar(S_C4, B_C4, "V", v_sc2)

    P.wait("vector", "V", v_b4)
    P.wait("vector", "G", P.now("G"))
    htk = tok // 2
    c_done = {}
    gc = [gi16, gi16b]
    vc_m = {}
    gc_m = {}
    for m in range(ni + 2):
        p = m % 2
        if m < ni:
            if m + 1 < ni:
                if m >= 1:
                    P.wait("sync", "V", vc_m[m - 1])
                arrC[m + 1] = dma("sync", spill_t[(m + 1) % 2].ap()[:, :],
                                  prod_dram[(m + 1) * 128:(m + 2) * 128, :], ("LC0", "LC1")[(m + 1) % 2])
            P.wait("vector", ("LC0", "LC1")[p], arrC[m])
            if m >= 2:
                P.wait("vector", ("SC0", "SC1")[p], P.now(("SC0", "SC1")[p]))
                P.wait("vector", "G", gc_m[m - 2])
            v_op(lambda e, m=m, p=p: e.tensor_scalar(gc[p].ap()[:, :], spill_t[p].ap()[:, :],
                                                     bcol(B_C4), None, ALU.mult))
            vc_m[m] = v_mark()
        mm2 = m - 2
        if mm2 >= 0:
            q2 = mm2 % 2
            P.wait("vector", "G", gc_m[mm2])
            v_op(lambda e, q2=q2, mm2=mm2: e.tensor_reduce(mxcols.ap()[:, mm2:mm2 + 1],
                                                           of32.ap()[:, q2 * tok:q2 * tok + tok],
                                                           AX.X, ALU.max, apply_absolute_value=True))
            c_done[q2] = v_mark()
        if m < ni:
            P.wait("sync", "V", vc_m[m])
            dma("sync", g2_dram[m * 128:(m + 1) * 128, :], gc[p].ap()[:, :], ("SC0", "SC1")[p])
            P.wait("gpsimd", "V", vc_m[m])
            if m >= 2:
                P.wait("gpsimd", "V", c_done[p])
            g_op(lambda e, m=m, p=p: e.tensor_tensor(of32.ap()[:, p * tok:p * tok + htk],
                                                     gc[p].ap()[:, 0:htk],
                                                     upq[m].ap()[:, 0:htk], ALU.mult))
            gc_m[m] = g_mark()
            v_op(lambda e, m=m, p=p: e.tensor_tensor(of32.ap()[:, p * tok + htk:p * tok + tok],
                                                     gc[p].ap()[:, htk:tok],
                                                     upq[m].ap()[:, htk:tok], ALU.mult))
    v_passC = P.now("V")
    scc_all = (P.now("SC0"), P.now("SC1"))

    vf3 = fold_cols(0, ni, 0, v_passC)
    cb3 = barrier_start(3, vf3)
    P.wait("sync", "SC0", scc_all[0])
    P.wait("sync", "SC1", scc_all[1])
    P.wait("sync", "G", P.now("G"))
    arrD = {0: dma("sync", gi16.ap()[:, :], g2_dram[0:128, :], "LP0")}
    gb3 = barrier_end(3, cb3)
    P.wait("vector", "GBR", gb3)
    v_op(lambda e: e.tensor_tensor(sc(S_T0), sc(S_S4), sc(S_S2), ALU.mult))
    sep("vector")
    v_op(lambda e: e.tensor_tensor(sc(S_T1), sc(0), sc(S_T0), ALU.mult))
    sep("vector")
    v_op(lambda e: e.tensor_scalar(sc(S_S5), sc(S_T1), 1.0 / QMAX, GUARD, ALU.mult, ALU.max))
    sep("vector")
    v_op(lambda e: e.reciprocal(sc(S_T1), sc(S_S5)))
    sep("vector")
    v_op(lambda e: e.tensor_tensor(sc(S_C5), sc(S_T1), sc(S_T0), ALU.mult))
    v_sc3 = v_mark(drain=True)
    v_b5 = bcast_scalar(S_C5, B_C5, "V", v_sc3)
    v_b6 = bcast_scalar(S_S5, B_S5, "V", v_b5)

    P.wait("vector", "V", v_b6)
    P.wait("gpsimd", "V", v_b6)
    d_done = {}
    sD_m = {}
    gslot = [gi16.ap()[:, :], gi16b.ap()[:, :]]
    gd_m = {}
    for m in range(ni + 1):
        p = m % 2
        if m < ni:
            if m + 1 < ni:
                if m >= 1:
                    P.wait("sync", "G", gd_m[m - 1])
                    if m >= 2:
                        P.wait("sync", "V", d_done[m % 2])
                arrD[m + 1] = dma("sync", gslot[(m + 1) % 2], g2_dram[(m + 1) * 128:(m + 2) * 128, :],
                                  ("LP0", "LP1")[(m + 1) % 2])
            P.wait("gpsimd", ("LP0", "LP1")[p], arrD[m])
            if m >= 2:
                P.wait("gpsimd", "V", d_done[p])
            g_op(lambda e, m=m, p=p: e.tensor_tensor(of32.ap()[:, p * tok:p * tok + htk],
                                                     gslot[p][:, 0:htk],
                                                     upq[m].ap()[:, 0:htk], ALU.mult))
            gd_m[m] = g_mark()
            P.wait("vector", ("LP0", "LP1")[p], arrD[m])
            v_op(lambda e, m=m, p=p: e.tensor_tensor(of32.ap()[:, p * tok + htk:p * tok + tok],
                                                     gslot[p][:, htk:tok],
                                                     upq[m].ap()[:, htk:tok], ALU.mult))
        mm1 = m - 1
        if mm1 >= 0:
            q = mm1 % 2
            P.wait("vector", "G", gd_m[mm1])
            if mm1 >= 2:
                P.wait("vector", "S", sD_m[mm1 - 2])
            v_op(lambda e, q=q: e.tensor_scalar(sg32[q], of32.ap()[:, q * tok:q * tok + tok],
                                                bcol(B_C5), None, ALU.mult))
            d_done[q] = v_mark()
            P.wait("scalar", "V", d_done[q])
            s_op(lambda e, q=q, mm1=mm1: e.activation(o_bf[mm1][:, :], sg32[q], AF.Copy))
            sD_m[mm1] = s_mark()
    v_passD = P.now("V")
    s_passD = P.now("S")

    P.wait("tensor", "V", v_passD)
    P.wait("tensor", "S", s_passD)
    P.wait("tensor", "LWDT", lwdt_all)
    dn_evac = {}
    for h in range(nk):
        pg = pages[h % 2]
        if h >= 2:
            P.wait("tensor", *dn_evac[h % 2])
        for i in range(ni):
            lhs = wdT[i][:, h * 128:(h + 1) * 128]
            for s in range(ns):
                mm(pg[:, s * 512:(s + 1) * 512], lhs,
                   o_bf[i][:, s * 512:(s + 1) * 512],
                   start=(i == 0), stop=(i == ni - 1))
        td = t_mark()
        if h == 0:
            P.wait("sync", "V", v_passD)
        lxr = dma("sync", of32.ap()[:, tok:2 * tok], xres_dram[h * 128:(h + 1) * 128, :], "LXR")
        P.wait("scalar", "T", td)
        if h >= 1:
            P.wait("scalar", "SOUT", P.now("SOUT"))
        s_op(lambda e, pg=pg: e.activation(of32.ap()[:, 0:tok], pg[:, 0:tok], AF.Copy, scale=bcol(B_S5)))
        sd = s_mark()
        P.wait("vector", "S", sd)
        P.wait("vector", "LXR", lxr)
        v_op(lambda e: e.tensor_tensor(of32.ap()[:, 0:tok], of32.ap()[:, 0:tok],
                                       of32.ap()[:, tok:2 * tok], ALU.add))
        vd = v_mark()
        dn_evac[h % 2] = ("V", vd)
        P.wait("sync", "V", vd)
        dma("sync", outT[h * 128:(h + 1) * 128, :], of32.ap()[:, 0:tok], "SOUT")
    if debug:
        d_obf = nc.dram_tensor("d_obf", [di, tok], BF16, kind="ExternalOutput")
        for i in range(ni):
            dma("sync", d_obf.ap()[i * 128:(i + 1) * 128, :], o_bf[i][:, :], "SOUT")
        dbg = {"d_xres": xres_dram, "d_up": up_dram, "d_gate": gate_dram,
               "d_prod": prod_dram, "d_g2": g2_dram, "d_wdT": wdT_dram,
               "d_kv": kv_cc_outB}
        for nm, t in dbg.items():
            o = nc.dram_tensor(nm, list(t.shape), t.dtype, kind="ExternalOutput")
            dma("sync", o.ap(), t.ap(), "SOUT")
        o1 = nc.dram_tensor("d_scal", [1, 64], F32, kind="ExternalOutput")
        dma("sync", o1.ap(), scal.ap()[0:1, :], "SOUT")
        o2 = nc.dram_tensor("d_bc", [128, 16], F32, kind="ExternalOutput")
        dma("sync", o2.ap(), bc.ap()[:, :], "SOUT")
    P.wait("sync", "SOUT", P.now("SOUT"))
    return P


def build(nc, d=D, tok=TOK, di=DI, debug=False):
    global SEMS
    SEMS = {}
    import contextlib
    with contextlib.ExitStack() as es:
        for s in SEM_NAMES:
            SEMS[s] = es.enter_context(nc.semaphore(s))
        P = build_program(nc, d, tok, di, debug)
        with nc.Block() as block:
            for eng_name in Plan.ENGINES:
                deco = getattr(block, eng_name)

                def make(fns):
                    def body(e):
                        for f in fns:
                            f(e)
                    return body
                deco(make(P.q[eng_name]))
    return nc


def kernel(L0_attn_x, L0_attn_W, L0_ffn_w_gate, L0_ffn_w_up, L0_ffn_w_down):
    x = np.ascontiguousarray(L0_attn_x, dtype=np.float32)
    W = np.ascontiguousarray(L0_attn_W, dtype=np.float32)
    wg = np.ascontiguousarray(L0_ffn_w_gate, dtype=np.float32)
    wu = np.ascontiguousarray(L0_ffn_w_up, dtype=np.float32)
    wd = np.ascontiguousarray(L0_ffn_w_down, dtype=np.float32)
    ident = np.eye(128, dtype=np.float32)

    nc = bass.Bass("TRN2", target_bir_lowering=False, detect_race_conditions=False)
    build(nc)

    in_maps = []
    for c in range(N_CORES):
        b, h = c >> 1, c & 1
        xTc = np.ascontiguousarray(x[b, h * TOK:(h + 1) * TOK, :].T)
        in_maps.append({"xT": xTc, "W": W, "wg": wg, "wu": wu, "wd": wd, "ident": ident})
    res = run_bass_kernel_spmd(nc, in_maps, core_ids=list(range(N_CORES)))
    out = np.empty((4, 4096, D), dtype=np.float32)
    for c in range(N_CORES):
        b, h = c >> 1, c & 1
        out[b, h * TOK:(h + 1) * TOK, :] = res.results[c]["outT"].T
    return out


# version 33
# speedup vs baseline: 1.0412x; 1.0056x over previous
import numpy as np
import concourse.bass as bass
import concourse.mybir as mybir
from concourse.bass_utils import run_bass_kernel_spmd

F32 = mybir.dt.float32
BF16 = mybir.dt.bfloat16
I32 = mybir.dt.int32
I16 = mybir.dt.int16
I8 = mybir.dt.int8
AF = mybir.ActivationFunctionType
ALU = mybir.AluOpType
AX = mybir.AxisListType

D = 1024
TOK = 2048
DI = 3072
BLK = 32
N_CORES = 8

LN_EPS = 1e-5
ATT_EPS = 1e-6
QMAX = 32767.0
GUARD = 1e-12

S_S1, S_C1, S_S2, S_F2, S_S3, S_F3, S_S4, S_C4, S_S5, S_C5, S_T0, S_T1 = range(16, 28)
S_MHALF = 30
S_MONE = 31
B_C1, B_F2, B_F3, B_S3, B_C4, B_C5, B_S5 = range(7)
B_UP, B_GATE, B_TMP, B_RUN = 10, 11, 12, 15

SEM_NAMES = ["T", "V", "S", "CC", "G",
             "LX", "GW", "KV1", "KV2", "KV3", "LX2", "SXR", "LR0", "LR1", "WR0", "WR1",
             "SPWD", "SPW1", "LWDT", "SA0", "SA1", "LB0", "LB1", "SB0", "SB1", "SC0", "SC1", "WR2",
             "LC0", "LC1", "LP0", "LP1", "LXR", "SOUT", "LBR", "GBR", "LW0", "LW1", "LW2", "LW3"]


class Plan:
    ENGINES = ("sync", "gpsimd", "tensor", "vector", "scalar")

    def __init__(self):
        self.q = {e: [] for e in self.ENGINES}
        self.cnt = {s: 0 for s in SEM_NAMES}
        self._last_wait = {e: False for e in self.ENGINES}

    def inc(self, name, k):
        self.cnt[name] += k
        return self.cnt[name]

    def now(self, name):
        return self.cnt[name]

    def emit(self, eng, fn):
        self.q[eng].append(fn)
        self._last_wait[eng] = False

    def wait(self, eng, sem_name, val=None):
        v = self.cnt[sem_name] if val is None else val
        if v > 0:
            if self._last_wait[eng]:
                self.q[eng].append(lambda e: e.nop())
            self.q[eng].append(lambda e, s=sem_name, vv=v: e.wait_ge(SEMS[s], vv))
            self._last_wait[eng] = True


SEMS = {}


class SbufMap:
    BASE = 16640
    TOP = 229360
    LIMIT = TOP - BASE

    def __init__(self, nc):
        self.nc = nc
        self.items = []

    def alloc(self, name, shape, dtype, ph0, ph1, align=64):
        esz = {F32: 4, BF16: 2, I32: 4, I16: 2, I8: 1}[dtype]
        nbytes = int(np.prod(shape[1:])) * esz
        nbytes_al = (nbytes + align - 1) // align * align
        taken = sorted((it[1], it[1] + it[2]) for it in self.items
                       if not (it[4] < ph0 or it[3] > ph1))
        merged = []
        for s, e in taken:
            if merged and s <= merged[-1][1]:
                merged[-1][1] = max(merged[-1][1], e)
            else:
                merged.append([s, e])
        holes = []
        prev = 0
        for s, e in merged:
            if s - prev >= nbytes_al:
                holes.append((prev, s - prev))
            prev = max(prev, e)
        if self.LIMIT - prev >= nbytes_al:
            holes.append((prev, self.LIMIT - prev))
        assert holes, f"SBUF overflow: {name} needs {nbytes_al} (no hole)"
        off = min(holes, key=lambda h: h[1])[0]
        self.items.append((name, off, nbytes_al, ph0, ph1))
        return self.nc.alloc_sbuf_tensor_at(name, list(shape), dtype, offset=self.BASE + off)

    def alloc_chunks(self, name, n, shape, dtype, ph0, ph1, align=64):
        return [self.alloc(f"{name}{i}", shape, dtype, ph0, ph1, align) for i in range(n)]


def build_program(nc, d=D, tok=TOK, di=DI, debug=False):
    nk = d // 128
    ni = di // 128
    ntc = tok // 128
    ns = tok // 512
    nblk_d = d // BLK
    nblk_i = di // BLK

    xT = nc.dram_tensor("xT", [d, tok], F32, kind="ExternalInput").ap()
    W_h = nc.dram_tensor("W", [d, 3 * d], F32, kind="ExternalInput").ap()
    wg_h = nc.dram_tensor("wg", [di, d], F32, kind="ExternalInput").ap()
    wu_h = nc.dram_tensor("wu", [di, d], F32, kind="ExternalInput").ap()
    wd_h = nc.dram_tensor("wd", [d, di], F32, kind="ExternalInput").ap()
    ident_h = nc.dram_tensor("ident", [128, 128], F32, kind="ExternalInput").ap()
    outT = nc.dram_tensor("outT", [d, tok], F32, kind="ExternalOutput").ap()

    up_dram = nc.dram_tensor("up_dram", [di, tok], BF16)
    gate_dram = nc.dram_tensor("gate_dram", [di, tok], BF16)
    prod_dram = nc.dram_tensor("prod_dram", [di, tok], BF16)
    g2_dram = nc.dram_tensor("g2_dram", [di, tok], I16)
    xres_dram = nc.dram_tensor("xres_dram", [d, tok], F32)
    wdT_dram = nc.dram_tensor("wdT_dram", [di, d], BF16)
    wuT_dram = nc.dram_tensor("wuT_dram", [di, d], BF16)
    wgT_dram = nc.dram_tensor("wgT_dram", [di, d], BF16)
    nh = nk // 2
    hw_kv = 16 + nh * d
    kv_cc_inA = nc.dram_tensor("kv_cc_inA", [128, hw_kv], F32)
    kv_cc_outA = nc.dram_tensor("kv_cc_outA", [128, hw_kv], F32)
    kv_cc_inB = nc.dram_tensor("kv_cc_inB", [128, hw_kv], F32)
    kv_cc_outB = nc.dram_tensor("kv_cc_outB", [128, hw_kv], F32)
    sc_cc_in = [nc.dram_tensor(f"sc_cc_in{i}", [1, 16], F32) for i in range(4)]
    sc_cc_out = [nc.dram_tensor(f"sc_cc_out{i}", [1, 16], F32) for i in range(4)]
    cc_w_in = nc.dram_tensor("cc_w_in", [1, 16], F32)
    cc_w_out = nc.dram_tensor("cc_w_out", [1, 16], F32)

    sm = SbufMap(nc)
    ident_f = sm.alloc("ident_f", [128, 128], F32, 0, 119)
    ident_b = sm.alloc("ident_b", [128, 128], BF16, 0, 119)
    ones_b = sm.alloc("ones_b", [128, 1], BF16, 0, 119)
    onerow_f = sm.alloc("onerow_f", [1, 128], F32, 0, 119)
    row0 = sm.alloc("row0", [1, tok], F32, 0, 79)
    row1a = sm.alloc("row1a", [1, tok], F32, 10, 29)
    row1b = sm.alloc("row1b", [1, tok], F32, 60, 69)
    scal = sm.alloc("scal", [1, 64], F32, 0, 119)
    bc = sm.alloc("bc", [128, 16], F32, 0, 119)
    mxcols = sm.alloc("mxcols", [128, 64], F32, 0, 119)

    W_bf = sm.alloc("W_bf", [128, nk * 3 * d], BF16, 0, 29)
    ttmp = sm.alloc("ttmp", [128, tok], F32, 10, 29)
    ttmp6 = sm.alloc("ttmp6", [128, tok], F32, 60, 69)
    ln1x = sm.alloc_chunks("ln1x", nk, [128, tok], BF16, 10, 29)
    qe = sm.alloc_chunks("qe", nk, [128, tok], BF16, 20, 59)
    kf = sm.alloc_chunks("kf", nk, [128, tok], BF16, 20, 31)
    vf = sm.alloc_chunks("vf", nk, [128, tok], BF16, 20, 33)
    x_f = sm.alloc_chunks("x_f", nk, [128, tok], F32, 10, 19)
    xbf_t = sm.alloc("xbf_t", [128, 4 * tok], BF16, 10, 19)
    keT = sm.alloc_chunks("keT", ntc, [128, d], BF16, 30, 49)
    vT = sm.alloc_chunks("vT", ntc, [128, d], BF16, 34, 49)
    kvt_f = sm.alloc("kvt_f", [128, nk * d + 32], F32, 40, 49)
    kvtks = sm.alloc("kvtks", [128, nk * d + 32], BF16, 50, 59)
    zbc = sm.alloc("zbc", [128, tok], BF16, 50, 59)
    xres_t = sm.alloc("xres_t", [128, tok], F32, 50, 59)
    xres_t2 = sm.alloc("xres_t2", [128, tok], F32, 50, 59)
    xresf = sm.alloc_chunks("xresf", nk, [128, tok], F32, 50, 69)
    wst_u = sm.alloc_chunks("wst_u", 4, [128, nk * 128], BF16, 80, 89)
    wst_g = sm.alloc_chunks("wst_g", 4, [128, nk * 128], BF16, 80, 89)
    wraw_s = sm.alloc("wraw_s", [128, 6 * d], F32, 30, 79)
    wqf_s = None
    wi8_s = sm.alloc("wi8_s", [128, 4 * d], I8, 50, 79)
    wdq_s = sm.alloc("wdq_s", [128, 4 * d], BF16, 50, 79)
    wsc_s = sm.alloc("wsc_s", [128, 12 * nblk_d], F32, 50, 79)
    xbf_t2 = sm.alloc("xbf_t2", [128, 4 * tok], BF16, 60, 69)
    ln2x = sm.alloc_chunks("ln2x", nk, [128, tok], BF16, 60, 79)
    xq = sm.alloc_chunks("xq", nk, [128, tok], BF16, 70, 89)
    xq16 = sm.alloc("xq16", [128, tok], I16, 70, 79)
    xq16b = sm.alloc("xq16b", [128, tok], I16, 70, 79)
    spill_t = sm.alloc_chunks("spill_t", 4, [128, tok], BF16, 80, 119)
    upq = sm.alloc_chunks("upq", ni, [128, tok], I16, 80, 119)
    gi16 = sm.alloc("gi16", [128, tok], I16, 90, 119)
    gi16b = sm.alloc("gi16b", [128, tok], I16, 90, 119)
    gi16c = sm.alloc("gi16c", [128, tok], I16, 90, 99)
    sig32 = sm.alloc("sig32", [128, tok], I32, 90, 119)
    sig32b = sm.alloc("sig32b", [128, tok], I32, 90, 119)
    sigf = sm.alloc("sigf", [128, tok], F32, 90, 99)
    sigfb = sm.alloc("sigfb", [128, tok], F32, 90, 99)
    prodt = sm.alloc_chunks("prodt", 2, [128, tok], BF16, 90, 99)
    wdT = sm.alloc_chunks("wdT", ni, [128, d], BF16, 100, 119)
    of32 = sm.alloc("of32", [128, 2 * tok], F32, 100, 119)

    o_bf = [upq[i].ap().bitcast(BF16) for i in range(ni)]

    PA = nc.alloc_psum_tensor("PA", [128, 2048], F32)
    PB = nc.alloc_psum_tensor("PB", [128, 2048], F32)
    pages = [PA, PB]

    P = Plan()

    def dma(eng_name, out_ap, in_ap, sem_name):
        v = P.inc(sem_name, 16)
        P.emit(eng_name, lambda e, o=out_ap, i=in_ap, s=sem_name:
               e.dma_start(out=o, in_=i).then_inc(SEMS[s], 16))
        return v

    def mm(out_ap, lhsT_ap, rhs_ap, start, stop):
        P.emit("tensor", lambda e, o=out_ap, l=lhsT_ap, r=rhs_ap, st=start, sp=stop:
               e.matmul(o, l, r, start=st, stop=sp))

    def sep(eng):
        P.emit(eng, lambda e: e.drain())

    _tagged = set()

    def _tag(eng, sem):
        key = (eng, len(P.q[eng]))
        assert key not in _tagged, f"double-tag on {eng} at {len(P.q[eng])}"
        _tagged.add(key)
        v = P.inc(sem, 1)
        fn = P.q[eng].pop()
        P.emit(eng, lambda e, f=fn, s=sem: f(e).then_inc(SEMS[s], 1))
        return v

    def t_mark():
        return _tag("tensor", "T")

    def v_op(fn):
        P.emit("vector", fn)

    def v_mark(drain=False):
        if drain:
            v = P.inc("V", 1)
            P.emit("vector", lambda e: e.drain().then_inc(SEMS["V"], 1))
            return v
        return _tag("vector", "V")

    def g_op(fn):
        P.emit("gpsimd", fn)

    def g_mark(drain=False):
        if drain:
            v = P.inc("G", 1)
            P.emit("gpsimd", lambda e: e.drain().then_inc(SEMS["G"], 1))
            return v
        return _tag("gpsimd", "G")

    def s_op(fn):
        P.emit("scalar", fn)

    def s_mark(drain=False):
        if drain:
            v = P.inc("S", 1)
            P.emit("scalar", lambda e: e.drain().then_inc(SEMS["S"], 1))
            return v
        return _tag("scalar", "S")

    R0 = row0.ap()[0:1, 0:tok]

    def sc(i):
        return scal.ap()[0:1, i:i + 1]

    def bcol(i):
        return bc.ap()[:, i:i + 1]

    dma("sync", ident_f[:, :], ident_h, "LX")
    P.wait("sync", "V", 1)
    ccw = dma("sync", cc_w_in.ap(), scal.ap()[0:1, 0:16], "LBR")
    P.wait("gpsimd", "LBR", ccw)
    P.emit("gpsimd", lambda e: e.collective_compute(
        "AllReduce", ALU.max, replica_groups=[[0, 1, 2, 3, 4, 5, 6, 7]],
        ins=[cc_w_in.ap()], outs=[cc_w_out.ap()]).then_inc(SEMS["CC"], 1))
    P.inc("CC", 1)
    P.emit("gpsimd", lambda e: e.collective_compute(
        "AllReduce", ALU.add, replica_groups=[[0, 1], [2, 3], [4, 5], [6, 7]],
        ins=[cc_w_in.ap()], outs=[cc_w_out.ap()]).then_inc(SEMS["CC"], 1))
    P.inc("CC", 1)
    dma("gpsimd", ident_b[:, :], ident_h, "GW")
    v_op(lambda e: e.memset(ones_b[:, :], 1.0))
    v_op(lambda e: e.memset(onerow_f[:, :], 1.0))
    v_op(lambda e: e.memset(scal.ap()[0:1, :], 0.0))
    v_op(lambda e: e.memset(sc(S_MHALF), -0.5))
    v_op(lambda e: e.memset(sc(S_MONE), -1.0))
    v_const = v_mark(drain=True)
    for k in range(nk):
        dma("gpsimd", W_bf[:, k * 3 * d:(k + 1) * 3 * d], W_h[k * 128:(k + 1) * 128, :], "GW")
    gw_all = P.now("GW")
    for k in range(nk // 2):
        dma("sync", x_f[k][:, :], xT[k * 128:(k + 1) * 128, :], "LX")
    lx_lo = P.now("LX")
    for k in range(nk // 2, nk):
        dma("scalar", x_f[k][:, :], xT[k * 128:(k + 1) * 128, :], "LX2")
    lx_hi = P.now("LX2")

    def ln_block(src, arrive, consumed, xbf_buf, R1):
        tks = {}
        for k in range(nk):
            xo = (k % 2) * tok
            so = 2 * tok + (k % 2) * tok
            arrive(k, "vector")
            if k >= 2:
                P.wait("vector", "T", tks[k - 2])
            v_op(lambda e, a=src(k), xb=xbf_buf, xo=xo: e.tensor_copy(xb.ap()[:, xo:xo + tok], a))
            vv = v_mark()
            arrive(k, "scalar")
            if k >= 2:
                P.wait("scalar", "T", tks[k - 2])
            s_op(lambda e, a=src(k), xb=xbf_buf, so=so: e.activation(xb.ap()[:, so:so + tok], a,
                                                                    AF.Square))
            sv = s_mark()
            P.wait("tensor", "V", vv)
            P.wait("tensor", "S", sv)
            for s in range(ns):
                mm(PA[0:1, s * 512:(s + 1) * 512], ones_b[:, 0:1],
                   xbf_buf.ap()[:, xo + s * 512: xo + (s + 1) * 512],
                   start=(k == 0), stop=(k == nk - 1))
                mm(PB[0:1, s * 512:(s + 1) * 512], ones_b[:, 0:1],
                   xbf_buf.ap()[:, so + s * 512: so + (s + 1) * 512],
                   start=(k == 0), stop=(k == nk - 1))
            tks[k] = t_mark()
            consumed(k, tks[k])
        tstat = P.now("T")
        P.wait("vector", "T", tstat)
        v_op(lambda e: e.tensor_scalar(R0, PA[0:1, 0:tok], -1.0 / d, None, ALU.mult))
        v_op(lambda e: e.tensor_scalar(R1, PB[0:1, 0:tok], 1.0 / d, None, ALU.mult))
        v_op(lambda e: e.tensor_tensor(PA[0:1, 0:tok], R0, R0, ALU.mult))
        v_op(lambda e: e.tensor_tensor(R1, R1, PA[0:1, 0:tok], ALU.subtract))
        v_op(lambda e: e.tensor_scalar(R1, R1, LN_EPS, None, ALU.add))
        vrec = v_mark(drain=True)
        P.wait("scalar", "V", vrec)
        s_op(lambda e: e.activation(R1, R1, AF.Ln))
        s_op(lambda e: e.activation(R1, R1, AF.Exp, scale=sc(S_MHALF)))
        ssq = s_mark(drain=True)
        P.wait("vector", "S", ssq)
        v_op(lambda e: e.tensor_tensor(R0, R0, R1, ALU.mult))
        vrows = v_mark(drain=True)
        P.wait("tensor", "V", vrows)
        for s in range(ns):
            mm(PA[:, s * 512:(s + 1) * 512], onerow_f[0:1, :], R1[:, s * 512:(s + 1) * 512], True, True)
            mm(PB[:, s * 512:(s + 1) * 512], onerow_f[0:1, :], R0[:, s * 512:(s + 1) * 512], True, True)
        return t_mark()

    P.wait("vector", "V", v_const)

    def arr1(k, eng):
        if k < nk // 2:
            P.wait(eng, "LX", lx_lo)
        else:
            P.wait(eng, "LX2", lx_hi)

    tb1 = ln_block(lambda k: x_f[k].ap()[:, :], arr1, lambda k, tk: None, xbf_t, row1a.ap()[0:1, :])
    P.wait("vector", "T", tb1)
    for k in range(nk):
        xk = x_f[k].ap()[:, :]
        ok = ln1x[k].ap()[:, :]
        v_op(lambda e, xk=xk: e.tensor_tensor(ttmp[:, 0:tok], xk, PA[:, 0:tok], ALU.mult))
        v_op(lambda e, ok=ok: e.tensor_tensor(ok, ttmp[:, 0:tok], PB[:, 0:tok], ALU.add))
    v_ln1 = v_mark()

    P.wait("tensor", "V", v_ln1)
    P.wait("tensor", "GW", gw_all)
    evac_done = {}
    morder = list(range(nk, 2 * nk)) + list(range(2 * nk, 3 * nk)) + list(range(nk))
    for idx, m in enumerate(morder):
        pg = pages[idx % 2]
        if idx >= 2:
            P.wait("tensor", *evac_done[idx % 2])
        for k in range(nk):
            lhs = W_bf.ap()[:, k * 3 * d + m * 128: k * 3 * d + (m + 1) * 128]
            for s in range(ns):
                mm(pg[:, s * 512:(s + 1) * 512], lhs,
                   ln1x[k].ap()[:, s * 512:(s + 1) * 512],
                   start=(k == 0), stop=(k == nk - 1))
        tq = t_mark()
        if m < 2 * nk:
            dstbuf = qe if m < nk else kf
            dst = dstbuf[m % nk].ap()[:, :]
            P.wait("vector", "T", tq)
            v_op(lambda e, pg=pg: e.tensor_scalar(ttmp[:, 0:tok], pg[:, 0:tok], 0.0, None, ALU.min))
            vv = v_mark()
            P.wait("scalar", "V", vv)
            s_op(lambda e: e.activation(ttmp[:, 0:tok], ttmp[:, 0:tok], AF.Exp))
            sv = s_mark()
            P.wait("vector", "S", sv)
            v_op(lambda e, pg=pg, dst=dst: e.scalar_tensor_tensor(
                dst, pg[:, 0:tok], 0.0, ttmp[:, 0:tok], ALU.max, ALU.add))
            evac_done[idx % 2] = ("V", v_mark())
        else:
            dst = vf[m - 2 * nk].ap()[:, :]
            P.wait("scalar", "T", tq)
            s_op(lambda e, pg=pg, dst=dst: e.copy(dst, pg[:, 0:tok]))
            evac_done[idx % 2] = ("S", s_mark())
    P.wait("tensor", *evac_done[0])
    P.wait("tensor", *evac_done[1])

    regions3 = [(PA, 0), (PA, 1024), (PB, 0), (PB, 1024)]
    tr_evac = {}
    ri = 0
    for src, dst in ((kf, keT), (vf, vT)):
        for c in range(ntc):
            pg, off = regions3[ri % 4]
            if ri >= 4:
                P.wait("tensor", *tr_evac[ri % 4])
            for k in range(nk):
                mm(pg[:, off + k * 128: off + (k + 1) * 128],
                   src[k].ap()[:, c * 128:(c + 1) * 128],
                   ident_b[:, :], True, True)
            tt = t_mark()
            P.wait("scalar", "T", tt)
            s_op(lambda e, pg=pg, off=off, dst=dst, c=c:
                 e.copy(dst[c].ap()[:, :], pg[:, off:off + d]))
            tr_evac[ri % 4] = ("S", s_mark())
            ri += 1
    for r in range(min(4, ri)):
        P.wait("tensor", *tr_evac[r])
    wu_entry = (P.now("S"), P.now("G"), P.now("V"))
    wu_entry_t = P.now("T")

    regions4 = [(PA, 0), (PA, 1024), (PB, 0)]
    kvt_evac = {}

    def kvoff(dk):
        return 16 + dk * d + (16 if dk >= nh else 0)

    def kscol(dk):
        return dk if dk < nh else 16 + nh * d + (dk - nh)

    v_half = {}
    for dk in range(nk):
        pg, off = regions4[dk % 3]
        if dk >= 3:
            P.wait("tensor", *kvt_evac[dk % 3])
        for c in range(ntc):
            lhs = keT[c].ap()[:, dk * 128:(dk + 1) * 128]
            w2 = min(512, d)
            for s2 in range(max(1, d // 512)):
                mm(pg[:, off + s2 * w2: off + (s2 + 1) * w2], lhs,
                   vT[c].ap()[:, s2 * w2:(s2 + 1) * w2],
                   start=(c == 0), stop=(c == ntc - 1))
            kcol = 2032 + dk if dk < nh else 1024 + (dk - nh)
            mm(PB[:, kcol: kcol + 1], lhs, ones_b[:, 0:1],
               start=(c == 0), stop=(c == ntc - 1))
        tkv = t_mark()
        P.wait("vector", "T", tkv)
        v_op(lambda e, pg=pg, off=off, dk=dk:
             e.tensor_copy(kvt_f.ap()[:, kvoff(dk):kvoff(dk) + d], pg[:, off:off + d]))
        kvt_evac[dk % 3] = ("V", v_mark())
        if dk == nh - 1 or dk == nk - 1:
            ksb = 0 if dk < nh else 16 + nh * d
            dk0 = 0 if dk < nh else nh
            kc0 = 2032 if dk < nh else 1024
            v_op(lambda e, ksb=ksb: e.memset(kvt_f.ap()[:, ksb:ksb + 16], 0.0))
            v_op(lambda e, ksb=ksb, kc0=kc0: e.tensor_copy(
                kvt_f.ap()[:, ksb:ksb + nh], PB[:, kc0:kc0 + nh]))
            v_half[dk >= nh] = v_mark(drain=True)
    v_kvt = v_half[True]

    P.wait("gpsimd", "V", v_half[False])
    kv1a = dma("gpsimd", kv_cc_inA[:, :], kvt_f.ap()[:, 0:hw_kv], "KV1")
    P.wait("gpsimd", "KV1", kv1a)
    P.emit("gpsimd", lambda e: e.collective_compute(
        "AllReduce", ALU.add, replica_groups=[[0, 1], [2, 3], [4, 5], [6, 7]],
        ins=[kv_cc_inA.ap()], outs=[kv_cc_outA.ap()]).then_inc(SEMS["CC"], 1))
    cc_kva = P.inc("CC", 1)
    P.wait("gpsimd", "V", v_half[True])
    kv1b = dma("gpsimd", kv_cc_inB[:, :], kvt_f.ap()[:, hw_kv:2 * hw_kv], "KV1")
    P.wait("gpsimd", "KV1", kv1b)
    P.emit("gpsimd", lambda e: e.collective_compute(
        "AllReduce", ALU.add, replica_groups=[[0, 1], [2, 3], [4, 5], [6, 7]],
        ins=[kv_cc_inB.ap()], outs=[kv_cc_outB.ap()]).then_inc(SEMS["CC"], 1))
    cc_kv = P.inc("CC", 1)

    def weight_prep(src_h, nch, wc, gsz, dstT_sb, dstT_dram, wraw, wi8, wdq, wsc, wqf, wsem, dbuf=True, col0=0, j0ofs=0, mrow=False, entry=None, entry_t=None):
        njt = wc // 128
        groups = [(s, min(gsz, nch - s)) for s in range(0, nch, gsz)]
        ngr = len(groups)
        gw_max = gsz * wc
        eS, eG, eV = entry if entry is not None else (P.now("S"), P.now("G"), P.now("V"))
        if entry_t is not None:
            P.wait("sync", "T", entry_t)
            P.wait("scalar", "T", entry_t)
        P.wait("sync", "S", eS)
        P.wait("sync", "G", eG)
        P.wait("sync", "V", eV)
        P.wait("scalar", "V", eV)
        P.wait("vector", "G", eG)
        arr = {}
        g_marks = {}
        t_marks = {}
        vtt1_marks = {}
        tpose_evac = {}
        s_prev = P.now("S")
        spwd_prev = {0: P.now("SPWD"), 1: P.now("SPW1")}
        s_hist = {}
        spwd_hist = {}

        def load(g):
            half = (g % 3) * gw_max
            m0, gn = groups[g]
            sem = wsem[g % 3]
            if gn == 1:
                h = wc // 2
                dma("sync", wraw.ap()[:, half:half + h],
                    src_h[m0 * 128:(m0 + 1) * 128, col0:col0 + h], sem)
                arr[g] = dma("scalar", wraw.ap()[:, half + h:half + wc],
                             src_h[m0 * 128:(m0 + 1) * 128, col0 + h:col0 + wc], sem)
            else:
                for mi in range(gn):
                    m = m0 + mi
                    arr[g] = dma("sync" if mi == 0 else "scalar",
                                 wraw.ap()[:, half + mi * wc: half + (mi + 1) * wc],
                                 src_h[m * 128:(m + 1) * 128, col0:col0 + wc], sem)

        def sl(g):
            m0, gn = groups[g]
            gw = gn * wc
            nbg = gw // BLK
            half_s = (g % 3) * 2 * (gw_max // BLK)
            svp = wsc.ap()[:, half_s:half_s + nbg]
            isv = wsc.ap()[:, half_s + (gw_max // BLK):half_s + (gw_max // BLK) + nbg]
            buf = wraw.ap()[:, (g % 3) * gw_max:][:, 0:gw]
            return gw, nbg, svp, isv, buf

        load(0)
        if ngr > 1:
            load(1)
        if ngr > 2:
            load(2)
        gw0, nbg0, svp0, isv0, buf0 = sl(0)
        P.wait("vector", wsem[0], arr[0])
        v_op(lambda e, buf=buf0, svp=svp0: e.tensor_reduce(
            svp, buf.rearrange("p (b c) -> p b c", c=BLK), AX.X, ALU.max,
            apply_absolute_value=True))
        sep("vector")
        v_op(lambda e, svp=svp0: e.tensor_scalar(svp, svp, 1.0 / 7.0, GUARD, ALU.mult, ALU.max))
        sep("vector")
        v_op(lambda e, svp=svp0, isv=isv0: e.reciprocal(isv, svp))
        if ngr == 1:
            sep("vector")
        for g in range(ngr):
            m0, gn = groups[g]
            gw, nbg, svp, isv, buf = sl(g)
            if g + 1 < ngr:
                gw1, nbg1, svp1, isv1, buf1 = sl(g + 1)
                P.wait("vector", wsem[(g + 1) % 3], arr[g + 1])
                if g >= 2:
                    P.wait("vector", "G", g_marks[g - 2])
                v_op(lambda e, buf=buf1, svp=svp1: e.tensor_reduce(
                    svp, buf.rearrange("p (b c) -> p b c", c=BLK), AX.X, ALU.max,
                    apply_absolute_value=True))
            else:
                sep("vector")
            v_op(lambda e, buf=buf, isv=isv, nbg=nbg, gw=gw: e.tensor_tensor(
                buf.rearrange("p (b c) -> p b c", c=BLK),
                buf.rearrange("p (b c) -> p b c", c=BLK),
                isv.rearrange("p (b o) -> p b o", o=1).broadcast_to([128, nbg, BLK]),
                ALU.mult))
            vtt1_marks[g] = v_mark()
            if g + 1 < ngr:
                v_op(lambda e, svp=svp1: e.tensor_scalar(svp, svp, 1.0 / 7.0, GUARD,
                                                         ALU.mult, ALU.max))
            w8h = (g % 2) * gw_max
            if g >= 2:
                P.wait("vector", "G", g_marks[g - 2])
            v_op(lambda e, gw=gw, w8h=w8h, buf=buf: e.tensor_scalar(wi8.ap()[:, w8h:w8h + gw], buf,
                                                                    1.0, None, ALU.mult))
            vr8 = v_mark()
            if g + 3 < ngr:
                P.wait("sync", "V", vr8)
                P.wait("scalar", "V", vr8)
                load(g + 3)
            if g + 1 < ngr:
                v_op(lambda e, svp=svp1, isv=isv1: e.reciprocal(isv, svp))
            P.wait("gpsimd", "V", vr8)
            wofs = (g % 2) * gw_max
            if g >= 2:
                P.wait("gpsimd", "T", t_marks[g - 2])
            if dstT_dram is not None:
                P.wait("gpsimd", "S", s_hist[g - 2] if g >= 2 else s_prev)
                P.wait("gpsimd", ("SPWD", "SPW1")[g % 2],
                       spwd_hist[g - 2] if g >= 2 else spwd_prev[g % 2])
            g_op(lambda e, svp=svp, nbg=nbg, gw=gw, w8h=w8h, wofs=wofs: e.tensor_tensor(
                wdq.ap()[:, wofs:wofs + gw].rearrange("p (b c) -> p b c", c=BLK),
                wi8.ap()[:, w8h:w8h + gw].rearrange("p (b c) -> p b c", c=BLK),
                svp.rearrange("p (b o) -> p b o", o=1).broadcast_to([128, nbg, BLK]),
                ALU.mult))
            g_marks[g] = g_mark()
            P.wait("tensor", "G", g_marks[g])
            for mi in range(gn):
                m = m0 + mi
                nbat = (njt + 7) // 8
                for jb in range(nbat):
                    j0, jn = jb * 8, min(8, njt - jb * 8)
                    rr = ((m0 + mi) * nbat + jb) % 4
                    pg, off = regions3[rr]
                    if (m0 + mi) * nbat + jb >= 4:
                        P.wait("tensor", *tpose_evac[rr])
                    for j in range(j0, j0 + jn):
                        mm(pg[:, off + (j - j0) * 128: off + (j - j0 + 1) * 128],
                           wdq.ap()[:, wofs + mi * wc + j * 128: wofs + mi * wc + (j + 1) * 128],
                           ident_b[:, :], True, True)
                    tt = t_mark()
                    P.wait("scalar", "T", tt)
                    if dstT_sb is not None:
                        s_op(lambda e, pg=pg, off=off, m=m, j0=j0, jn=jn:
                             e.copy(dstT_sb[m].ap()[:, j0 * 128:(j0 + jn) * 128],
                                    pg[:, off: off + jn * 128]))
                        svv = s_mark()
                    else:
                        s_op(lambda e, pg=pg, off=off, j0=j0, jn=jn, mi=mi, wc=wc, wofs=wofs:
                             e.copy(wdq.ap()[:, wofs + mi * wc + j0 * 128: wofs + mi * wc + (j0 + jn) * 128],
                                    pg[:, off: off + jn * 128]))
                        svv = s_mark()
                        P.wait("sync", "S", svv)
                        spsem = ("SPWD", "SPW1")[g % 2]
                        if mrow:
                            dma("sync",
                                dstT_dram.ap()[m * 128:(m + 1) * 128,
                                               (j0ofs + j0) * 128:(j0ofs + j0 + jn) * 128],
                                wdq.ap()[:, wofs + mi * wc + j0 * 128: wofs + mi * wc + (j0 + jn) * 128], spsem)
                        else:
                            dma("sync",
                                dstT_dram.ap()[(j0ofs + j0) * 128:(j0ofs + j0 + jn) * 128,
                                               m * 128:(m + 1) * 128].rearrange("(j p) c -> p j c", p=128),
                                wdq.ap()[:, wofs + mi * wc + j0 * 128: wofs + mi * wc + (j0 + jn) * 128]
                                .rearrange("p (j c) -> p j c", c=128), spsem)
                    tpose_evac[rr] = ("S", svv)
            t_marks[g] = P.now("T")
            s_hist[g] = P.now("S")
            spwd_hist[g] = P.now(("SPWD", "SPW1")[g % 2])
        for r in tpose_evac.values():
            P.wait("tensor", *r)

    weight_prep(wu_h, ni, d, 2, None, wuT_dram, wraw_s, wi8_s, wdq_s, wsc_s, wqf_s,
                ("WR0", "WR1", "WR2"), mrow=True, entry=wu_entry, entry_t=wu_entry_t)
    weight_prep(wg_h, ni, d, 2, None, wgT_dram, wraw_s, wi8_s, wdq_s, wsc_s, wqf_s, ("WR0", "WR1", "WR2"), mrow=True)
    wd_entry = (P.now("S"), P.now("G"), P.now("V"))

    P.wait("gpsimd", "CC", cc_kva)
    kv2a = dma("gpsimd", kvtks.ap()[:, 0:hw_kv], kv_cc_outA[:, :], "KV2")
    P.wait("gpsimd", "CC", cc_kv)
    kv2b = dma("gpsimd", kvtks.ap()[:, hw_kv:2 * hw_kv], kv_cc_outB[:, :], "KV3")

    P.wait("tensor", "KV2", kv2a)
    for dk in range(nk):
        if dk == nh:
            P.wait("tensor", "KV3", kv2b)
        ksc = kvtks.ap()[:, kscol(dk): kscol(dk) + 1]
        for s in range(ns):
            mm(PA[0:1, s * 512:(s + 1) * 512], ksc,
               qe[dk].ap()[:, s * 512:(s + 1) * 512],
               start=(dk == 0), stop=(dk == nk - 1))
    tden = t_mark()
    P.wait("vector", "T", tden)
    v_op(lambda e: e.tensor_scalar(R0, PA[0:1, 0:tok], ATT_EPS, None, ALU.add))
    vz0 = v_mark(drain=True)
    P.wait("scalar", "V", vz0)
    s_op(lambda e: e.activation(R0, R0, AF.Ln))
    s_op(lambda e: e.activation(R0, R0, AF.Exp, scale=sc(S_MONE)))
    vz = s_mark(drain=True)
    P.wait("tensor", "S", vz)
    for s in range(ns):
        mm(PB[:, s * 512:(s + 1) * 512], onerow_f[0:1, :], R0[:, s * 512:(s + 1) * 512], True, True)
    tzb = t_mark()
    P.wait("vector", "T", tzb)
    v_op(lambda e: e.tensor_copy(zbc.ap()[:, :], PB[:, 0:tok]))
    v_zbc = v_mark()

    P.wait("tensor", "V", v_zbc)
    xres_evac = {}
    xts = [xres_t, xres_t2]
    xsem = ("LX2", "LXR")
    lx2a = {0: dma("sync", xres_t.ap()[:, :], xT[0:128, :], "LX2"),
            1: dma("sync", xres_t2.ap()[:, :], xT[128:256, :], "LXR")}
    for h in range(nk):
        pg = pages[h % 2]
        if h >= 2:
            P.wait("tensor", *xres_evac[h % 2])
        for dk in range(nk):
            lhs = kvtks.ap()[:, kvoff(dk) + h * 128: kvoff(dk) + (h + 1) * 128]
            for s in range(ns):
                mm(pg[:, s * 512:(s + 1) * 512], lhs,
                   qe[dk].ap()[:, s * 512:(s + 1) * 512],
                   start=(dk == 0), stop=(dk == nk - 1))
        tv = t_mark()
        P.wait("vector", "T", tv)
        P.wait("vector", xsem[h % 2], lx2a[h])
        v_op(lambda e, pg=pg, h=h: e.tensor_tensor(xresf[h].ap()[:, :], pg[:, 0:tok], zbc.ap()[:, :], ALU.mult))
        v_op(lambda e, h=h: e.tensor_tensor(xresf[h].ap()[:, :], xresf[h].ap()[:, :],
                                            xts[h % 2].ap()[:, :], ALU.add))
        vv = v_mark()
        xres_evac[h % 2] = ("V", vv)
        xres_vv = xres_evac[h % 2]
        P.wait("sync", "V", vv)
        if h + 2 < nk:
            lx2a[h + 2] = dma("sync", xts[h % 2].ap()[:, :], xT[(h + 2) * 128:(h + 3) * 128, :],
                              xsem[h % 2])
        dma("sync", xres_dram[h * 128:(h + 1) * 128, :], xresf[h].ap()[:, :], "SXR")
    sxr_all = P.now("SXR")

    def arrive6(k, eng):
        pass

    v_xres_all = P.now("V")
    P.wait("vector", "V", v_xres_all)
    P.wait("scalar", "V", v_xres_all)
    tb2 = ln_block(lambda k: xresf[k].ap()[:, :], arrive6, lambda k, tk: None,
                   xbf_t2, row1b.ap()[0:1, :])
    P.wait("vector", "T", tb2)
    app6 = {}
    for k in range(nk):
        xk = xresf[k].ap()[:, :]
        ok = ln2x[k].ap()[:, :]
        v_op(lambda e, xk=xk: e.tensor_tensor(ttmp6[:, 0:tok], xk, PA[:, 0:tok], ALU.mult))
        v_op(lambda e, ok=ok: e.tensor_tensor(ok, ttmp6[:, 0:tok], PB[:, 0:tok], ALU.add))
        v_op(lambda e, k=k: e.tensor_reduce(mxcols.ap()[:, k:k + 1], ln2x[k].ap()[:, :],
                                            AX.X, ALU.max, apply_absolute_value=True))
        app6[k] = v_mark()
    v_ln2 = app6[nk - 1]

    def fold_cols(c0, c1, slot, wait_val):
        P.wait("vector", "V", wait_val)
        sep("vector")
        v_op(lambda e, c0=c0, c1=c1: e.tensor_reduce(bcol(B_RUN), mxcols.ap()[:, c0:c1],
                                                     AX.X, ALU.max))
        vr = v_mark(drain=True)
        P.wait("tensor", "V", vr)
        mm(PA[0:1, 0:128], bcol(B_RUN), ident_f[:, :], True, True)
        tf = t_mark()
        P.wait("vector", "T", tf)
        v_op(lambda e, slot=slot: e.tensor_reduce(sc(slot), PA[0:1, 0:128], AX.X, ALU.max))
        return v_mark(drain=True)

    def barrier_start(idx, wait_v):
        P.wait("sync", "V", wait_v)
        lbr = dma("sync", sc_cc_in[idx][:, :], scal.ap()[0:1, 0:16], "LBR")
        P.wait("gpsimd", "LBR", lbr)
        P.emit("gpsimd", lambda e, idx=idx: e.collective_compute(
            "AllReduce", ALU.max, replica_groups=[list(range(N_CORES))],
            ins=[sc_cc_in[idx].ap()], outs=[sc_cc_out[idx].ap()]).then_inc(SEMS["CC"], 1))
        return P.inc("CC", 1)

    def barrier_end(idx, cb):
        P.wait("sync", "CC", cb)
        return dma("sync", scal.ap()[0:1, 0:16], sc_cc_out[idx][:, :], "GBR")

    def barrier(idx, wait_v):
        return barrier_end(idx, barrier_start(idx, wait_v))

    def bcast_scalar(src_slot, dst_col, wait_sem, wait_val):
        P.wait("tensor", wait_sem, wait_val)
        mm(PA[:, 0:1], onerow_f[0:1, :], sc(src_slot), True, True)
        tf = t_mark()
        P.wait("vector", "T", tf)
        v_op(lambda e, dst_col=dst_col: e.tensor_copy(bcol(dst_col), PA[:, 0:1]))
        return v_mark(drain=True)

    weight_prep(wd_h, nk, di // 2, 1, None, wdT_dram, wraw_s, wi8_s, wdq_s, wsc_s, wqf_s,
                ("WR0", "WR1", "WR2"), entry=wd_entry, col0=0, j0ofs=0)
    weight_prep(wd_h, nk, di // 2, 1, None, wdT_dram, wraw_s, wi8_s, wdq_s, wsc_s, wqf_s,
                ("WR0", "WR1", "WR2"), col0=di // 2, j0ofs=di // 256)
    spwd_all = (P.now("SPWD"), P.now("SPW1"))

    vf0 = fold_cols(0, nk, 0, v_ln2)
    gb0 = barrier(0, vf0)
    P.wait("vector", "GBR", gb0)
    v_op(lambda e: e.tensor_scalar(sc(S_S1), sc(0), 1.0 / QMAX, GUARD, ALU.mult, ALU.max))
    sep("vector")
    v_op(lambda e: e.reciprocal(sc(S_C1), sc(S_S1)))
    v_s1 = v_mark(drain=True)
    v_c1b = bcast_scalar(S_C1, B_C1, "V", v_s1)

    P.wait("vector", "V", v_c1b)
    xq16s = [xq16.ap()[:, :], xq16b.ap()[:, :]]
    sxq_m = {}
    for k in range(nk):
        src = ln2x[k].ap()[:, :]
        if k >= 2:
            P.wait("vector", "S", sxq_m[k - 2])
        v_op(lambda e, src=src, k=k: e.tensor_scalar(xq16s[k % 2], src, bcol(B_C1), None, ALU.mult))
        vvq = v_mark()
        P.wait("scalar", "V", vvq)
        s_op(lambda e, k=k: e.activation(xq[k].ap()[:, :], xq16s[k % 2], AF.Copy))
        sxq_m[k] = s_mark()
    v_xq = P.now("V")
    s_xq = P.now("S")

    P.wait("tensor", "V", v_xq)
    P.wait("tensor", "S", s_xq)
    a_evac = {}
    thw = min(1024, tok)
    nth = (tok + 1023) // 1024
    LWS = ("LW0", "LW1", "LW2", "LW3")
    arrW = {}
    ta_m = {}
    P.wait("sync", "SPWD", spwd_all[0])
    P.wait("sync", "SPW1", spwd_all[1])
    P.wait("sync", "V", v_xq)
    P.wait("sync", "S", s_xq)

    def loadw(m):
        dma("sync", wst_u[m % 4][:, :], wuT_dram[m * 128:(m + 1) * 128, :], LWS[m % 4])
        arrW[m] = dma("sync", wst_g[m % 4][:, :], wgT_dram[m * 128:(m + 1) * 128, :], LWS[m % 4])

    loadw(0)
    loadw(1)
    for m in range(ni):
        if m + 2 < ni:
            if m >= 2:
                P.wait("sync", "T", ta_m[m - 2])
            loadw(m + 2)
        P.wait("tensor", LWS[m % 4], arrW[m])
        for th in range(nth):
            pi = (nth * m + th) % 2
            pg = pages[pi]
            if nth * m + th >= 2:
                P.wait("tensor", *a_evac[pi])
            for k in range(nk):
                lu = wst_u[m % 4][:, k * 128:(k + 1) * 128]
                lg = wst_g[m % 4][:, k * 128:(k + 1) * 128]
                for s in range(max(1, thw // 512)):
                    mv = xq[k].ap()[:, th * 1024 + s * 512: th * 1024 + s * 512 + 512]
                    mm(pg[:, s * 512:(s + 1) * 512], lu, mv, start=(k == 0), stop=(k == nk - 1))
                    mm(pg[:, 1024 + s * 512: 1024 + (s + 1) * 512], lg, mv, start=(k == 0), stop=(k == nk - 1))
            ta = t_mark()
            ta_m[m] = ta
            P.wait("scalar", "T", ta)
            if m >= 2 and th == 0:
                P.wait("scalar", ("SA0", "SA1")[m % 2], P.now(("SA0", "SA1")[m % 2]))
            us = o_bf[m]
            gs = spill_t[2 + m % 2].ap()
            tb = th * 1024
            s_op(lambda e, pg=pg, us=us, tb=tb: e.copy(us[:, tb:tb + thw], pg[:, 0:thw]))
            s_op(lambda e, pg=pg, gs=gs, tb=tb: e.copy(gs[:, tb:tb + thw], pg[:, 1024:1024 + thw]))
            a_evac[pi] = ("S", s_mark())
        P.wait("vector", "S", a_evac[(nth * m + nth - 1) % 2][1])
        v_op(lambda e, m=m: e.tensor_reduce(mxcols.ap()[:, m:m + 1], o_bf[m][:, :],
                                            AX.X, ALU.max, apply_absolute_value=True))
        v_op(lambda e, m=m: e.tensor_reduce(mxcols.ap()[:, 32 + m:33 + m], spill_t[2 + m % 2].ap()[:, :],
                                            AX.X, ALU.max, apply_absolute_value=True))
        va = v_mark()
        P.wait("sync", "V", va)
        dma("sync", gate_dram[m * 128:(m + 1) * 128, :], spill_t[2 + m % 2].ap()[:, :], ("SA0", "SA1")[m % 2])
    v_passA = P.now("V")
    sa_all = (P.now("SA0"), P.now("SA1"))

    vfu = fold_cols(0, ni, 0, v_passA)
    vfg = fold_cols(32, 32 + ni, 1, vfu)
    cb1 = barrier_start(1, vfg)
    P.wait("sync", "SA0", sa_all[0])
    P.wait("sync", "SA1", sa_all[1])
    arrB = {}
    arrB[0] = dma("sync", spill_t[2].ap()[:, :], gate_dram[0:128, :], "LB0")
    gb1 = barrier_end(1, cb1)

    P.wait("vector", "GBR", gb1)
    v_op(lambda e: e.tensor_tensor(sc(S_T0), sc(0), sc(S_S1), ALU.mult))
    sep("vector")
    v_op(lambda e: e.tensor_scalar(sc(S_S2), sc(S_T0), 1.0 / QMAX, GUARD, ALU.mult, ALU.max))
    sep("vector")
    v_op(lambda e: e.reciprocal(sc(S_T0), sc(S_S2)))
    sep("vector")
    v_op(lambda e: e.tensor_tensor(sc(S_F2), sc(S_T0), sc(S_S1), ALU.mult))
    v_op(lambda e: e.tensor_tensor(sc(S_T1), sc(1), sc(S_S1), ALU.mult))
    sep("vector")
    v_op(lambda e: e.tensor_scalar(sc(S_S3), sc(S_T1), 1.0 / QMAX, GUARD, ALU.mult, ALU.max))
    sep("vector")
    v_op(lambda e: e.reciprocal(sc(S_T1), sc(S_S3)))
    sep("vector")
    v_op(lambda e: e.tensor_tensor(sc(S_F3), sc(S_T1), sc(S_S1), ALU.mult))
    v_sc1 = v_mark(drain=True)
    v_b1 = bcast_scalar(S_F2, B_F2, "V", v_sc1)
    v_b2 = bcast_scalar(S_F3, B_F3, "V", v_b1)
    v_b3 = bcast_scalar(S_S3, B_S3, "V", v_b2)

    P.wait("vector", "V", v_b3)
    b_done = {}
    gcb = [gi16.ap()[:, :], gi16b.ap()[:, :], gi16c.ap()[:, :]]
    sgf = [sigf[:, :], sigfb[:, :]]
    sg32 = [sig32.ap()[:, :], sig32b.ap()[:, :]]
    vg_m = {}
    vs_m = {}
    sg_m = {}
    gp_m = {}
    for m in range(ni + 2):
        p = m % 2
        if m < ni:
            if m + 1 < ni:
                if m >= 1:
                    P.wait("sync", "V", vg_m[m - 1])
                arrB[m + 1] = dma("sync", spill_t[2 + (m + 1) % 2].ap()[:, :],
                                  gate_dram[(m + 1) * 128:(m + 2) * 128, :], ("LB0", "LB1")[(m + 1) % 2])
            P.wait("vector", ("LB0", "LB1")[p], arrB[m])
            if m >= 3:
                P.wait("vector", "G", gp_m[m - 3])
            v_op(lambda e, m=m: e.tensor_scalar(upq[m].ap()[:, :], o_bf[m][:, :],
                                                bcol(B_F2), None, ALU.mult))
            v_op(lambda e, m=m, p=p: e.tensor_scalar(gcb[m % 3], spill_t[2 + p].ap()[:, :],
                                                     bcol(B_F3), None, ALU.mult))
            vg_m[m] = v_mark()
            P.wait("scalar", "V", vg_m[m])
            if m >= 2:
                P.wait("scalar", "V", vs_m[m - 2])
            s_op(lambda e, m=m, p=p: e.activation(sgf[p], gcb[m % 3], AF.Sigmoid, scale=bcol(B_S3)))
            sg_m[m] = s_mark()
        mm1 = m - 1
        if 0 <= mm1 < ni:
            q = mm1 % 2
            P.wait("vector", "S", sg_m[mm1])
            v_op(lambda e, q=q: e.tensor_scalar(sg32[q], sgf[q], 65536.0, None, ALU.mult))
            vs_m[mm1] = v_mark()
            P.wait("gpsimd", "V", vs_m[mm1])
            if mm1 >= 2:
                P.wait("gpsimd", ("SB0", "SB1")[q], P.now(("SB0", "SB1")[q]))
            g_op(lambda e, q=q, mm1=mm1: e.tensor_tensor(prodt[q].ap()[:, :], gcb[mm1 % 3],
                                                          sg32[q], ALU.mult))
            gp_m[mm1] = g_mark()
        mm2 = m - 2
        if mm2 >= 0:
            q2 = mm2 % 2
            P.wait("vector", "G", gp_m[mm2])
            v_op(lambda e, q2=q2, mm2=mm2: e.tensor_reduce(mxcols.ap()[:, mm2:mm2 + 1],
                                                           prodt[q2].ap()[:, :], AX.X,
                                                           ALU.max, apply_absolute_value=True))
            b_done[q2] = v_mark()
            P.wait("sync", "V", b_done[q2])
            dma("sync", prod_dram[mm2 * 128:(mm2 + 1) * 128, :], prodt[q2].ap()[:, :],
                ("SB0", "SB1")[q2])
    v_passB = P.now("V")
    sb_all = (P.now("SB0"), P.now("SB1"))

    vf2 = fold_cols(0, ni, 0, v_passB)
    cb2 = barrier_start(2, vf2)
    P.wait("sync", "SB0", sb_all[0])
    P.wait("sync", "SB1", sb_all[1])
    arrC = {0: dma("sync", spill_t[0].ap()[:, :], prod_dram[0:128, :], "LC0")}
    P.wait("sync", "SPWD", spwd_all[0])
    P.wait("sync", "SPW1", spwd_all[1])
    for i in range(ni):
        dma("sync", wdT[i][:, :], wdT_dram[i * 128:(i + 1) * 128, :], "LWDT")
    lwdt_all = P.now("LWDT")
    gb2 = barrier_end(2, cb2)
    P.wait("vector", "GBR", gb2)
    v_op(lambda e: e.tensor_scalar(sc(S_T0), sc(S_S3), 1.0 / 65536.0, None, ALU.mult))
    sep("vector")
    v_op(lambda e: e.tensor_tensor(sc(S_T1), sc(0), sc(S_T0), ALU.mult))
    sep("vector")
    v_op(lambda e: e.tensor_scalar(sc(S_S4), sc(S_T1), 1.0 / QMAX, GUARD, ALU.mult, ALU.max))
    sep("vector")
    v_op(lambda e: e.reciprocal(sc(S_T1), sc(S_S4)))
    sep("vector")
    v_op(lambda e: e.tensor_tensor(sc(S_C4), sc(S_T1), sc(S_T0), ALU.mult))
    v_sc2 = v_mark(drain=True)
    v_b4 = bcast_scalar(S_C4, B_C4, "V", v_sc2)

    P.wait("vector", "V", v_b4)
    P.wait("vector", "G", P.now("G"))
    htk = tok // 2
    c_done = {}
    gc = [gi16, gi16b]
    vc_m = {}
    gc_m = {}
    for m in range(ni + 2):
        p = m % 2
        if m < ni:
            if m + 1 < ni:
                if m >= 1:
                    P.wait("sync", "V", vc_m[m - 1])
                arrC[m + 1] = dma("sync", spill_t[(m + 1) % 2].ap()[:, :],
                                  prod_dram[(m + 1) * 128:(m + 2) * 128, :], ("LC0", "LC1")[(m + 1) % 2])
            P.wait("vector", ("LC0", "LC1")[p], arrC[m])
            if m >= 2:
                P.wait("vector", ("SC0", "SC1")[p], P.now(("SC0", "SC1")[p]))
                P.wait("vector", "G", gc_m[m - 2])
            v_op(lambda e, m=m, p=p: e.tensor_scalar(gc[p].ap()[:, :], spill_t[p].ap()[:, :],
                                                     bcol(B_C4), None, ALU.mult))
            vc_m[m] = v_mark()
        mm2 = m - 2
        if mm2 >= 0:
            q2 = mm2 % 2
            P.wait("vector", "G", gc_m[mm2])
            v_op(lambda e, q2=q2, mm2=mm2: e.tensor_reduce(mxcols.ap()[:, mm2:mm2 + 1],
                                                           of32.ap()[:, q2 * tok:q2 * tok + tok],
                                                           AX.X, ALU.max, apply_absolute_value=True))
            c_done[q2] = v_mark()
        if m < ni:
            P.wait("sync", "V", vc_m[m])
            dma("sync", g2_dram[m * 128:(m + 1) * 128, :], gc[p].ap()[:, :], ("SC0", "SC1")[p])
            P.wait("gpsimd", "V", vc_m[m])
            if m >= 2:
                P.wait("gpsimd", "V", c_done[p])
            g_op(lambda e, m=m, p=p: e.tensor_tensor(of32.ap()[:, p * tok:p * tok + htk],
                                                     gc[p].ap()[:, 0:htk],
                                                     upq[m].ap()[:, 0:htk], ALU.mult))
            gc_m[m] = g_mark()
            v_op(lambda e, m=m, p=p: e.tensor_tensor(of32.ap()[:, p * tok + htk:p * tok + tok],
                                                     gc[p].ap()[:, htk:tok],
                                                     upq[m].ap()[:, htk:tok], ALU.mult))
    v_passC = P.now("V")
    scc_all = (P.now("SC0"), P.now("SC1"))

    vf3 = fold_cols(0, ni, 0, v_passC)
    cb3 = barrier_start(3, vf3)
    P.wait("sync", "SC0", scc_all[0])
    P.wait("sync", "SC1", scc_all[1])
    P.wait("sync", "G", P.now("G"))
    arrD = {0: dma("sync", gi16.ap()[:, :], g2_dram[0:128, :], "LP0")}
    gb3 = barrier_end(3, cb3)
    P.wait("vector", "GBR", gb3)
    v_op(lambda e: e.tensor_tensor(sc(S_T0), sc(S_S4), sc(S_S2), ALU.mult))
    sep("vector")
    v_op(lambda e: e.tensor_tensor(sc(S_T1), sc(0), sc(S_T0), ALU.mult))
    sep("vector")
    v_op(lambda e: e.tensor_scalar(sc(S_S5), sc(S_T1), 1.0 / QMAX, GUARD, ALU.mult, ALU.max))
    sep("vector")
    v_op(lambda e: e.reciprocal(sc(S_T1), sc(S_S5)))
    sep("vector")
    v_op(lambda e: e.tensor_tensor(sc(S_C5), sc(S_T1), sc(S_T0), ALU.mult))
    v_sc3 = v_mark(drain=True)
    v_b5 = bcast_scalar(S_C5, B_C5, "V", v_sc3)
    v_b6 = bcast_scalar(S_S5, B_S5, "V", v_b5)

    P.wait("vector", "V", v_b6)
    P.wait("gpsimd", "V", v_b6)
    P.wait("tensor", "LWDT", lwdt_all)
    htk2 = tok // 2
    w2d = min(512, htk2)
    ns2 = max(1, htk2 // 512)
    d_done = {}
    sD_m = {}
    td_p0 = None
    gslot = [gi16.ap()[:, :], gi16b.ap()[:, :]]
    gd_m = {}
    for m in range(ni + 1):
        p = m % 2
        if m < ni:
            if m + 1 < ni:
                if m >= 1:
                    P.wait("sync", "G", gd_m[m - 1])
                    if m >= 2:
                        P.wait("sync", "V", d_done[m % 2])
                arrD[m + 1] = dma("sync", gslot[(m + 1) % 2], g2_dram[(m + 1) * 128:(m + 2) * 128, :],
                                  ("LP0", "LP1")[(m + 1) % 2])
            P.wait("gpsimd", ("LP0", "LP1")[p], arrD[m])
            if m >= 2:
                P.wait("gpsimd", "V", d_done[p])
            g_op(lambda e, m=m, p=p: e.tensor_tensor(of32.ap()[:, p * tok:p * tok + htk],
                                                     gslot[p][:, 0:htk],
                                                     upq[m].ap()[:, 0:htk], ALU.mult))
            gd_m[m] = g_mark()
            P.wait("vector", ("LP0", "LP1")[p], arrD[m])
            v_op(lambda e, m=m, p=p: e.tensor_tensor(of32.ap()[:, p * tok + htk:p * tok + tok],
                                                     gslot[p][:, htk:tok],
                                                     upq[m].ap()[:, htk:tok], ALU.mult))
        mm1 = m - 1
        if mm1 >= 0:
            q = mm1 % 2
            P.wait("vector", "G", gd_m[mm1])
            if mm1 >= 2:
                P.wait("vector", "S", sD_m[mm1 - 2])
            v_op(lambda e, q=q: e.tensor_scalar(sg32[q], of32.ap()[:, q * tok:q * tok + tok],
                                                bcol(B_C5), None, ALU.mult))
            d_done[q] = v_mark()
            P.wait("scalar", "V", d_done[q])
            s_op(lambda e, q=q, mm1=mm1: e.activation(o_bf[mm1][:, :], sg32[q], AF.Copy))
            sD_m[mm1] = s_mark()
            P.wait("tensor", "S", sD_m[mm1])
            for j in range(2):
                lhs0 = wdT[mm1][:, j * 128:(j + 1) * 128]
                for s2 in range(ns2):
                    mm(PA[:, j * htk2 + s2 * w2d: j * htk2 + (s2 + 1) * w2d], lhs0,
                       o_bf[mm1][:, s2 * w2d:(s2 + 1) * w2d],
                       start=(mm1 == 0), stop=(mm1 == ni - 1))
            if mm1 == ni - 1:
                td_p0 = t_mark()
    v_passD = P.now("V")
    s_passD = P.now("S")

    npair = nk // 2
    npass = 2 * npair
    dn_evac = {}
    td_pass = {0: td_p0}
    xr_arr = {}
    P.wait("sync", "V", v_passD)

    def xreload(pp):
        oc0 = (pp % npair) * 2
        th = pp // npair
        for j in range(2):
            oc = oc0 + j
            xr_arr[pp] = dma("sync", of32.ap()[:, tok + j * htk2: tok + (j + 1) * htk2],
                             xres_dram[oc * 128:(oc + 1) * 128, th * htk2:(th + 1) * htk2],
                             "LXR")

    xreload(0)
    for pp in range(npass):
        oc0 = (pp % npair) * 2
        th = pp // npair
        pg = pages[pp % 2]
        if pp > 0:
            if pp >= 2:
                P.wait("tensor", *dn_evac[pp % 2])
            for i in range(ni):
                for j in range(2):
                    lhs = wdT[i][:, (oc0 + j) * 128:(oc0 + j + 1) * 128]
                    for s2 in range(ns2):
                        mm(pg[:, j * htk2 + s2 * w2d: j * htk2 + (s2 + 1) * w2d], lhs,
                           o_bf[i][:, th * htk2 + s2 * w2d: th * htk2 + (s2 + 1) * w2d],
                           start=(i == 0), stop=(i == ni - 1))
            td_pass[pp] = t_mark()
        P.wait("scalar", "T", td_pass[pp])
        if pp >= 1:
            P.wait("scalar", "SOUT", P.now("SOUT"))
        s_op(lambda e, pg=pg: e.activation(of32.ap()[:, 0:tok], pg[:, 0:tok], AF.Copy,
                                           scale=bcol(B_S5)))
        sd = s_mark()
        P.wait("vector", "S", sd)
        P.wait("vector", "LXR", xr_arr[pp])
        v_op(lambda e: e.tensor_tensor(of32.ap()[:, 0:tok], of32.ap()[:, 0:tok],
                                       of32.ap()[:, tok:2 * tok], ALU.add))
        vd = v_mark()
        dn_evac[pp % 2] = ("V", vd)
        P.wait("sync", "V", vd)
        if pp + 1 < npass:
            xreload(pp + 1)
        for j in range(2):
            oc = oc0 + j
            dma("sync", outT[oc * 128:(oc + 1) * 128, th * htk2:(th + 1) * htk2],
                of32.ap()[:, j * htk2:(j + 1) * htk2], "SOUT")
    if debug:
        d_obf = nc.dram_tensor("d_obf", [di, tok], BF16, kind="ExternalOutput")
        for i in range(ni):
            dma("sync", d_obf.ap()[i * 128:(i + 1) * 128, :], o_bf[i][:, :], "SOUT")
        dbg = {"d_xres": xres_dram, "d_up": up_dram, "d_gate": gate_dram,
               "d_prod": prod_dram, "d_g2": g2_dram, "d_wdT": wdT_dram,
               "d_kv": kv_cc_outB}
        for nm, t in dbg.items():
            o = nc.dram_tensor(nm, list(t.shape), t.dtype, kind="ExternalOutput")
            dma("sync", o.ap(), t.ap(), "SOUT")
        o1 = nc.dram_tensor("d_scal", [1, 64], F32, kind="ExternalOutput")
        dma("sync", o1.ap(), scal.ap()[0:1, :], "SOUT")
        o2 = nc.dram_tensor("d_bc", [128, 16], F32, kind="ExternalOutput")
        dma("sync", o2.ap(), bc.ap()[:, :], "SOUT")
    P.wait("sync", "SOUT", P.now("SOUT"))
    return P


def build(nc, d=D, tok=TOK, di=DI, debug=False):
    global SEMS
    SEMS = {}
    import contextlib
    with contextlib.ExitStack() as es:
        for s in SEM_NAMES:
            SEMS[s] = es.enter_context(nc.semaphore(s))
        P = build_program(nc, d, tok, di, debug)
        with nc.Block() as block:
            for eng_name in Plan.ENGINES:
                deco = getattr(block, eng_name)

                def make(fns):
                    def body(e):
                        for f in fns:
                            f(e)
                    return body
                deco(make(P.q[eng_name]))
    return nc


def kernel(L0_attn_x, L0_attn_W, L0_ffn_w_gate, L0_ffn_w_up, L0_ffn_w_down):
    x = np.ascontiguousarray(L0_attn_x, dtype=np.float32)
    W = np.ascontiguousarray(L0_attn_W, dtype=np.float32)
    wg = np.ascontiguousarray(L0_ffn_w_gate, dtype=np.float32)
    wu = np.ascontiguousarray(L0_ffn_w_up, dtype=np.float32)
    wd = np.ascontiguousarray(L0_ffn_w_down, dtype=np.float32)
    ident = np.eye(128, dtype=np.float32)

    nc = bass.Bass("TRN2", target_bir_lowering=False, detect_race_conditions=False)
    build(nc)

    in_maps = []
    for c in range(N_CORES):
        b, h = c >> 1, c & 1
        xTc = np.ascontiguousarray(x[b, h * TOK:(h + 1) * TOK, :].T)
        in_maps.append({"xT": xTc, "W": W, "wg": wg, "wu": wu, "wd": wd, "ident": ident})
    res = run_bass_kernel_spmd(nc, in_maps, core_ids=list(range(N_CORES)))
    out = np.empty((4, 4096, D), dtype=np.float32)
    for c in range(N_CORES):
        b, h = c >> 1, c & 1
        out[b, h * TOK:(h + 1) * TOK, :] = res.results[c]["outT"].T
    return out


# version 35
# speedup vs baseline: 1.0723x; 1.0298x over previous
import numpy as np
import concourse.bass as bass
import concourse.mybir as mybir
from concourse.bass_utils import run_bass_kernel_spmd

F32 = mybir.dt.float32
BF16 = mybir.dt.bfloat16
I32 = mybir.dt.int32
I16 = mybir.dt.int16
I8 = mybir.dt.int8
AF = mybir.ActivationFunctionType
ALU = mybir.AluOpType
AX = mybir.AxisListType

D = 1024
TOK = 2048
DI = 3072
BLK = 32
N_CORES = 8

LN_EPS = 1e-5
ATT_EPS = 1e-6
QMAX = 32767.0
GUARD = 1e-12

S_S1, S_C1, S_S2, S_F2, S_S3, S_F3, S_S4, S_C4, S_S5, S_C5, S_T0, S_T1 = range(16, 28)
S_MHALF = 30
S_MONE = 31
B_C1, B_F2, B_F3, B_S3, B_C4, B_C5, B_S5 = range(7)
B_UP, B_GATE, B_TMP, B_RUN = 10, 11, 12, 15

SEM_NAMES = ["T", "V", "S", "CC", "G",
             "LX", "GW", "KV1", "KV2", "KV3", "LX2", "SXR", "LR0", "LR1", "WR0", "WR1",
             "SPWD", "SPW1", "LWDT", "SA0", "SA1", "LB0", "LB1", "SB0", "SB1", "SC0", "SC1", "WR2",
             "LC0", "LC1", "LP0", "LP1", "LXR", "SOUT", "LBR", "GBR", "LW0", "LW1", "LW2", "LW3"]


class Plan:
    ENGINES = ("sync", "gpsimd", "tensor", "vector", "scalar")

    def __init__(self):
        self.q = {e: [] for e in self.ENGINES}
        self.cnt = {s: 0 for s in SEM_NAMES}
        self._last_wait = {e: False for e in self.ENGINES}

    def inc(self, name, k):
        self.cnt[name] += k
        return self.cnt[name]

    def now(self, name):
        return self.cnt[name]

    def emit(self, eng, fn):
        self.q[eng].append(fn)
        self._last_wait[eng] = False

    def wait(self, eng, sem_name, val=None):
        v = self.cnt[sem_name] if val is None else val
        if v > 0:
            if self._last_wait[eng]:
                self.q[eng].append(lambda e: e.nop())
            self.q[eng].append(lambda e, s=sem_name, vv=v: e.wait_ge(SEMS[s], vv))
            self._last_wait[eng] = True


SEMS = {}


class SbufMap:
    BASE = 16640
    TOP = 229360
    LIMIT = TOP - BASE

    def __init__(self, nc):
        self.nc = nc
        self.items = []

    def alloc(self, name, shape, dtype, ph0, ph1, align=64):
        esz = {F32: 4, BF16: 2, I32: 4, I16: 2, I8: 1}[dtype]
        nbytes = int(np.prod(shape[1:])) * esz
        nbytes_al = (nbytes + align - 1) // align * align
        taken = sorted((it[1], it[1] + it[2]) for it in self.items
                       if not (it[4] < ph0 or it[3] > ph1))
        merged = []
        for s, e in taken:
            if merged and s <= merged[-1][1]:
                merged[-1][1] = max(merged[-1][1], e)
            else:
                merged.append([s, e])
        holes = []
        prev = 0
        for s, e in merged:
            if s - prev >= nbytes_al:
                holes.append((prev, s - prev))
            prev = max(prev, e)
        if self.LIMIT - prev >= nbytes_al:
            holes.append((prev, self.LIMIT - prev))
        assert holes, f"SBUF overflow: {name} needs {nbytes_al} (no hole)"
        off = min(holes, key=lambda h: h[1])[0]
        self.items.append((name, off, nbytes_al, ph0, ph1))
        return self.nc.alloc_sbuf_tensor_at(name, list(shape), dtype, offset=self.BASE + off)

    def alloc_chunks(self, name, n, shape, dtype, ph0, ph1, align=64):
        return [self.alloc(f"{name}{i}", shape, dtype, ph0, ph1, align) for i in range(n)]

    def reserve(self, name, off, nbytes, ph0, ph1):
        self.items.append((name, off, nbytes, ph0, ph1))


def build_program(nc, d=D, tok=TOK, di=DI, debug=False):
    nk = d // 128
    ni = di // 128
    ntc = tok // 128
    ns = tok // 512
    nblk_d = d // BLK
    nblk_i = di // BLK

    xT = nc.dram_tensor("xT", [d, tok], F32, kind="ExternalInput").ap()
    W_h = nc.dram_tensor("W", [d, 3 * d], F32, kind="ExternalInput").ap()
    wg_h = nc.dram_tensor("wg", [di, d], F32, kind="ExternalInput").ap()
    wu_h = nc.dram_tensor("wu", [di, d], F32, kind="ExternalInput").ap()
    wd_h = nc.dram_tensor("wd", [d, di], F32, kind="ExternalInput").ap()
    ident_h = nc.dram_tensor("ident", [128, 128], F32, kind="ExternalInput").ap()
    outT = nc.dram_tensor("outT", [d, tok], F32, kind="ExternalOutput").ap()

    up_dram = nc.dram_tensor("up_dram", [di, tok], BF16)
    gate_dram = nc.dram_tensor("gate_dram", [di, tok], BF16)
    prod_dram = nc.dram_tensor("prod_dram", [di, tok], BF16)
    g2_dram = nc.dram_tensor("g2_dram", [di, tok], I16)
    xres_dram = nc.dram_tensor("xres_dram", [d, tok], F32)
    wdT_dram = nc.dram_tensor("wdT_dram", [di, d], BF16)
    wuT_dram = nc.dram_tensor("wuT_dram", [di, d], BF16)
    wgT_dram = nc.dram_tensor("wgT_dram", [di, d], BF16)
    nh = nk // 2
    hw_kv = 16 + nh * d
    kv_cc_inA = nc.dram_tensor("kv_cc_inA", [128, hw_kv], F32)
    kv_cc_outA = nc.dram_tensor("kv_cc_outA", [128, hw_kv], F32)
    kv_cc_inB = nc.dram_tensor("kv_cc_inB", [128, hw_kv], F32)
    kv_cc_outB = nc.dram_tensor("kv_cc_outB", [128, hw_kv], F32)
    sc_cc_in = [nc.dram_tensor(f"sc_cc_in{i}", [1, 16], F32) for i in range(4)]
    sc_cc_out = [nc.dram_tensor(f"sc_cc_out{i}", [1, 16], F32) for i in range(4)]
    cc_w_in = nc.dram_tensor("cc_w_in", [1, 16], F32)
    cc_w_out = nc.dram_tensor("cc_w_out", [1, 16], F32)

    sm = SbufMap(nc)
    ident_f = sm.alloc("ident_f", [128, 128], F32, 0, 119)
    ident_b = sm.alloc("ident_b", [128, 128], BF16, 0, 119)
    ones_b = sm.alloc("ones_b", [128, 1], BF16, 0, 119)
    onerow_f = sm.alloc("onerow_f", [1, 128], F32, 0, 119)
    row0 = sm.alloc("row0", [1, tok], F32, 0, 79)
    row1a = sm.alloc("row1a", [1, tok], F32, 10, 29)
    row1b = sm.alloc("row1b", [1, tok], F32, 60, 69)
    scal = sm.alloc("scal", [1, 64], F32, 0, 119)
    bc = sm.alloc("bc", [128, 16], F32, 0, 119)
    mxcols = sm.alloc("mxcols", [128, 64], F32, 0, 119)

    W_bf = sm.alloc("W_bf", [128, nk * 3 * d], BF16, 0, 29)
    ttmp = sm.alloc("ttmp", [128, tok], F32, 10, 29)
    ttmp6 = sm.alloc("ttmp6", [128, tok], F32, 60, 69)
    ln1x = sm.alloc_chunks("ln1x", nk, [128, tok], BF16, 10, 29)
    qe = sm.alloc_chunks("qe", nk, [128, tok], BF16, 20, 59)
    kf = sm.alloc_chunks("kf", nk, [128, tok], BF16, 20, 31)
    vf = sm.alloc_chunks("vf", nk, [128, tok], BF16, 20, 33)
    x_f = sm.alloc_chunks("x_f", nk, [128, tok], F32, 10, 19)
    xbf_t = sm.alloc("xbf_t", [128, 4 * tok], BF16, 10, 19)
    keT = sm.alloc_chunks("keT", ntc, [128, d], BF16, 30, 49)
    vT = sm.alloc_chunks("vT", ntc, [128, d], BF16, 34, 49)
    kvt_f = sm.alloc("kvt_f", [128, nk * d + 32], F32, 40, 49)
    kvtks = sm.alloc("kvtks", [128, nk * d + 32], BF16, 50, 59)
    kvtks_off = sm.items[-1][1]
    zbc = sm.alloc("zbc", [128, tok], BF16, 50, 59)
    xres_t = sm.alloc("xres_t", [128, tok], F32, 50, 59)
    xres_t2 = sm.alloc("xres_t2", [128, tok], F32, 50, 59)
    xresf = sm.alloc_chunks("xresf", nk, [128, tok], F32, 50, 69)
    wst_overlay = 8 * nk * 128 * 2 <= (nk * d + 32) * 2
    if wst_overlay:
        sm.reserve("wst_ovl", kvtks_off, 8 * nk * 256, 60, 89)
        wst_u = [nc.alloc_sbuf_tensor_at(f"wst_u{i}", [128, nk * 128], BF16,
                                         offset=SbufMap.BASE + kvtks_off + i * nk * 256)
                 for i in range(4)]
        wst_g = [nc.alloc_sbuf_tensor_at(f"wst_g{i}", [128, nk * 128], BF16,
                                         offset=SbufMap.BASE + kvtks_off + (4 + i) * nk * 256)
                 for i in range(4)]
    else:
        wst_u = sm.alloc_chunks("wst_u", 4, [128, nk * 128], BF16, 80, 89)
        wst_g = sm.alloc_chunks("wst_g", 4, [128, nk * 128], BF16, 80, 89)
    wraw_s = sm.alloc("wraw_s", [128, 6 * d], F32, 30, 79)
    wqf_s = None
    wi8_s = sm.alloc("wi8_s", [128, 4 * d], I8, 50, 79)
    wdq_s = sm.alloc("wdq_s", [128, 4 * d], BF16, 50, 79)
    wsc_s = sm.alloc("wsc_s", [128, 12 * nblk_d], F32, 50, 79)
    xbf_t2 = sm.alloc("xbf_t2", [128, 4 * tok], BF16, 60, 69)
    ln2x = sm.alloc_chunks("ln2x", nk, [128, tok], BF16, 60, 79)
    xq = sm.alloc_chunks("xq", nk, [128, tok], BF16, 70, 89)
    xq16 = sm.alloc("xq16", [128, tok], I16, 70, 79)
    xq16b = sm.alloc("xq16b", [128, tok], I16, 70, 79)
    spill_t = sm.alloc_chunks("spill_t", 4, [128, tok], BF16, 80, 119)
    upq = sm.alloc_chunks("upq", ni, [128, tok], I16, 80, 119)
    gi16 = sm.alloc("gi16", [128, tok], I16, 90, 119)
    gi16b = sm.alloc("gi16b", [128, tok], I16, 90, 119)
    gi16c = sm.alloc("gi16c", [128, tok], I16, 90, 99)
    sig32 = sm.alloc("sig32", [128, tok], I32, 90, 119)
    sig32b = sm.alloc("sig32b", [128, tok], I32, 90, 119)
    sigf = sm.alloc("sigf", [128, tok], F32, 90, 99)
    sigfb = sm.alloc("sigfb", [128, tok], F32, 90, 99)
    prodt = sm.alloc_chunks("prodt", 2, [128, tok], BF16, 90, 99)
    wdT = sm.alloc_chunks("wdT", ni, [128, d], BF16, 100, 119)
    of32 = sm.alloc("of32", [128, 2 * tok], F32, 100, 119)

    o_bf = [upq[i].ap().bitcast(BF16) for i in range(ni)]

    PA = nc.alloc_psum_tensor("PA", [128, 2048], F32)
    PB = nc.alloc_psum_tensor("PB", [128, 2048], F32)
    pages = [PA, PB]

    P = Plan()

    def dma(eng_name, out_ap, in_ap, sem_name):
        v = P.inc(sem_name, 16)
        P.emit(eng_name, lambda e, o=out_ap, i=in_ap, s=sem_name:
               e.dma_start(out=o, in_=i).then_inc(SEMS[s], 16))
        return v

    def mm(out_ap, lhsT_ap, rhs_ap, start, stop):
        P.emit("tensor", lambda e, o=out_ap, l=lhsT_ap, r=rhs_ap, st=start, sp=stop:
               e.matmul(o, l, r, start=st, stop=sp))

    def sep(eng):
        P.emit(eng, lambda e: e.drain())

    _tagged = set()

    def _tag(eng, sem):
        key = (eng, len(P.q[eng]))
        assert key not in _tagged, f"double-tag on {eng} at {len(P.q[eng])}"
        _tagged.add(key)
        v = P.inc(sem, 1)
        fn = P.q[eng].pop()
        P.emit(eng, lambda e, f=fn, s=sem: f(e).then_inc(SEMS[s], 1))
        return v

    def t_mark():
        return _tag("tensor", "T")

    def v_op(fn):
        P.emit("vector", fn)

    def v_mark(drain=False):
        if drain:
            v = P.inc("V", 1)
            P.emit("vector", lambda e: e.drain().then_inc(SEMS["V"], 1))
            return v
        return _tag("vector", "V")

    def g_op(fn):
        P.emit("gpsimd", fn)

    def g_mark(drain=False):
        if drain:
            v = P.inc("G", 1)
            P.emit("gpsimd", lambda e: e.drain().then_inc(SEMS["G"], 1))
            return v
        return _tag("gpsimd", "G")

    def s_op(fn):
        P.emit("scalar", fn)

    def s_mark(drain=False):
        if drain:
            v = P.inc("S", 1)
            P.emit("scalar", lambda e: e.drain().then_inc(SEMS["S"], 1))
            return v
        return _tag("scalar", "S")

    R0 = row0.ap()[0:1, 0:tok]

    def sc(i):
        return scal.ap()[0:1, i:i + 1]

    def bcol(i):
        return bc.ap()[:, i:i + 1]

    dma("sync", ident_f[:, :], ident_h, "LX")
    P.wait("sync", "V", 1)
    ccw = dma("sync", cc_w_in.ap(), scal.ap()[0:1, 0:16], "LBR")
    P.wait("gpsimd", "LBR", ccw)
    P.emit("gpsimd", lambda e: e.collective_compute(
        "AllReduce", ALU.max, replica_groups=[[0, 1, 2, 3, 4, 5, 6, 7]],
        ins=[cc_w_in.ap()], outs=[cc_w_out.ap()]).then_inc(SEMS["CC"], 1))
    P.inc("CC", 1)
    P.emit("gpsimd", lambda e: e.collective_compute(
        "AllReduce", ALU.add, replica_groups=[[0, 1], [2, 3], [4, 5], [6, 7]],
        ins=[cc_w_in.ap()], outs=[cc_w_out.ap()]).then_inc(SEMS["CC"], 1))
    P.inc("CC", 1)
    dma("gpsimd", ident_b[:, :], ident_h, "GW")
    v_op(lambda e: e.memset(ones_b[:, :], 1.0))
    v_op(lambda e: e.memset(onerow_f[:, :], 1.0))
    v_op(lambda e: e.memset(scal.ap()[0:1, :], 0.0))
    v_op(lambda e: e.memset(sc(S_MHALF), -0.5))
    v_op(lambda e: e.memset(sc(S_MONE), -1.0))
    v_const = v_mark(drain=True)
    for k in range(nk):
        dma("gpsimd", W_bf[:, k * 3 * d:(k + 1) * 3 * d], W_h[k * 128:(k + 1) * 128, :], "GW")
    gw_all = P.now("GW")
    for k in range(nk // 2):
        dma("sync", x_f[k][:, :], xT[k * 128:(k + 1) * 128, :], "LX")
    lx_lo = P.now("LX")
    for k in range(nk // 2, nk):
        dma("scalar", x_f[k][:, :], xT[k * 128:(k + 1) * 128, :], "LX2")
    lx_hi = P.now("LX2")

    def ln_block(src, arrive, consumed, xbf_buf, R1):
        tks = {}
        for k in range(nk):
            xo = (k % 2) * tok
            so = 2 * tok + (k % 2) * tok
            arrive(k, "vector")
            if k >= 2:
                P.wait("vector", "T", tks[k - 2])
            v_op(lambda e, a=src(k), xb=xbf_buf, xo=xo: e.tensor_copy(xb.ap()[:, xo:xo + tok], a))
            vv = v_mark()
            arrive(k, "scalar")
            if k >= 2:
                P.wait("scalar", "T", tks[k - 2])
            s_op(lambda e, a=src(k), xb=xbf_buf, so=so: e.activation(xb.ap()[:, so:so + tok], a,
                                                                    AF.Square))
            sv = s_mark()
            P.wait("tensor", "V", vv)
            P.wait("tensor", "S", sv)
            for s in range(ns):
                mm(PA[0:1, s * 512:(s + 1) * 512], ones_b[:, 0:1],
                   xbf_buf.ap()[:, xo + s * 512: xo + (s + 1) * 512],
                   start=(k == 0), stop=(k == nk - 1))
                mm(PB[0:1, s * 512:(s + 1) * 512], ones_b[:, 0:1],
                   xbf_buf.ap()[:, so + s * 512: so + (s + 1) * 512],
                   start=(k == 0), stop=(k == nk - 1))
            tks[k] = t_mark()
            consumed(k, tks[k])
        tstat = P.now("T")
        P.wait("vector", "T", tstat)
        v_op(lambda e: e.tensor_scalar(R0, PA[0:1, 0:tok], -1.0 / d, None, ALU.mult))
        v_op(lambda e: e.tensor_scalar(R1, PB[0:1, 0:tok], 1.0 / d, None, ALU.mult))
        v_op(lambda e: e.tensor_tensor(PA[0:1, 0:tok], R0, R0, ALU.mult))
        v_op(lambda e: e.tensor_tensor(R1, R1, PA[0:1, 0:tok], ALU.subtract))
        v_op(lambda e: e.tensor_scalar(R1, R1, LN_EPS, None, ALU.add))
        vrec = v_mark(drain=True)
        P.wait("scalar", "V", vrec)
        s_op(lambda e: e.activation(R1, R1, AF.Ln))
        s_op(lambda e: e.activation(R1, R1, AF.Exp, scale=sc(S_MHALF)))
        ssq = s_mark(drain=True)
        P.wait("vector", "S", ssq)
        v_op(lambda e: e.tensor_tensor(R0, R0, R1, ALU.mult))
        vrows = v_mark(drain=True)
        P.wait("tensor", "V", vrows)
        for s in range(ns):
            mm(PA[:, s * 512:(s + 1) * 512], onerow_f[0:1, :], R1[:, s * 512:(s + 1) * 512], True, True)
            mm(PB[:, s * 512:(s + 1) * 512], onerow_f[0:1, :], R0[:, s * 512:(s + 1) * 512], True, True)
        return t_mark()

    P.wait("vector", "V", v_const)

    def arr1(k, eng):
        if k < nk // 2:
            P.wait(eng, "LX", lx_lo)
        else:
            P.wait(eng, "LX2", lx_hi)

    tb1 = ln_block(lambda k: x_f[k].ap()[:, :], arr1, lambda k, tk: None, xbf_t, row1a.ap()[0:1, :])
    P.wait("vector", "T", tb1)
    for k in range(nk):
        xk = x_f[k].ap()[:, :]
        ok = ln1x[k].ap()[:, :]
        v_op(lambda e, xk=xk: e.tensor_tensor(ttmp[:, 0:tok], xk, PA[:, 0:tok], ALU.mult))
        v_op(lambda e, ok=ok: e.tensor_tensor(ok, ttmp[:, 0:tok], PB[:, 0:tok], ALU.add))
    v_ln1 = v_mark()

    P.wait("tensor", "V", v_ln1)
    P.wait("tensor", "GW", gw_all)
    evac_done = {}
    morder = list(range(nk, 2 * nk)) + list(range(2 * nk, 3 * nk)) + list(range(nk))
    for idx, m in enumerate(morder):
        pg = pages[idx % 2]
        if idx >= 2:
            P.wait("tensor", *evac_done[idx % 2])
        for k in range(nk):
            lhs = W_bf.ap()[:, k * 3 * d + m * 128: k * 3 * d + (m + 1) * 128]
            for s in range(ns):
                mm(pg[:, s * 512:(s + 1) * 512], lhs,
                   ln1x[k].ap()[:, s * 512:(s + 1) * 512],
                   start=(k == 0), stop=(k == nk - 1))
        tq = t_mark()
        if m < 2 * nk:
            dstbuf = qe if m < nk else kf
            dst = dstbuf[m % nk].ap()[:, :]
            P.wait("vector", "T", tq)
            v_op(lambda e, pg=pg: e.tensor_scalar(ttmp[:, 0:tok], pg[:, 0:tok], 0.0, None, ALU.min))
            vv = v_mark()
            P.wait("scalar", "V", vv)
            s_op(lambda e: e.activation(ttmp[:, 0:tok], ttmp[:, 0:tok], AF.Exp))
            sv = s_mark()
            P.wait("vector", "S", sv)
            v_op(lambda e, pg=pg, dst=dst: e.scalar_tensor_tensor(
                dst, pg[:, 0:tok], 0.0, ttmp[:, 0:tok], ALU.max, ALU.add))
            evac_done[idx % 2] = ("V", v_mark())
        else:
            dst = vf[m - 2 * nk].ap()[:, :]
            P.wait("scalar", "T", tq)
            s_op(lambda e, pg=pg, dst=dst: e.copy(dst, pg[:, 0:tok]))
            evac_done[idx % 2] = ("S", s_mark())
    P.wait("tensor", *evac_done[0])
    P.wait("tensor", *evac_done[1])

    regions3 = [(PA, 0), (PA, 1024), (PB, 0), (PB, 1024)]
    tr_evac = {}
    ri = 0
    for src, dst in ((kf, keT), (vf, vT)):
        for c in range(ntc):
            pg, off = regions3[ri % 4]
            if ri >= 4:
                P.wait("tensor", *tr_evac[ri % 4])
            for k in range(nk):
                mm(pg[:, off + k * 128: off + (k + 1) * 128],
                   src[k].ap()[:, c * 128:(c + 1) * 128],
                   ident_b[:, :], True, True)
            tt = t_mark()
            P.wait("scalar", "T", tt)
            s_op(lambda e, pg=pg, off=off, dst=dst, c=c:
                 e.copy(dst[c].ap()[:, :], pg[:, off:off + d]))
            tr_evac[ri % 4] = ("S", s_mark())
            ri += 1
    for r in range(min(4, ri)):
        P.wait("tensor", *tr_evac[r])
    wu_entry = (P.now("S"), P.now("G"), P.now("V"))
    wu_entry_t = P.now("T")

    regions4 = [(PA, 0), (PA, 1024), (PB, 0)]
    kvt_evac = {}

    def kvoff(dk):
        return 16 + dk * d + (16 if dk >= nh else 0)

    def kscol(dk):
        return dk if dk < nh else 16 + nh * d + (dk - nh)

    v_half = {}
    for dk in range(nk):
        pg, off = regions4[dk % 3]
        if dk >= 3:
            P.wait("tensor", *kvt_evac[dk % 3])
        for c in range(ntc):
            lhs = keT[c].ap()[:, dk * 128:(dk + 1) * 128]
            w2 = min(512, d)
            for s2 in range(max(1, d // 512)):
                mm(pg[:, off + s2 * w2: off + (s2 + 1) * w2], lhs,
                   vT[c].ap()[:, s2 * w2:(s2 + 1) * w2],
                   start=(c == 0), stop=(c == ntc - 1))
            kcol = 2032 + dk if dk < nh else 1024 + (dk - nh)
            mm(PB[:, kcol: kcol + 1], lhs, ones_b[:, 0:1],
               start=(c == 0), stop=(c == ntc - 1))
        tkv = t_mark()
        P.wait("vector", "T", tkv)
        v_op(lambda e, pg=pg, off=off, dk=dk:
             e.tensor_copy(kvt_f.ap()[:, kvoff(dk):kvoff(dk) + d], pg[:, off:off + d]))
        kvt_evac[dk % 3] = ("V", v_mark())
        if dk == nh - 1 or dk == nk - 1:
            ksb = 0 if dk < nh else 16 + nh * d
            dk0 = 0 if dk < nh else nh
            kc0 = 2032 if dk < nh else 1024
            v_op(lambda e, ksb=ksb: e.memset(kvt_f.ap()[:, ksb:ksb + 16], 0.0))
            v_op(lambda e, ksb=ksb, kc0=kc0: e.tensor_copy(
                kvt_f.ap()[:, ksb:ksb + nh], PB[:, kc0:kc0 + nh]))
            v_half[dk >= nh] = v_mark(drain=True)
    v_kvt = v_half[True]

    P.wait("gpsimd", "V", v_half[False])
    kv1a = dma("gpsimd", kv_cc_inA[:, :], kvt_f.ap()[:, 0:hw_kv], "KV1")
    P.wait("gpsimd", "KV1", kv1a)
    P.emit("gpsimd", lambda e: e.collective_compute(
        "AllReduce", ALU.add, replica_groups=[[0, 1], [2, 3], [4, 5], [6, 7]],
        ins=[kv_cc_inA.ap()], outs=[kv_cc_outA.ap()]).then_inc(SEMS["CC"], 1))
    cc_kva = P.inc("CC", 1)
    P.wait("gpsimd", "V", v_half[True])
    kv1b = dma("gpsimd", kv_cc_inB[:, :], kvt_f.ap()[:, hw_kv:2 * hw_kv], "KV1")
    P.wait("gpsimd", "KV1", kv1b)
    P.emit("gpsimd", lambda e: e.collective_compute(
        "AllReduce", ALU.add, replica_groups=[[0, 1], [2, 3], [4, 5], [6, 7]],
        ins=[kv_cc_inB.ap()], outs=[kv_cc_outB.ap()]).then_inc(SEMS["CC"], 1))
    cc_kv = P.inc("CC", 1)

    def weight_prep(src_h, nch, wc, gsz, dstT_sb, dstT_dram, wraw, wi8, wdq, wsc, wqf, wsem, dbuf=True, col0=0, j0ofs=0, mrow=False, entry=None, entry_t=None):
        njt = wc // 128
        groups = [(s, min(gsz, nch - s)) for s in range(0, nch, gsz)]
        ngr = len(groups)
        gw_max = gsz * wc
        eS, eG, eV = entry if entry is not None else (P.now("S"), P.now("G"), P.now("V"))
        if entry_t is not None:
            P.wait("sync", "T", entry_t)
            P.wait("scalar", "T", entry_t)
        P.wait("sync", "S", eS)
        P.wait("sync", "G", eG)
        P.wait("sync", "V", eV)
        P.wait("scalar", "V", eV)
        P.wait("vector", "G", eG)
        arr = {}
        g_marks = {}
        t_marks = {}
        vtt1_marks = {}
        tpose_evac = {}
        s_prev = P.now("S")
        spwd_prev = {0: P.now("SPWD"), 1: P.now("SPW1")}
        s_hist = {}
        spwd_hist = {}

        def load(g):
            half = (g % 3) * gw_max
            m0, gn = groups[g]
            sem = wsem[g % 3]
            if gn == 1:
                h = wc // 2
                dma("sync", wraw.ap()[:, half:half + h],
                    src_h[m0 * 128:(m0 + 1) * 128, col0:col0 + h], sem)
                arr[g] = dma("scalar", wraw.ap()[:, half + h:half + wc],
                             src_h[m0 * 128:(m0 + 1) * 128, col0 + h:col0 + wc], sem)
            else:
                for mi in range(gn):
                    m = m0 + mi
                    arr[g] = dma("sync" if mi == 0 else "scalar",
                                 wraw.ap()[:, half + mi * wc: half + (mi + 1) * wc],
                                 src_h[m * 128:(m + 1) * 128, col0:col0 + wc], sem)

        def sl(g):
            m0, gn = groups[g]
            gw = gn * wc
            nbg = gw // BLK
            half_s = (g % 3) * 2 * (gw_max // BLK)
            svp = wsc.ap()[:, half_s:half_s + nbg]
            isv = wsc.ap()[:, half_s + (gw_max // BLK):half_s + (gw_max // BLK) + nbg]
            buf = wraw.ap()[:, (g % 3) * gw_max:][:, 0:gw]
            return gw, nbg, svp, isv, buf

        load(0)
        if ngr > 1:
            load(1)
        if ngr > 2:
            load(2)
        gw0, nbg0, svp0, isv0, buf0 = sl(0)
        P.wait("vector", wsem[0], arr[0])
        v_op(lambda e, buf=buf0, svp=svp0: e.tensor_reduce(
            svp, buf.rearrange("p (b c) -> p b c", c=BLK), AX.X, ALU.max,
            apply_absolute_value=True))
        sep("vector")
        v_op(lambda e, svp=svp0: e.tensor_scalar(svp, svp, 1.0 / 7.0, GUARD, ALU.mult, ALU.max))
        sep("vector")
        v_op(lambda e, svp=svp0, isv=isv0: e.reciprocal(isv, svp))
        if ngr == 1:
            sep("vector")
        for g in range(ngr):
            m0, gn = groups[g]
            gw, nbg, svp, isv, buf = sl(g)
            if g + 1 < ngr:
                gw1, nbg1, svp1, isv1, buf1 = sl(g + 1)
                P.wait("vector", wsem[(g + 1) % 3], arr[g + 1])
                if g >= 2:
                    P.wait("vector", "G", g_marks[g - 2])
                v_op(lambda e, buf=buf1, svp=svp1: e.tensor_reduce(
                    svp, buf.rearrange("p (b c) -> p b c", c=BLK), AX.X, ALU.max,
                    apply_absolute_value=True))
            else:
                sep("vector")
            v_op(lambda e, buf=buf, isv=isv, nbg=nbg, gw=gw: e.tensor_tensor(
                buf.rearrange("p (b c) -> p b c", c=BLK),
                buf.rearrange("p (b c) -> p b c", c=BLK),
                isv.rearrange("p (b o) -> p b o", o=1).broadcast_to([128, nbg, BLK]),
                ALU.mult))
            vtt1_marks[g] = v_mark()
            if g + 1 < ngr:
                v_op(lambda e, svp=svp1: e.tensor_scalar(svp, svp, 1.0 / 7.0, GUARD,
                                                         ALU.mult, ALU.max))
            w8h = (g % 2) * gw_max
            if g >= 2:
                P.wait("vector", "G", g_marks[g - 2])
            v_op(lambda e, gw=gw, w8h=w8h, buf=buf: e.tensor_scalar(wi8.ap()[:, w8h:w8h + gw], buf,
                                                                    1.0, None, ALU.mult))
            vr8 = v_mark()
            if g + 3 < ngr:
                P.wait("sync", "V", vr8)
                P.wait("scalar", "V", vr8)
                load(g + 3)
            if g + 1 < ngr:
                v_op(lambda e, svp=svp1, isv=isv1: e.reciprocal(isv, svp))
            P.wait("gpsimd", "V", vr8)
            wofs = (g % 2) * gw_max
            if g >= 2:
                P.wait("gpsimd", "T", t_marks[g - 2])
            if dstT_dram is not None:
                P.wait("gpsimd", "S", s_hist[g - 2] if g >= 2 else s_prev)
                P.wait("gpsimd", ("SPWD", "SPW1")[g % 2],
                       spwd_hist[g - 2] if g >= 2 else spwd_prev[g % 2])
            g_op(lambda e, svp=svp, nbg=nbg, gw=gw, w8h=w8h, wofs=wofs: e.tensor_tensor(
                wdq.ap()[:, wofs:wofs + gw].rearrange("p (b c) -> p b c", c=BLK),
                wi8.ap()[:, w8h:w8h + gw].rearrange("p (b c) -> p b c", c=BLK),
                svp.rearrange("p (b o) -> p b o", o=1).broadcast_to([128, nbg, BLK]),
                ALU.mult))
            g_marks[g] = g_mark()
            P.wait("tensor", "G", g_marks[g])
            for mi in range(gn):
                m = m0 + mi
                nbat = (njt + 7) // 8
                for jb in range(nbat):
                    j0, jn = jb * 8, min(8, njt - jb * 8)
                    rr = ((m0 + mi) * nbat + jb) % 4
                    pg, off = regions3[rr]
                    if (m0 + mi) * nbat + jb >= 4:
                        P.wait("tensor", *tpose_evac[rr])
                    for j in range(j0, j0 + jn):
                        mm(pg[:, off + (j - j0) * 128: off + (j - j0 + 1) * 128],
                           wdq.ap()[:, wofs + mi * wc + j * 128: wofs + mi * wc + (j + 1) * 128],
                           ident_b[:, :], True, True)
                    tt = t_mark()
                    P.wait("scalar", "T", tt)
                    if dstT_sb is not None:
                        s_op(lambda e, pg=pg, off=off, m=m, j0=j0, jn=jn:
                             e.copy(dstT_sb[m].ap()[:, j0 * 128:(j0 + jn) * 128],
                                    pg[:, off: off + jn * 128]))
                        svv = s_mark()
                    else:
                        s_op(lambda e, pg=pg, off=off, j0=j0, jn=jn, mi=mi, wc=wc, wofs=wofs:
                             e.copy(wdq.ap()[:, wofs + mi * wc + j0 * 128: wofs + mi * wc + (j0 + jn) * 128],
                                    pg[:, off: off + jn * 128]))
                        svv = s_mark()
                        P.wait("sync", "S", svv)
                        spsem = ("SPWD", "SPW1")[g % 2]
                        if mrow:
                            dma("sync",
                                dstT_dram.ap()[m * 128:(m + 1) * 128,
                                               (j0ofs + j0) * 128:(j0ofs + j0 + jn) * 128],
                                wdq.ap()[:, wofs + mi * wc + j0 * 128: wofs + mi * wc + (j0 + jn) * 128], spsem)
                        else:
                            dma("sync",
                                dstT_dram.ap()[(j0ofs + j0) * 128:(j0ofs + j0 + jn) * 128,
                                               m * 128:(m + 1) * 128].rearrange("(j p) c -> p j c", p=128),
                                wdq.ap()[:, wofs + mi * wc + j0 * 128: wofs + mi * wc + (j0 + jn) * 128]
                                .rearrange("p (j c) -> p j c", c=128), spsem)
                    tpose_evac[rr] = ("S", svv)
            t_marks[g] = P.now("T")
            s_hist[g] = P.now("S")
            spwd_hist[g] = P.now(("SPWD", "SPW1")[g % 2])
        for r in tpose_evac.values():
            P.wait("tensor", *r)

    weight_prep(wu_h, ni, d, 2, None, wuT_dram, wraw_s, wi8_s, wdq_s, wsc_s, wqf_s,
                ("WR0", "WR1", "WR2"), mrow=True, entry=wu_entry, entry_t=wu_entry_t)
    weight_prep(wg_h, ni, d, 2, None, wgT_dram, wraw_s, wi8_s, wdq_s, wsc_s, wqf_s, ("WR0", "WR1", "WR2"), mrow=True)
    spwd_ug = (P.now("SPWD"), P.now("SPW1"))
    wd_entry = (P.now("S"), P.now("G"), P.now("V"))

    P.wait("gpsimd", "CC", cc_kva)
    kv2a = dma("gpsimd", kvtks.ap()[:, 0:hw_kv], kv_cc_outA[:, :], "KV2")
    P.wait("gpsimd", "CC", cc_kv)
    kv2b = dma("gpsimd", kvtks.ap()[:, hw_kv:2 * hw_kv], kv_cc_outB[:, :], "KV3")

    P.wait("tensor", "KV2", kv2a)
    for dk in range(nk):
        if dk == nh:
            P.wait("tensor", "KV3", kv2b)
        ksc = kvtks.ap()[:, kscol(dk): kscol(dk) + 1]
        for s in range(ns):
            mm(PA[0:1, s * 512:(s + 1) * 512], ksc,
               qe[dk].ap()[:, s * 512:(s + 1) * 512],
               start=(dk == 0), stop=(dk == nk - 1))
    tden = t_mark()
    P.wait("vector", "T", tden)
    v_op(lambda e: e.tensor_scalar(R0, PA[0:1, 0:tok], ATT_EPS, None, ALU.add))
    vz0 = v_mark(drain=True)
    P.wait("scalar", "V", vz0)
    s_op(lambda e: e.activation(R0, R0, AF.Ln))
    s_op(lambda e: e.activation(R0, R0, AF.Exp, scale=sc(S_MONE)))
    vz = s_mark(drain=True)
    P.wait("tensor", "S", vz)
    for s in range(ns):
        mm(PB[:, s * 512:(s + 1) * 512], onerow_f[0:1, :], R0[:, s * 512:(s + 1) * 512], True, True)
    tzb = t_mark()
    P.wait("vector", "T", tzb)
    v_op(lambda e: e.tensor_copy(zbc.ap()[:, :], PB[:, 0:tok]))
    v_zbc = v_mark()

    P.wait("tensor", "V", v_zbc)
    xres_evac = {}
    xts = [xres_t, xres_t2]
    xsem = ("LX2", "LXR")
    lx2a = {0: dma("sync", xres_t.ap()[:, :], xT[0:128, :], "LX2"),
            1: dma("sync", xres_t2.ap()[:, :], xT[128:256, :], "LXR")}
    for h in range(nk):
        pg = pages[h % 2]
        if h >= 2:
            P.wait("tensor", *xres_evac[h % 2])
        for dk in range(nk):
            lhs = kvtks.ap()[:, kvoff(dk) + h * 128: kvoff(dk) + (h + 1) * 128]
            for s in range(ns):
                mm(pg[:, s * 512:(s + 1) * 512], lhs,
                   qe[dk].ap()[:, s * 512:(s + 1) * 512],
                   start=(dk == 0), stop=(dk == nk - 1))
        tv = t_mark()
        P.wait("vector", "T", tv)
        P.wait("vector", xsem[h % 2], lx2a[h])
        v_op(lambda e, pg=pg, h=h: e.tensor_tensor(xresf[h].ap()[:, :], pg[:, 0:tok], zbc.ap()[:, :], ALU.mult))
        v_op(lambda e, h=h: e.tensor_tensor(xresf[h].ap()[:, :], xresf[h].ap()[:, :],
                                            xts[h % 2].ap()[:, :], ALU.add))
        vv = v_mark()
        xres_evac[h % 2] = ("V", vv)
        xres_vv = xres_evac[h % 2]
        P.wait("sync", "V", vv)
        if h + 2 < nk:
            lx2a[h + 2] = dma("sync", xts[h % 2].ap()[:, :], xT[(h + 2) * 128:(h + 3) * 128, :],
                              xsem[h % 2])
        dma("sync", xres_dram[h * 128:(h + 1) * 128, :], xresf[h].ap()[:, :], "SXR")
    sxr_all = P.now("SXR")
    t_vphase = P.now("T")

    def arrive6(k, eng):
        pass

    v_xres_all = P.now("V")
    P.wait("vector", "V", v_xres_all)
    P.wait("scalar", "V", v_xres_all)
    tb2 = ln_block(lambda k: xresf[k].ap()[:, :], arrive6, lambda k, tk: None,
                   xbf_t2, row1b.ap()[0:1, :])
    P.wait("vector", "T", tb2)
    app6 = {}
    for k in range(nk):
        xk = xresf[k].ap()[:, :]
        ok = ln2x[k].ap()[:, :]
        v_op(lambda e, xk=xk: e.tensor_tensor(ttmp6[:, 0:tok], xk, PA[:, 0:tok], ALU.mult))
        v_op(lambda e, ok=ok: e.tensor_tensor(ok, ttmp6[:, 0:tok], PB[:, 0:tok], ALU.add))
        v_op(lambda e, k=k: e.tensor_reduce(mxcols.ap()[:, k:k + 1], ln2x[k].ap()[:, :],
                                            AX.X, ALU.max, apply_absolute_value=True))
        app6[k] = v_mark()
    v_ln2 = app6[nk - 1]

    def fold_cols(c0, c1, slot, wait_val):
        P.wait("vector", "V", wait_val)
        sep("vector")
        v_op(lambda e, c0=c0, c1=c1: e.tensor_reduce(bcol(B_RUN), mxcols.ap()[:, c0:c1],
                                                     AX.X, ALU.max))
        vr = v_mark(drain=True)
        P.wait("tensor", "V", vr)
        mm(PA[0:1, 0:128], bcol(B_RUN), ident_f[:, :], True, True)
        tf = t_mark()
        P.wait("vector", "T", tf)
        v_op(lambda e, slot=slot: e.tensor_reduce(sc(slot), PA[0:1, 0:128], AX.X, ALU.max))
        return v_mark(drain=True)

    def barrier_start(idx, wait_v):
        P.wait("sync", "V", wait_v)
        lbr = dma("sync", sc_cc_in[idx][:, :], scal.ap()[0:1, 0:16], "LBR")
        P.wait("gpsimd", "LBR", lbr)
        P.emit("gpsimd", lambda e, idx=idx: e.collective_compute(
            "AllReduce", ALU.max, replica_groups=[list(range(N_CORES))],
            ins=[sc_cc_in[idx].ap()], outs=[sc_cc_out[idx].ap()]).then_inc(SEMS["CC"], 1))
        return P.inc("CC", 1)

    def barrier_end(idx, cb):
        P.wait("sync", "CC", cb)
        return dma("sync", scal.ap()[0:1, 0:16], sc_cc_out[idx][:, :], "GBR")

    def barrier(idx, wait_v):
        return barrier_end(idx, barrier_start(idx, wait_v))

    def bcast_scalar(src_slot, dst_col, wait_sem, wait_val):
        P.wait("tensor", wait_sem, wait_val)
        mm(PA[:, 0:1], onerow_f[0:1, :], sc(src_slot), True, True)
        tf = t_mark()
        P.wait("vector", "T", tf)
        v_op(lambda e, dst_col=dst_col: e.tensor_copy(bcol(dst_col), PA[:, 0:1]))
        return v_mark(drain=True)

    weight_prep(wd_h, nk, di // 2, 1, None, wdT_dram, wraw_s, wi8_s, wdq_s, wsc_s, wqf_s,
                ("WR0", "WR1", "WR2"), entry=wd_entry, col0=0, j0ofs=0)
    weight_prep(wd_h, nk, di // 2, 1, None, wdT_dram, wraw_s, wi8_s, wdq_s, wsc_s, wqf_s,
                ("WR0", "WR1", "WR2"), col0=di // 2, j0ofs=di // 256)
    spwd_all = (P.now("SPWD"), P.now("SPW1"))

    vf0 = fold_cols(0, nk, 0, v_ln2)
    gb0 = barrier(0, vf0)
    P.wait("vector", "GBR", gb0)
    v_op(lambda e: e.tensor_scalar(sc(S_S1), sc(0), 1.0 / QMAX, GUARD, ALU.mult, ALU.max))
    sep("vector")
    v_op(lambda e: e.reciprocal(sc(S_C1), sc(S_S1)))
    v_s1 = v_mark(drain=True)
    v_c1b = bcast_scalar(S_C1, B_C1, "V", v_s1)

    P.wait("vector", "V", v_c1b)
    xq16s = [xq16.ap()[:, :], xq16b.ap()[:, :]]
    sxq_m = {}
    for k in range(nk):
        src = ln2x[k].ap()[:, :]
        if k >= 2:
            P.wait("vector", "S", sxq_m[k - 2])
        v_op(lambda e, src=src, k=k: e.tensor_scalar(xq16s[k % 2], src, bcol(B_C1), None, ALU.mult))
        vvq = v_mark()
        P.wait("scalar", "V", vvq)
        s_op(lambda e, k=k: e.activation(xq[k].ap()[:, :], xq16s[k % 2], AF.Copy))
        sxq_m[k] = s_mark()
    v_xq = P.now("V")
    s_xq = P.now("S")

    P.wait("tensor", "V", v_xq)
    P.wait("tensor", "S", s_xq)
    a_evac = {}
    thw = min(1024, tok)
    nth = (tok + 1023) // 1024
    LWS = ("LW0", "LW1", "LW2", "LW3")
    arrW = {}
    ta_m = {}
    P.wait("sync", "SPWD", spwd_ug[0])
    P.wait("sync", "SPW1", spwd_ug[1])
    if wst_overlay:
        P.wait("sync", "T", t_vphase)
    else:
        P.wait("sync", "V", v_xq)
        P.wait("sync", "S", s_xq)

    def loadw(m):
        dma("sync", wst_u[m % 4][:, :], wuT_dram[m * 128:(m + 1) * 128, :], LWS[m % 4])
        arrW[m] = dma("sync", wst_g[m % 4][:, :], wgT_dram[m * 128:(m + 1) * 128, :], LWS[m % 4])

    loadw(0)
    loadw(1)
    for m in range(ni):
        if m + 2 < ni:
            if m >= 2:
                P.wait("sync", "T", ta_m[m - 2])
            loadw(m + 2)
        P.wait("tensor", LWS[m % 4], arrW[m])
        for th in range(nth):
            pi = (nth * m + th) % 2
            pg = pages[pi]
            if nth * m + th >= 2:
                P.wait("tensor", *a_evac[pi])
            for k in range(nk):
                lu = wst_u[m % 4][:, k * 128:(k + 1) * 128]
                lg = wst_g[m % 4][:, k * 128:(k + 1) * 128]
                for s in range(max(1, thw // 512)):
                    mv = xq[k].ap()[:, th * 1024 + s * 512: th * 1024 + s * 512 + 512]
                    mm(pg[:, s * 512:(s + 1) * 512], lu, mv, start=(k == 0), stop=(k == nk - 1))
                    mm(pg[:, 1024 + s * 512: 1024 + (s + 1) * 512], lg, mv, start=(k == 0), stop=(k == nk - 1))
            ta = t_mark()
            ta_m[m] = ta
            P.wait("scalar", "T", ta)
            if m >= 2 and th == 0:
                P.wait("scalar", ("SA0", "SA1")[m % 2], P.now(("SA0", "SA1")[m % 2]))
            us = o_bf[m]
            gs = spill_t[2 + m % 2].ap()
            tb = th * 1024
            s_op(lambda e, pg=pg, us=us, tb=tb: e.copy(us[:, tb:tb + thw], pg[:, 0:thw]))
            s_op(lambda e, pg=pg, gs=gs, tb=tb: e.copy(gs[:, tb:tb + thw], pg[:, 1024:1024 + thw]))
            a_evac[pi] = ("S", s_mark())
        P.wait("vector", "S", a_evac[(nth * m + nth - 1) % 2][1])
        v_op(lambda e, m=m: e.tensor_reduce(mxcols.ap()[:, m:m + 1], o_bf[m][:, :],
                                            AX.X, ALU.max, apply_absolute_value=True))
        v_op(lambda e, m=m: e.tensor_reduce(mxcols.ap()[:, 32 + m:33 + m], spill_t[2 + m % 2].ap()[:, :],
                                            AX.X, ALU.max, apply_absolute_value=True))
        va = v_mark()
        P.wait("sync", "V", va)
        dma("sync", gate_dram[m * 128:(m + 1) * 128, :], spill_t[2 + m % 2].ap()[:, :], ("SA0", "SA1")[m % 2])
    v_passA = P.now("V")
    sa_all = (P.now("SA0"), P.now("SA1"))

    vfu = fold_cols(0, ni, 0, v_passA)
    vfg = fold_cols(32, 32 + ni, 1, vfu)
    cb1 = barrier_start(1, vfg)
    P.wait("sync", "SA0", sa_all[0])
    P.wait("sync", "SA1", sa_all[1])
    arrB = {}
    arrB[0] = dma("sync", spill_t[2].ap()[:, :], gate_dram[0:128, :], "LB0")
    gb1 = barrier_end(1, cb1)

    P.wait("vector", "GBR", gb1)
    v_op(lambda e: e.tensor_tensor(sc(S_T0), sc(0), sc(S_S1), ALU.mult))
    sep("vector")
    v_op(lambda e: e.tensor_scalar(sc(S_S2), sc(S_T0), 1.0 / QMAX, GUARD, ALU.mult, ALU.max))
    sep("vector")
    v_op(lambda e: e.reciprocal(sc(S_T0), sc(S_S2)))
    sep("vector")
    v_op(lambda e: e.tensor_tensor(sc(S_F2), sc(S_T0), sc(S_S1), ALU.mult))
    v_op(lambda e: e.tensor_tensor(sc(S_T1), sc(1), sc(S_S1), ALU.mult))
    sep("vector")
    v_op(lambda e: e.tensor_scalar(sc(S_S3), sc(S_T1), 1.0 / QMAX, GUARD, ALU.mult, ALU.max))
    sep("vector")
    v_op(lambda e: e.reciprocal(sc(S_T1), sc(S_S3)))
    sep("vector")
    v_op(lambda e: e.tensor_tensor(sc(S_F3), sc(S_T1), sc(S_S1), ALU.mult))
    v_sc1 = v_mark(drain=True)
    v_b1 = bcast_scalar(S_F2, B_F2, "V", v_sc1)
    v_b2 = bcast_scalar(S_F3, B_F3, "V", v_b1)
    v_b3 = bcast_scalar(S_S3, B_S3, "V", v_b2)

    P.wait("vector", "V", v_b3)
    b_done = {}
    gcb = [gi16.ap()[:, :], gi16b.ap()[:, :], gi16c.ap()[:, :]]
    sgf = [sigf[:, :], sigfb[:, :]]
    sg32 = [sig32.ap()[:, :], sig32b.ap()[:, :]]
    vg_m = {}
    vs_m = {}
    sg_m = {}
    gp_m = {}
    for m in range(ni + 2):
        p = m % 2
        if m < ni:
            if m + 1 < ni:
                if m >= 1:
                    P.wait("sync", "V", vg_m[m - 1])
                arrB[m + 1] = dma("sync", spill_t[2 + (m + 1) % 2].ap()[:, :],
                                  gate_dram[(m + 1) * 128:(m + 2) * 128, :], ("LB0", "LB1")[(m + 1) % 2])
            P.wait("vector", ("LB0", "LB1")[p], arrB[m])
            if m >= 3:
                P.wait("vector", "G", gp_m[m - 3])
            v_op(lambda e, m=m: e.tensor_scalar(upq[m].ap()[:, :], o_bf[m][:, :],
                                                bcol(B_F2), None, ALU.mult))
            v_op(lambda e, m=m, p=p: e.tensor_scalar(gcb[m % 3], spill_t[2 + p].ap()[:, :],
                                                     bcol(B_F3), None, ALU.mult))
            vg_m[m] = v_mark()
            P.wait("scalar", "V", vg_m[m])
            if m >= 2:
                P.wait("scalar", "V", vs_m[m - 2])
            s_op(lambda e, m=m, p=p: e.activation(sgf[p], gcb[m % 3], AF.Sigmoid, scale=bcol(B_S3)))
            sg_m[m] = s_mark()
        mm1 = m - 1
        if 0 <= mm1 < ni:
            q = mm1 % 2
            P.wait("vector", "S", sg_m[mm1])
            v_op(lambda e, q=q: e.tensor_scalar(sg32[q], sgf[q], 65536.0, None, ALU.mult))
            vs_m[mm1] = v_mark()
            P.wait("gpsimd", "V", vs_m[mm1])
            if mm1 >= 2:
                P.wait("gpsimd", ("SB0", "SB1")[q], P.now(("SB0", "SB1")[q]))
            g_op(lambda e, q=q, mm1=mm1: e.tensor_tensor(prodt[q].ap()[:, :], gcb[mm1 % 3],
                                                          sg32[q], ALU.mult))
            gp_m[mm1] = g_mark()
        mm2 = m - 2
        if mm2 >= 0:
            q2 = mm2 % 2
            P.wait("vector", "G", gp_m[mm2])
            v_op(lambda e, q2=q2, mm2=mm2: e.tensor_reduce(mxcols.ap()[:, mm2:mm2 + 1],
                                                           prodt[q2].ap()[:, :], AX.X,
                                                           ALU.max, apply_absolute_value=True))
            b_done[q2] = v_mark()
            P.wait("sync", "V", b_done[q2])
            dma("sync", prod_dram[mm2 * 128:(mm2 + 1) * 128, :], prodt[q2].ap()[:, :],
                ("SB0", "SB1")[q2])
    v_passB = P.now("V")
    sb_all = (P.now("SB0"), P.now("SB1"))

    vf2 = fold_cols(0, ni, 0, v_passB)
    cb2 = barrier_start(2, vf2)
    P.wait("sync", "SB0", sb_all[0])
    P.wait("sync", "SB1", sb_all[1])
    arrC = {0: dma("sync", spill_t[0].ap()[:, :], prod_dram[0:128, :], "LC0")}
    P.wait("sync", "SPWD", spwd_all[0])
    P.wait("sync", "SPW1", spwd_all[1])
    for i in range(ni):
        dma("sync", wdT[i][:, :], wdT_dram[i * 128:(i + 1) * 128, :], "LWDT")
    lwdt_all = P.now("LWDT")
    gb2 = barrier_end(2, cb2)
    P.wait("vector", "GBR", gb2)
    v_op(lambda e: e.tensor_scalar(sc(S_T0), sc(S_S3), 1.0 / 65536.0, None, ALU.mult))
    sep("vector")
    v_op(lambda e: e.tensor_tensor(sc(S_T1), sc(0), sc(S_T0), ALU.mult))
    sep("vector")
    v_op(lambda e: e.tensor_scalar(sc(S_S4), sc(S_T1), 1.0 / QMAX, GUARD, ALU.mult, ALU.max))
    sep("vector")
    v_op(lambda e: e.reciprocal(sc(S_T1), sc(S_S4)))
    sep("vector")
    v_op(lambda e: e.tensor_tensor(sc(S_C4), sc(S_T1), sc(S_T0), ALU.mult))
    v_sc2 = v_mark(drain=True)
    v_b4 = bcast_scalar(S_C4, B_C4, "V", v_sc2)

    P.wait("vector", "V", v_b4)
    P.wait("vector", "G", P.now("G"))
    htk = tok // 2
    c_done = {}
    gc = [gi16, gi16b]
    vc_m = {}
    gc_m = {}
    for m in range(ni + 2):
        p = m % 2
        if m < ni:
            if m + 1 < ni:
                if m >= 1:
                    P.wait("sync", "V", vc_m[m - 1])
                arrC[m + 1] = dma("sync", spill_t[(m + 1) % 2].ap()[:, :],
                                  prod_dram[(m + 1) * 128:(m + 2) * 128, :], ("LC0", "LC1")[(m + 1) % 2])
            P.wait("vector", ("LC0", "LC1")[p], arrC[m])
            if m >= 2:
                P.wait("vector", ("SC0", "SC1")[p], P.now(("SC0", "SC1")[p]))
                P.wait("vector", "G", gc_m[m - 2])
            v_op(lambda e, m=m, p=p: e.tensor_scalar(gc[p].ap()[:, :], spill_t[p].ap()[:, :],
                                                     bcol(B_C4), None, ALU.mult))
            vc_m[m] = v_mark()
        mm2 = m - 2
        if mm2 >= 0:
            q2 = mm2 % 2
            P.wait("vector", "G", gc_m[mm2])
            v_op(lambda e, q2=q2, mm2=mm2: e.tensor_reduce(mxcols.ap()[:, mm2:mm2 + 1],
                                                           of32.ap()[:, q2 * tok:q2 * tok + tok],
                                                           AX.X, ALU.max, apply_absolute_value=True))
            c_done[q2] = v_mark()
        if m < ni:
            P.wait("sync", "V", vc_m[m])
            dma("sync", g2_dram[m * 128:(m + 1) * 128, :], gc[p].ap()[:, :], ("SC0", "SC1")[p])
            P.wait("gpsimd", "V", vc_m[m])
            if m >= 2:
                P.wait("gpsimd", "V", c_done[p])
            g_op(lambda e, m=m, p=p: e.tensor_tensor(of32.ap()[:, p * tok:p * tok + htk],
                                                     gc[p].ap()[:, 0:htk],
                                                     upq[m].ap()[:, 0:htk], ALU.mult))
            gc_m[m] = g_mark()
            v_op(lambda e, m=m, p=p: e.tensor_tensor(of32.ap()[:, p * tok + htk:p * tok + tok],
                                                     gc[p].ap()[:, htk:tok],
                                                     upq[m].ap()[:, htk:tok], ALU.mult))
    v_passC = P.now("V")
    scc_all = (P.now("SC0"), P.now("SC1"))

    vf3 = fold_cols(0, ni, 0, v_passC)
    cb3 = barrier_start(3, vf3)
    P.wait("sync", "SC0", scc_all[0])
    P.wait("sync", "SC1", scc_all[1])
    P.wait("sync", "G", P.now("G"))
    arrD = {0: dma("sync", gi16.ap()[:, :], g2_dram[0:128, :], "LP0")}
    gb3 = barrier_end(3, cb3)
    P.wait("vector", "GBR", gb3)
    v_op(lambda e: e.tensor_tensor(sc(S_T0), sc(S_S4), sc(S_S2), ALU.mult))
    sep("vector")
    v_op(lambda e: e.tensor_tensor(sc(S_T1), sc(0), sc(S_T0), ALU.mult))
    sep("vector")
    v_op(lambda e: e.tensor_scalar(sc(S_S5), sc(S_T1), 1.0 / QMAX, GUARD, ALU.mult, ALU.max))
    sep("vector")
    v_op(lambda e: e.reciprocal(sc(S_T1), sc(S_S5)))
    sep("vector")
    v_op(lambda e: e.tensor_tensor(sc(S_C5), sc(S_T1), sc(S_T0), ALU.mult))
    v_sc3 = v_mark(drain=True)
    v_b5 = bcast_scalar(S_C5, B_C5, "V", v_sc3)
    v_b6 = bcast_scalar(S_S5, B_S5, "V", v_b5)

    P.wait("vector", "V", v_b6)
    P.wait("gpsimd", "V", v_b6)
    P.wait("tensor", "LWDT", lwdt_all)
    htk2 = tok // 2
    w2d = min(512, htk2)
    ns2 = max(1, htk2 // 512)
    d_done = {}
    sD_m = {}
    td_p0 = None
    gslot = [gi16.ap()[:, :], gi16b.ap()[:, :]]
    gd_m = {}
    for m in range(ni + 1):
        p = m % 2
        if m < ni:
            if m + 1 < ni:
                if m >= 1:
                    P.wait("sync", "G", gd_m[m - 1])
                    if m >= 2:
                        P.wait("sync", "V", d_done[m % 2])
                arrD[m + 1] = dma("sync", gslot[(m + 1) % 2], g2_dram[(m + 1) * 128:(m + 2) * 128, :],
                                  ("LP0", "LP1")[(m + 1) % 2])
            P.wait("gpsimd", ("LP0", "LP1")[p], arrD[m])
            if m >= 2:
                P.wait("gpsimd", "V", d_done[p])
            g_op(lambda e, m=m, p=p: e.tensor_tensor(of32.ap()[:, p * tok:p * tok + htk],
                                                     gslot[p][:, 0:htk],
                                                     upq[m].ap()[:, 0:htk], ALU.mult))
            gd_m[m] = g_mark()
            P.wait("vector", ("LP0", "LP1")[p], arrD[m])
            v_op(lambda e, m=m, p=p: e.tensor_tensor(of32.ap()[:, p * tok + htk:p * tok + tok],
                                                     gslot[p][:, htk:tok],
                                                     upq[m].ap()[:, htk:tok], ALU.mult))
        mm1 = m - 1
        if mm1 >= 0:
            q = mm1 % 2
            P.wait("vector", "G", gd_m[mm1])
            if mm1 >= 2:
                P.wait("vector", "S", sD_m[mm1 - 2])
            v_op(lambda e, q=q: e.tensor_scalar(sg32[q], of32.ap()[:, q * tok:q * tok + tok],
                                                bcol(B_C5), None, ALU.mult))
            d_done[q] = v_mark()
            P.wait("scalar", "V", d_done[q])
            s_op(lambda e, q=q, mm1=mm1: e.activation(o_bf[mm1][:, :], sg32[q], AF.Copy))
            sD_m[mm1] = s_mark()
            P.wait("tensor", "S", sD_m[mm1])
            for j in range(2):
                lhs0 = wdT[mm1][:, j * 128:(j + 1) * 128]
                for s2 in range(ns2):
                    mm(PA[:, j * htk2 + s2 * w2d: j * htk2 + (s2 + 1) * w2d], lhs0,
                       o_bf[mm1][:, s2 * w2d:(s2 + 1) * w2d],
                       start=(mm1 == 0), stop=(mm1 == ni - 1))
            if mm1 == ni - 1:
                td_p0 = t_mark()
    v_passD = P.now("V")
    s_passD = P.now("S")

    npair = nk // 2
    npass = 2 * npair
    dn_evac = {}
    td_pass = {0: td_p0}
    xr_arr = {}
    P.wait("sync", "V", v_passD)

    def xreload(pp):
        oc0 = (pp % npair) * 2
        th = pp // npair
        for j in range(2):
            oc = oc0 + j
            xr_arr[pp] = dma("sync", of32.ap()[:, tok + j * htk2: tok + (j + 1) * htk2],
                             xres_dram[oc * 128:(oc + 1) * 128, th * htk2:(th + 1) * htk2],
                             "LXR")

    xreload(0)
    for pp in range(npass):
        oc0 = (pp % npair) * 2
        th = pp // npair
        pg = pages[pp % 2]
        if pp > 0:
            if pp >= 2:
                P.wait("tensor", *dn_evac[pp % 2])
            for i in range(ni):
                for j in range(2):
                    lhs = wdT[i][:, (oc0 + j) * 128:(oc0 + j + 1) * 128]
                    for s2 in range(ns2):
                        mm(pg[:, j * htk2 + s2 * w2d: j * htk2 + (s2 + 1) * w2d], lhs,
                           o_bf[i][:, th * htk2 + s2 * w2d: th * htk2 + (s2 + 1) * w2d],
                           start=(i == 0), stop=(i == ni - 1))
            td_pass[pp] = t_mark()
        P.wait("scalar", "T", td_pass[pp])
        if pp >= 1:
            P.wait("scalar", "SOUT", P.now("SOUT"))
        s_op(lambda e, pg=pg: e.activation(of32.ap()[:, 0:tok], pg[:, 0:tok], AF.Copy,
                                           scale=bcol(B_S5)))
        sd = s_mark()
        P.wait("vector", "S", sd)
        P.wait("vector", "LXR", xr_arr[pp])
        v_op(lambda e: e.tensor_tensor(of32.ap()[:, 0:tok], of32.ap()[:, 0:tok],
                                       of32.ap()[:, tok:2 * tok], ALU.add))
        vd = v_mark()
        dn_evac[pp % 2] = ("V", vd)
        P.wait("sync", "V", vd)
        if pp + 1 < npass:
            xreload(pp + 1)
        for j in range(2):
            oc = oc0 + j
            dma("sync", outT[oc * 128:(oc + 1) * 128, th * htk2:(th + 1) * htk2],
                of32.ap()[:, j * htk2:(j + 1) * htk2], "SOUT")
    if debug:
        d_obf = nc.dram_tensor("d_obf", [di, tok], BF16, kind="ExternalOutput")
        for i in range(ni):
            dma("sync", d_obf.ap()[i * 128:(i + 1) * 128, :], o_bf[i][:, :], "SOUT")
        dbg = {"d_xres": xres_dram, "d_up": up_dram, "d_gate": gate_dram,
               "d_prod": prod_dram, "d_g2": g2_dram, "d_wdT": wdT_dram,
               "d_kv": kv_cc_outB}
        for nm, t in dbg.items():
            o = nc.dram_tensor(nm, list(t.shape), t.dtype, kind="ExternalOutput")
            dma("sync", o.ap(), t.ap(), "SOUT")
        o1 = nc.dram_tensor("d_scal", [1, 64], F32, kind="ExternalOutput")
        dma("sync", o1.ap(), scal.ap()[0:1, :], "SOUT")
        o2 = nc.dram_tensor("d_bc", [128, 16], F32, kind="ExternalOutput")
        dma("sync", o2.ap(), bc.ap()[:, :], "SOUT")
    P.wait("sync", "SOUT", P.now("SOUT"))
    return P


def build(nc, d=D, tok=TOK, di=DI, debug=False):
    global SEMS
    SEMS = {}
    import contextlib
    with contextlib.ExitStack() as es:
        for s in SEM_NAMES:
            SEMS[s] = es.enter_context(nc.semaphore(s))
        P = build_program(nc, d, tok, di, debug)
        with nc.Block() as block:
            for eng_name in Plan.ENGINES:
                deco = getattr(block, eng_name)

                def make(fns):
                    def body(e):
                        for f in fns:
                            f(e)
                    return body
                deco(make(P.q[eng_name]))
    return nc


def kernel(L0_attn_x, L0_attn_W, L0_ffn_w_gate, L0_ffn_w_up, L0_ffn_w_down):
    x = np.ascontiguousarray(L0_attn_x, dtype=np.float32)
    W = np.ascontiguousarray(L0_attn_W, dtype=np.float32)
    wg = np.ascontiguousarray(L0_ffn_w_gate, dtype=np.float32)
    wu = np.ascontiguousarray(L0_ffn_w_up, dtype=np.float32)
    wd = np.ascontiguousarray(L0_ffn_w_down, dtype=np.float32)
    ident = np.eye(128, dtype=np.float32)

    nc = bass.Bass("TRN2", target_bir_lowering=False, detect_race_conditions=False)
    build(nc)

    in_maps = []
    for c in range(N_CORES):
        b, h = c >> 1, c & 1
        xTc = np.ascontiguousarray(x[b, h * TOK:(h + 1) * TOK, :].T)
        in_maps.append({"xT": xTc, "W": W, "wg": wg, "wu": wu, "wd": wd, "ident": ident})
    res = run_bass_kernel_spmd(nc, in_maps, core_ids=list(range(N_CORES)))
    out = np.empty((4, 4096, D), dtype=np.float32)
    for c in range(N_CORES):
        b, h = c >> 1, c & 1
        out[b, h * TOK:(h + 1) * TOK, :] = res.results[c]["outT"].T
    return out
